# version 01
# baseline (speedup 1.0000x reference)
import numpy as np
import ml_dtypes
from contextlib import ExitStack

import concourse.bass as bass
import concourse.mybir as mybir
from concourse.bass_utils import run_bass_kernel_spmd

F32 = mybir.dt.float32
BF16 = mybir.dt.bfloat16
AF = mybir.ActivationFunctionType
ALU = mybir.AluOpType
AX = mybir.AxisListType
BF = ml_dtypes.bfloat16

NC = 8
B_ = 2
SEQ = 8192
D = 1024
MLOC = SEQ // NC
TLOC = B_ * MLOC
NT = TLOC // 128
EPS = 1e-5


ENGS = ("te", "ve", "se", "gp", "sy")


class Builder:
    K_DMA = 6

    def __init__(self, name):
        self.name = name
        self.nc = bass.Bass("TRN2", target_bir_lowering=False)
        self.ctx = ExitStack()
        self.counts = {e: 0 for e in ENGS}
        self.dma_i = {e: 0 for e in ENGS}
        self.marks = {e: {} for e in ENGS}
        self.records = {e: [] for e in ENGS}

    def sbuf(self, name, shape, dt):
        return self.ctx.enter_context(self.nc.sbuf_tensor(name, list(shape), dt))

    def psum(self, name, shape, dt=F32):
        h = self.ctx.enter_context(self.nc.psum_tensor(name, list(shape), dt))
        return PsumTile(h)

    def din(self, name, shape, dt):
        return self.nc.dram_tensor(name, list(shape), dt, kind="ExternalInput").ap()

    def dout(self, name, shape, dt):
        return self.nc.dram_tensor(name, list(shape), dt, kind="ExternalOutput").ap()

    def dtmp(self, name, shape, dt):
        return self.nc.dram_tensor(name, list(shape), dt).ap()

    def _want(self, eng, deps):
        want = {}
        for tok in deps:
            if tok is None:
                continue
            semkey, val = tok
            if want.get(semkey, 0) < val:
                want[semkey] = val
        waits = []
        for semkey, val in want.items():
            if self.marks[eng].get(semkey, 0) < val:
                waits.append((semkey, val))
                self.marks[eng][semkey] = val
        return waits

    def op(self, eng, fn, deps=(), dma=False):
        waits = self._want(eng, deps)
        semkey = ("e", eng)
        self.counts[eng] += 1
        self.records[eng].append((waits, fn, semkey, 1, False))
        return (semkey, self.counts[eng])

    def dma(self, eng, out, in_, deps=(), **kw):
        i = self.dma_i[eng]
        self.dma_i[eng] = i + 1
        j = i % self.K_DMA
        semkey = ("d", eng, j)
        slot_use = i // self.K_DMA
        deps = list(deps) + [(semkey, 16 * slot_use)]
        waits = self._want(eng, deps)
        fn = lambda e: e.dma_start(out=out, in_=in_, **kw)
        self.records[eng].append((waits, fn, semkey, 16, True))
        return (semkey, 16 * (slot_use + 1))

    def barrier_tokens(self):
        toks = [(("e", e), self.counts[e]) for e in ENGS if self.counts[e] > 0]
        for e in ENGS:
            n = self.dma_i[e]
            for j in range(self.K_DMA):
                full = n // self.K_DMA + (1 if j < n % self.K_DMA else 0)
                if full > 0:
                    toks.append((("d", e, j), 16 * full))
        return toks

    def finalize(self, final_deps):
        gw = self._want("gp", list(final_deps) + self.barrier_tokens())
        self.records["gp"].append((gw, None, None, 0, False))
        for e in ENGS:
            if e != "gp" and self.counts[e] > 0:
                w = self._want(e, [(("e", e), self.counts[e])])
                if w:
                    self.records[e].append((w, None, None, 0, False))
        nc = self.nc
        sems = {}
        for e in ENGS:
            sems[("e", e)] = self.ctx.enter_context(nc.semaphore(f"s_{e}"))
            for j in range(self.K_DMA):
                if any(r[2] == ("d", e, j) or any(w[0] == ("d", e, j) for w in r[0])
                       for recs in self.records.values() for r in recs):
                    sems[("d", e, j)] = self.ctx.enter_context(
                        nc.semaphore(f"d_{e}{j}"))
        block = self.ctx.enter_context(nc.Block())

        def ninstr():
            return sum(len(bb.instructions) for bb in nc.m.functions[0].blocks)

        def section(engname):
            def body(engobj):
                for waits, fn, semkey, inc, isdma in self.records[engname]:
                    for (sk, val) in waits:
                        engobj.wait_ge(sems[sk], val)
                    if fn is None:
                        continue
                    n0 = ninstr()
                    instr = fn(engobj)
                    if isdma:
                        nparts = ninstr() - n0
                        assert nparts == 1, (
                            f"{self.name}/{engname}: dma_start emitted "
                            f"{nparts} instructions; split it manually")
                    instr.then_inc(sems[semkey], inc)
            return body

        for engname, deco in (("te", block.tensor), ("ve", block.vector),
                              ("se", block.scalar), ("gp", block.gpsimd),
                              ("sy", block.sync)):
            if self.records[engname]:
                deco(section(engname))
        self.ctx.close()
        return nc


class PsumTile:
    def __init__(self, h):
        self.h = h
        self.wtoks = []
        self.rtoks = []

    def __getitem__(self, idx):
        return self.h[idx]

    def write_deps(self):
        return list(self.rtoks) + list(self.wtoks)

    def read_deps(self):
        return list(self.wtoks)

    def note_write(self, tok):
        if self.rtoks:
            self.rtoks = []
            self.wtoks = []
        self.wtoks.append(tok)

    def note_read(self, tok):
        self.rtoks.append(tok)


def mm(b, pt, out_ap, lhsT, rhs, deps=(), start=True, stop=True, first=False):
    d = list(deps)
    if first or start:
        d += pt.write_deps()
    tok = b.op("te", lambda e: e.matmul(out_ap, lhsT, rhs, start=start, stop=stop), deps=d)
    pt.note_write(tok)
    return tok


def pread(b, eng, pt, fn, deps=()):
    tok = b.op(eng, fn, deps=list(deps) + pt.read_deps())
    pt.note_read(tok)
    return tok



def emit_const_identity(b, name="ident"):
    ident = b.sbuf(name, [128, 128], BF16)
    t0 = b.op("gp", lambda e: e.memset(ident[:, :], 0.0))
    t1 = b.op("gp", lambda e: e.affine_select(
        out=ident[:, :], in_=ident[:, :], pattern=[[-1, 128]],
        compare_op=ALU.not_equal, fill=1.0, base=0, channel_multiplier=1),
        deps=[t0])
    return ident, t1


def emit_ln_tile(b, x_ap, xdeps, out_ap, scratch, stats, sc, scdep=None):
    s_sum = stats[:, sc:sc + 1]
    s_sq = stats[:, sc + 1:sc + 2]
    s_mean = stats[:, sc + 2:sc + 3]
    s_rstd = stats[:, sc + 3:sc + 4]
    t_sum = b.op("ve", lambda e: e.reduce_sum(s_sum, x_ap, axis=AX.X), deps=xdeps)
    t_sq = b.op("se", lambda e: e.activation(scratch, x_ap, AF.Square,
                accum_out=s_sq), deps=list(xdeps) + ([scdep] if scdep else []))
    t_mean = b.op("ve", lambda e: e.tensor_scalar(
        s_mean, s_sum, 1.0 / D, None, op0=ALU.mult, op1=ALU.bypass), deps=[t_sum])
    t_m2 = b.op("ve", lambda e: e.tensor_tensor(
        s_rstd, s_mean, s_mean, op=ALU.mult), deps=[t_mean])
    t_var = b.op("ve", lambda e: e.scalar_tensor_tensor(
        s_rstd, s_sq, 1.0 / D, s_rstd, op0=ALU.mult, op1=ALU.subtract),
        deps=[t_sq, t_m2])
    t_vare = b.op("ve", lambda e: e.tensor_scalar(
        s_rstd, s_rstd, EPS, None, op0=ALU.add, op1=ALU.bypass), deps=[t_var])
    t_ln = b.op("se", lambda e: e.activation(s_rstd, s_rstd, AF.Ln), deps=[t_vare])
    t_rstd = b.op("se", lambda e: e.activation(s_rstd, s_rstd, AF.Exp, scale=-0.5),
                  deps=[t_ln])
    t_out = b.op("ve", lambda e: e.tensor_scalar(
        out_ap, x_ap, s_mean, s_rstd, op0=ALU.subtract, op1=ALU.mult),
        deps=[t_rstd] + list(xdeps))
    return t_out, t_sq



def build_A1():
    b = Builder("A1")
    x_in = b.din("x", [TLOC, D], F32)
    wq = b.din("wq", [D, D], F32)
    wk = b.din("wk", [D, D], F32)
    wv = b.din("wv", [D, D], F32)
    wg = b.din("wg", [D, D], F32)
    wgk1 = b.din("wgk1", [D, 16], F32)
    wgk2 = b.din("wgk2", [16, D], F32)
    bgk2 = b.din("bgk2", [1, D], F32)

    o_q = b.dout("o_q", [TLOC, D], BF16)
    o_k = b.dout("o_k", [TLOC, D], BF16)
    o_v = b.dout("o_v", [TLOC, D], BF16)
    o_g = b.dout("o_g", [TLOC, D], BF16)
    o_gk = b.dout("o_gk", [TLOC, D], F32)
    o_xn = b.dout("o_xn", [TLOC, D], BF16)

    ident, tid = emit_const_identity(b)

    W = {}
    wtok = {}
    for nm, src in (("wq", wq), ("wk", wk), ("wv", wv), ("wg", wg)):
        W[nm] = b.sbuf(f"W_{nm}", [128, 8 * D], BF16)
        wtok[nm] = b.dma("gp", W[nm][:, :].rearrange("p (c d) -> p c d", c=8),
                         src.rearrange("(c p) d -> p c d", p=128))
    Wgk1 = b.sbuf("W_wgk1", [128, 8 * 16], BF16)
    t_wgk1 = b.dma("gp", Wgk1[:, :].rearrange("p (c r) -> p c r", c=8),
                   wgk1.rearrange("(c p) r -> p c r", p=128))
    Wgk2 = b.sbuf("W_wgk2", [16, D], BF16)
    t_wgk2 = b.dma("gp", Wgk2[:, :], wgk2[:, :])
    BG = b.sbuf("BG", [128, D], F32)
    t_bg = b.dma("gp", BG[:, :], bgk2[0:1, :].partition_broadcast(128))

    XR = b.sbuf("XR", [128, 2 * D], F32)
    XN = b.sbuf("XN", [128, NT * D], BF16)
    scratch = b.sbuf("scratch", [128, D], F32)
    stats = b.sbuf("stats", [128, 4 * NT], F32)
    x_r = x_in.rearrange("(t p) d -> p t d", p=128)
    ln_toks = []
    ring_dep = [None, None]
    scdep = None
    for t in range(NT):
        r = t % 2
        xa = XR[:, r * D:(r + 1) * D]
        t_x = b.dma("sy", xa, x_r[:, t, :],
                    deps=[ring_dep[r]] if ring_dep[r] else [])
        t_out, scdep = emit_ln_tile(b, xa, [t_x], XN[:, t * D:(t + 1) * D],
                                    scratch[:, :], stats, 4 * t, scdep=scdep)
        ring_dep[r] = t_out
        ln_toks.append(t_out)

    t_oxn = b.dma("sy", o_xn.rearrange("(t p) d -> p t d", p=128),
                  XN[:, :].rearrange("p (t d) -> p t d", t=NT), deps=ln_toks)

    XT = b.sbuf("XT", [128, 8 * TLOC], BF16)
    pT = [b.psum(f"pT{i}", [128, 128], BF16) for i in range(2)]
    xt_tok = {}
    xt_all = []
    for t in range(NT):
        for k in range(8):
            pt = pT[(t * 8 + k) % 2]
            ttok = b.op("te", lambda e, t=t, k=k, pt=pt: e.transpose(
                pt[:, :], XN[:, t * D + k * 128: t * D + (k + 1) * 128],
                ident[:, :]), deps=[ln_toks[t], tid] + pt.write_deps())
            pt.note_write(ttok)
            etok = pread(b, "ve", pt, lambda e, t=t, k=k, pt=pt: e.tensor_copy(
                XT[:, k * TLOC + t * 128: k * TLOC + (t + 1) * 128], pt[:, :]))
            xt_all.append(etok)
            xt_tok[(t, k)] = etok

    pj = [b.psum(f"pj{i}", [128, 512]) for i in range(4)]
    stage = b.sbuf("stage", [128, 2 * D], BF16)
    out_toks = []
    st_dep = [None, None]
    outs = {"wq": o_q, "wk": o_k, "wv": o_v, "wg": o_g}
    for t in range(NT):
        for pi, nm in enumerate(("wq", "wk", "wv", "wg")):
            sbase = (t * 4 + pi) % 2
            ev_toks = []
            for h in range(2):
                pt = pj[(t * 8 + pi * 2 + h) % 4]
                for k in range(8):
                    tok = mm(b, pt, pt[:, :],
                             XT[:, k * TLOC + t * 128: k * TLOC + (t + 1) * 128],
                             W[nm][:, k * D + h * 512: k * D + (h + 1) * 512],
                             deps=[xt_tok[(t, k)], wtok[nm]],
                             start=(k == 0), stop=(k == 7), first=(k == 0))
                dst = stage[:, sbase * D + h * 512: sbase * D + (h + 1) * 512]
                dd = [st_dep[sbase]] if st_dep[sbase] else []
                if nm == "wg":
                    ev = pread(b, "se", pt, lambda e, pt=pt, dst=dst: e.activation(
                        dst, pt[:, :], AF.Silu), deps=dd)
                else:
                    ev = pread(b, "ve", pt, lambda e, pt=pt, dst=dst: e.tensor_copy(
                        dst, pt[:, :]), deps=dd)
                ev_toks.append(ev)
            otok = b.dma("sy",
                         outs[nm].rearrange("(t p) d -> p t d", p=128)[:, t, :],
                         stage[:, sbase * D:(sbase + 1) * D], deps=ev_toks)
            st_dep[sbase] = otok
            out_toks.append(otok)

    pr1 = b.psum("pr1", [16, 512])
    R1 = b.sbuf("R1", [16, TLOC], BF16)
    r1_toks = []
    for s in range(TLOC // 512):
        for k in range(8):
            tok = mm(b, pr1, pr1[:, :],
                     Wgk1[:, k * 16:(k + 1) * 16],
                     XT[:, k * TLOC + s * 512: k * TLOC + (s + 1) * 512],
                     deps=xt_all + [t_wgk1], start=(k == 0), stop=(k == 7),
                     first=(k == 0))
        r1_toks.append(pread(b, "ve", pr1, lambda e, s=s: e.tensor_copy(
            R1[:, s * 512:(s + 1) * 512], pr1[:, :])))

    pz = [pj[0], pj[1]]
    ZA = b.sbuf("ZA", [128, 2 * D], F32)
    GK = b.sbuf("GK", [128, 2 * D], F32)
    gk_dep = [None, None]
    for t in range(NT):
        sbase = t % 2
        ev = []
        for h in range(2):
            pt = pz[h]
            tok = mm(b, pt, pt[:, :], R1[:, t * 128:(t + 1) * 128],
                     Wgk2[:, h * 512:(h + 1) * 512],
                     deps=r1_toks + [t_wgk2], start=True, stop=True, first=True)
            dd = [gk_dep[sbase]] if gk_dep[sbase] else []
            ev.append(pread(b, "ve", pt, lambda e, pt=pt, h=h, sbase=sbase: e.tensor_tensor(
                ZA[:, sbase * D + h * 512: sbase * D + (h + 1) * 512], pt[:, :],
                BG[:, h * 512:(h + 1) * 512], op=ALU.add), deps=dd + [t_bg]))
        zs = ZA[:, sbase * D:(sbase + 1) * D]
        gs = GK[:, sbase * D:(sbase + 1) * D]
        t1 = b.op("se", lambda e, zs=zs, gs=gs: e.activation(gs, zs, AF.Abs),
                  deps=ev)
        t2 = b.op("se", lambda e, gs=gs: e.activation(gs, gs, AF.Exp, scale=-1.0),
                  deps=[t1])
        t3 = b.op("se", lambda e, gs=gs: e.activation(gs, gs, AF.Ln, bias=1.0),
                  deps=[t2])
        t4 = b.op("se", lambda e, zs=zs: e.activation(zs, zs, AF.Relu, scale=-1.0),
                  deps=ev + [t1])
        t5 = b.op("ve", lambda e, zs=zs, gs=gs: e.tensor_tensor(
            gs, gs, zs, op=ALU.add), deps=[t3, t4])
        otok = b.dma("sy", o_gk.rearrange("(t p) d -> p t d", p=128)[:, t, :],
                     gs, deps=[t5])
        gk_dep[sbase] = otok
        out_toks.append(otok)

    nc = b.finalize(out_toks + [t_oxn])
    return nc



NCH = SEQ // 128


def build_A2():
    b = Builder("A2")
    d_fm = b.din("d_fm", [NCH, 64, 1024], BF16)
    d_tv = b.din("d_tv", [NCH, 128, 512], BF16)
    d_gk = b.din("d_gk", [NCH, 128, 256], F32)
    d_o = b.dout("d_o", [NCH, 128, 256], F32)

    ident, tid = emit_const_identity(b)

    TRI = b.sbuf("TRI", [128, 128], F32)
    t0 = b.op("gp", lambda e: e.memset(TRI[:, :], -1.0 / 16.0))
    t_tri = b.op("gp", lambda e: e.affine_select(
        out=TRI[:, :], in_=TRI[:, :], pattern=[[1, 128]], compare_op=ALU.is_ge,
        fill=0.0, base=0, channel_multiplier=-1), deps=[t0])
    ONESC = b.sbuf("ONESC", [128, 1], F32)
    t_ones = b.op("gp", lambda e: e.memset(ONESC[:, :], -1.0 / 16.0))
    U4 = b.sbuf("U4", [128, 512], BF16)
    t1 = b.op("gp", lambda e: e.memset(U4[:, :], 1.0))
    t_u4 = b.op("gp", lambda e: e.affine_select(
        out=U4[:, :], in_=U4[:, :], pattern=[[0, 4], [1, 128]],
        compare_op=ALU.is_ge, fill=0.0, base=0, channel_multiplier=-1),
        deps=[t1])

    FMR = b.sbuf("FMR", [64, 2 * 1024], BF16)
    TVR = b.sbuf("TVR", [128, 2 * 512], BF16)
    GKR = b.sbuf("GKR", [128, 2 * 256], F32)
    CUMP = b.sbuf("CUMP", [128, 256], F32)
    CUMN = b.sbuf("CUMN", [128, 256], F32)
    VT = b.sbuf("VT", [128, 256], BF16)
    BM = b.sbuf("BM", [128, 512], BF16)
    OS = b.sbuf("OS", [128, 2 * 256], F32)
    S = b.sbuf("S", [64, 256], F32)
    SB = b.sbuf("SB", [64, 256], BF16)
    STB = b.sbuf("STB", [64, 256], BF16)
    GEXP = b.sbuf("GEXP", [64, 4], F32)

    p_cs = b.psum("p_cs", [128, 256])
    p_sc = b.psum("p_sc", [128, 512])
    p_O = b.psum("p_O", [128, 256])
    p_VTK = b.psum("p_VTK", [64, 256])
    p_G = b.psum("p_G", [64, 4])
    p_ST = b.psum("p_ST", [64, 256], BF16)

    t_s0 = b.op("gp", lambda e: e.memset(S[:, :], 0.0))
    t_stb0 = b.op("gp", lambda e: e.memset(STB[:, :], 0.0))

    ring_dep = [None, None]
    gk_ring_dep = [None, None]
    os_dep = [None, None]
    s_prev = t_s0
    stb_prev = t_stb0
    out_toks = []
    war = {"CUMP": None, "CUMN": None, "VT": None, "BM": None, "GEXP": None,
           "SB": None}

    for c in range(NCH):
        r = c % 2
        fm = FMR[:, r * 1024:(r + 1) * 1024]
        tv = TVR[:, r * 512:(r + 1) * 512]
        gk = GKR[:, r * 256:(r + 1) * 256]
        t_fm = b.dma("sy", fm, d_fm[c, :, :],
                     deps=[ring_dep[r]] if ring_dep[r] else [])
        t_tv = b.dma("sy", tv, d_tv[c, :, :],
                     deps=[ring_dep[r]] if ring_dep[r] else [])
        t_bf = (t_tv[0], t_tv[1])
        t_gk = b.dma("gp", gk, d_gk[c, :, :],
                     deps=[gk_ring_dep[r]] if gk_ring_dep[r] else [])
        qfm = lambda u: fm[0:64, u * 128:(u + 1) * 128]
        kfm = lambda u: fm[0:64, 512 + u * 128: 512 + (u + 1) * 128]
        ktm = lambda u: tv[:, u * 64:(u + 1) * 64]
        vtm = tv[:, 256:512]

        t_cs = mm(b, p_cs, p_cs[:, :], TRI[:, :], gk, deps=[t_tri, t_gk],
                  start=True, stop=True, first=True)
        g_toks = []
        for u in range(4):
            g_toks.append(mm(b, p_G, p_G[:, u:u + 1],
                             gk[:, u * 64:(u + 1) * 64], ONESC[:, :],
                             deps=[t_gk, t_ones], start=True, stop=True,
                             first=(u == 0)))
        t_cp = pread(b, "se", p_cs, lambda e: e.activation(
            CUMP[:, :], p_cs[:, :], AF.Exp), deps=[war["CUMP"]])
        t_cn = pread(b, "se", p_cs, lambda e: e.activation(
            CUMN[:, :], p_cs[:, :], AF.Exp, scale=-1.0), deps=[war["CUMN"]])
        t_g = pread(b, "se", p_G, lambda e: e.activation(
            GEXP[:, :], p_G[:, :], AF.Exp), deps=[war["GEXP"]])
        t_vt = b.op("ve", lambda e, vtm=vtm: e.tensor_tensor(
            VT[:, :], vtm, CUMN[:, :], op=ALU.mult),
            deps=[t_tv, t_cn, war["VT"]])
        war["CUMN"] = t_vt
        sc_toks = []
        for u in range(4):
            sc_toks.append(mm(b, p_sc, p_sc[:, u * 128:(u + 1) * 128],
                              kfm(u), qfm(u), deps=[t_fm],
                              start=True, stop=True, first=(u == 0)))
        t_bm = pread(b, "ve", p_sc, lambda e: e.tensor_tensor(
            BM[:, :], p_sc[:, :], U4[:, :], op=ALU.mult),
            deps=[t_u4, war["BM"]])
        o_mm = []
        for u in range(4):
            o_mm.append(mm(b, p_O, p_O[:, u * 64:(u + 1) * 64],
                           BM[:, u * 128:(u + 1) * 128],
                           VT[:, u * 64:(u + 1) * 64],
                           deps=[t_bm, t_vt], start=True, stop=False,
                           first=(u == 0)))
            o_mm.append(mm(b, p_O, p_O[:, u * 64:(u + 1) * 64],
                           qfm(u), STB[0:64, u * 64:(u + 1) * 64],
                           deps=[t_fm, stb_prev], start=False, stop=True))
        stb_read_o = o_mm[-1]
        war["BM"] = o_mm[-1]
        osl = OS[:, (c % 2) * 256:(c % 2) * 256 + 256]
        t_oe = pread(b, "ve", p_O, lambda e, osl=osl: e.tensor_tensor(
            osl, p_O[:, :], CUMP[:, :], op=ALU.mult),
            deps=[t_cp] + ([os_dep[c % 2]] if os_dep[c % 2] else []))
        t_od = b.dma("sy", d_o[c, :, :], osl, deps=[t_oe])
        os_dep[c % 2] = t_od
        out_toks.append(t_od)
        war["CUMP"] = t_oe
        vtk_toks = []
        for u in range(4):
            vtk_toks.append(mm(b, p_VTK, p_VTK[:, u * 64:(u + 1) * 64],
                               VT[:, u * 64:(u + 1) * 64], ktm(u),
                               deps=[t_vt, t_tv], start=True, stop=True,
                               first=(u == 0)))
        war["VT"] = vtk_toks[-1]
        t_sa = pread(b, "ve", p_VTK, lambda e: e.tensor_tensor(
            S[:, :], S[:, :], p_VTK[:, :], op=ALU.add), deps=[s_prev])
        gb = GEXP[:, :].unsqueeze(2).broadcast_to([64, 4, 64])
        s3 = S[:, :].rearrange("p (u k) -> p u k", u=4)
        t_sm = b.op("ve", lambda e: e.tensor_tensor(s3, s3, gb, op=ALU.mult),
                    deps=[t_sa, t_g])
        s_prev = t_sm
        war["GEXP"] = t_sm
        t_sb = b.op("se", lambda e: e.copy(SB[:, :], S[:, :]),
                    deps=[t_sm, war["SB"]])
        tr_toks = []
        for u in range(4):
            tr_toks.append(b.op("te", lambda e, u=u: e.transpose(
                p_ST[:, u * 64:(u + 1) * 64], SB[:, u * 64:(u + 1) * 64],
                ident[0:64, 0:64]),
                deps=[t_sb, tid] + (p_ST.write_deps() if u == 0 else [])))
            p_ST.note_write(tr_toks[-1])
        t_stb = pread(b, "ve", p_ST, lambda e: e.tensor_copy(
            STB[:, :], p_ST[:, :]), deps=[stb_read_o])
        stb_prev = t_stb
        war["SB"] = tr_toks[-1]
        ring_dep[r] = vtk_toks[-1]
        gk_ring_dep[r] = g_toks[-1] if g_toks[-1][1] > t_cs[1] else t_cs

    nc = b.finalize(out_toks)
    return nc



def emit_ln_res(b, a_ap, adeps, x_ap, xdeps, out_ap, tmp, scratch, stats, sc,
                scdep=None):
    t_ln, t_sq = emit_ln_tile(b, a_ap, adeps, tmp, scratch, stats, sc,
                              scdep=scdep)
    t_out = b.op("ve", lambda e: e.tensor_tensor(out_ap, x_ap, tmp, op=ALU.add),
                 deps=[t_ln] + list(xdeps))
    return t_out, t_sq


def build_AB():
    b = Builder("AB")
    x_in = b.din("x", [TLOC, D], F32)
    og_in = b.din("og", [TLOC, D], F32)
    gs_in = b.din("gsil", [TLOC, D], BF16)
    gamma = b.din("gamma", [1, D], F32)
    beta = b.din("beta", [1, D], F32)
    wout = b.din("wout", [D, D], F32)
    phi = b.din("phi", [D, 2048], F32)
    o_x1 = b.dout("o_x1", [TLOC, D], F32)
    o_C = b.dout("o_C", [TLOC, 2048], BF16)
    o_ps = b.dout("o_ps", [B_, 2048, D], F32)
    o_pb = b.dout("o_pb", [TLOC, D], BF16)
    o_at = b.dout("o_at", [TLOC, D], F32)
    o_xn2 = b.dout("o_xn2", [TLOC, D], BF16)
    o_on = b.dout("o_on", [TLOC, D], F32)
    o_gt = b.dout("o_gt", [128, D], F32)
    o_la = b.dout("o_la", [128, 2048], F32)
    d_D = b.dtmp("d_D", [TLOC, 2048], BF16)

    ident, tid = emit_const_identity(b)
    GT = b.sbuf("GT", [128, D], F32)
    BT = b.sbuf("BT", [128, D], F32)
    t_gt = b.dma("gp", GT[:, :], gamma[0:1, :].partition_broadcast(128))
    t_bt = b.dma("gp", BT[:, :], beta[0:1, :].partition_broadcast(128))
    WBUF = b.sbuf("WBUF", [128, 8 * 2048], BF16)
    t_wo = b.dma("gp", WBUF[:, 0:8 * D].rearrange("p (c d) -> p c d", c=8),
                 wout.rearrange("(c p) d -> p c d", p=128))

    OG = b.sbuf("OG", [128, 2 * D], F32)
    
    GS = b.sbuf("GS", [128, 2 * D], BF16)
    XR = b.sbuf("XR", [128, 2 * D], F32)
    ON = b.sbuf("ON", [128, D], F32)
    PB = b.sbuf("PB", [128, D], BF16)
    PTS = b.sbuf("PTS", [128, 8 * 128], BF16)
    AT = b.sbuf("AT", [128, D], F32)
    TMP = b.sbuf("TMP", [128, D], F32)
    X1R = b.sbuf("X1R", [128, 2 * D], F32)
    XN2 = b.sbuf("XN2", [128, NT * D], BF16)
    XT2 = b.sbuf("XT2", [128, 8 * TLOC], BF16)
    scratch = b.sbuf("scratch", [128, D], F32)
    stats = b.sbuf("stats", [128, 64], F32)

    LG0 = b.psum("LG0", [128, 2048])
    pTs = [b.psum("pT0", [128, 128], BF16), b.psum("pT1", [128, 128], BF16)]

    ring = {"OG": [None, None], "GS": [None, None], "XR": [None, None],
            "X1R": [None, None]}
    out_gt_tok = b.dma("gp", o_gt[:, :], GT[:, :], deps=[t_gt])
    war = {"ON": None, "PB": None, "PTS": None, "AT": None, "TMP": None}
    scdep = None
    out_toks = []
    xt2_tok = {}
    xn2_toks = []
    x_r = x_in.rearrange("(t p) d -> p t d", p=128)
    og_r = og_in.rearrange("(t p) d -> p t d", p=128)
    gs_r = gs_in.rearrange("(t p) d -> p t d", p=128)
    x1_r = o_x1.rearrange("(t p) d -> p t d", p=128)

    for t in range(NT):
        r = t % 2
        oga = OG[:, r * D:(r + 1) * D]
        gsa = GS[:, r * D:(r + 1) * D]
        xra = XR[:, r * D:(r + 1) * D]
        t_og = b.dma("sy", oga, og_r[:, t, :], deps=[ring["OG"][r]])
        t_gs = b.dma("sy", gsa, gs_r[:, t, :], deps=[ring["GS"][r]])
        t_xr = b.dma("sy", xra, x_r[:, t, :], deps=[ring["XR"][r]])
        t_on, scdep = emit_ln_tile(b, oga, [t_og, war["ON"]], ON[:, :],
                                   scratch[:, :], stats, 0, scdep=scdep)
        t_g1 = b.op("ve", lambda e: e.tensor_tensor(
            ON[:, :], ON[:, :], GT[:, :], op=ALU.mult), deps=[t_on, t_gt])
        t_g2 = b.op("ve", lambda e: e.tensor_tensor(
            ON[:, :], ON[:, :], BT[:, :], op=ALU.add), deps=[t_g1, t_bt])
        t_ond = b.dma("gp", o_on.rearrange("(t p) d -> p t d", p=128)[:, t, :],
                      ON[:, :], deps=[t_g2])
        out_toks.append(t_ond)
        t_pb = b.op("ve", lambda e, gsa=gsa: e.tensor_tensor(
            PB[:, :], ON[:, :], gsa, op=ALU.mult),
            deps=[t_g2, t_gs, war["PB"]])
        ring["OG"][r] = t_on
        ring["GS"][r] = t_pb
        war["ON"] = t_pb
        tr_evs = []
        for k in range(8):
            pt = pTs[k % 2]
            ttok = b.op("te", lambda e, k=k, pt=pt: e.transpose(
                pt[:, :], PB[:, k * 128:(k + 1) * 128], ident[:, :]),
                deps=[t_pb, tid] + pt.write_deps())
            pt.note_write(ttok)
            tr_evs.append(pread(b, "ve", pt, lambda e, k=k, pt=pt: e.tensor_copy(
                PTS[:, k * 128:(k + 1) * 128], pt[:, :]), deps=[war["PTS"]]))
        war["PB"] = tr_evs[-1]
        wo_ev = []
        for h in range(2):
            for k in range(8):
                mm(b, LG0, LG0[:, h * 512:(h + 1) * 512],
                   PTS[:, k * 128:(k + 1) * 128],
                   WBUF[:, k * D + h * 512: k * D + (h + 1) * 512],
                   deps=[tr_evs[k], t_wo], start=(k == 0), stop=(k == 7),
                   first=(k == 0))
        t_at = pread(b, "ve", LG0, lambda e: e.tensor_copy(
            AT[:, :], LG0[:, 0:D]), deps=[war["AT"]])
        war["PTS"] = t_at
        x1a = X1R[:, r * D:(r + 1) * D]
        t_x1, scdep = emit_ln_res(b, AT[:, :], [t_at], xra, [t_xr],
                                  x1a, TMP[:, :], scratch[:, :], stats, 8,
                                  scdep=scdep)
        war["AT"] = t_x1
        war["TMP"] = t_x1
        ring["XR"][r] = t_x1
        t_x1d = b.dma("sy", x1_r[:, t, :], x1a, deps=[t_x1])
        out_toks.append(t_x1d)
        out_toks.append(b.dma("gp", o_pb.rearrange("(t p) d -> p t d", p=128)[:, t, :],
                              PB[:, :], deps=[t_pb]))
        out_toks.append(b.dma("gp", o_at.rearrange("(t p) d -> p t d", p=128)[:, t, :],
                              AT[:, :], deps=[t_at, t_x1]))
        t_xn2, scdep = emit_ln_tile(b, x1a, [t_x1], XN2[:, t * D:(t + 1) * D],
                                    scratch[:, :], stats, 16, scdep=scdep)
        xn2_toks.append(t_xn2)
        out_toks.append(b.dma("gp", o_xn2.rearrange("(t p) d -> p t d", p=128)[:, t, :],
                              XN2[:, t * D:(t + 1) * D], deps=[t_xn2]))
        ring["X1R"][r] = t_xn2
        for k in range(8):
            pt = pTs[k % 2]
            ttok = b.op("te", lambda e, t=t, k=k, pt=pt: e.transpose(
                pt[:, :], XN2[:, t * D + k * 128: t * D + (k + 1) * 128],
                ident[:, :]), deps=[t_xn2, tid] + pt.write_deps())
            pt.note_write(ttok)
            xt2_tok[(t, k)] = pread(b, "ve", pt,
                lambda e, t=t, k=k, pt=pt: e.tensor_copy(
                    XT2[:, k * TLOC + t * 128: k * TLOC + (t + 1) * 128], pt[:, :]))
        ring["X1R"][r] = xt2_tok[(t, 7)]

    phi_dep = [tok for tok in out_toks]
    t_phi = b.dma("gp", WBUF[:, :].rearrange("p (c n) -> p c n", c=8),
                  phi.rearrange("(c p) n -> p c n", p=128),
                  deps=b.barrier_tokens())
    DS = b.sbuf("DS", [128, 2 * 2048], BF16)
    LA = b.sbuf("LA", [128, 2048], F32)
    DIF = b.sbuf("DIF", [128, 2048], F32)
    CC = b.sbuf("CC", [128, 2 * 2048], BF16)
    c_r = o_C.rearrange("(t p) n -> p t n", p=128)
    d_r = d_D.rearrange("(t p) n -> p t n", p=128)
    d_toks = []
    war2 = {"LA": None, "DIF": None, "CC0": None, "CC1": None,
            "DS0": None, "DS1": None}

    def logits_tile(t, deps):
        mms = []
        for h in range(4):
            for k in range(8):
                mms.append(mm(b, LG0, LG0[:, h * 512:(h + 1) * 512],
                              XT2[:, k * TLOC + t * 128: k * TLOC + (t + 1) * 128],
                              WBUF[:, k * 2048 + h * 512: k * 2048 + (h + 1) * 512],
                              deps=[xt2_tok[(t, k)], t_phi] + deps,
                              start=(k == 0), stop=(k == 7),
                              first=(k == 0)))
        return mms

    for i in range(NT // 2):
        m0 = logits_tile(i, [])
        t_la = pread(b, "ve", LG0, lambda e: e.tensor_copy(LA[:, :], LG0[:, :]),
                     deps=[war2["LA"]])
        m1 = logits_tile(i + 8, [t_la])
        if i == 0:
            out_toks.append(b.dma("gp", o_la[:, :], LA[:, :], deps=[t_la]))
        t_dif = pread(b, "ve", LG0, lambda e: e.tensor_tensor(
            DIF[:, :], LA[:, :], LG0[:, :], op=ALU.subtract),
            deps=[t_la, war2["DIF"]])
        t_d0 = b.op("se", lambda e: e.activation(
            DS[:, 0:2048], DIF[:, :], AF.Sigmoid), deps=[t_dif, war2["DS0"]])
        t_d1 = b.op("se", lambda e: e.activation(
            DS[:, 2048:4096], DIF[:, :], AF.Sigmoid,
            scale=-1.0), deps=[t_dif, war2["DS1"]])
        t_dd0 = b.dma("sy", d_r[:, i, :], DS[:, 0:2048], deps=[t_d0])
        t_dd1 = b.dma("sy", d_r[:, i + 8, :], DS[:, 2048:4096], deps=[t_d1])
        war2["DS0"] = t_dd0
        war2["DS1"] = t_dd1
        d_toks += [t_dd0, t_dd1]
        war2["DIF"] = t_d1
        for (tt, srcap, is_psum, wkey) in ((i, LA[:, :], False, "CC0"),
                                           (i + 8, LG0[:, :], True, "CC1")):
            sb0 = 24 if not is_psum else 28
            cc = CC[:, (0 if not is_psum else 2048):(2048 if not is_psum else 4096)]
            mx = stats[:, sb0:sb0 + 1]
            sm = stats[:, sb0 + 1:sb0 + 2]
            rc = stats[:, sb0 + 2:sb0 + 3]
            if is_psum:
                t_mx0 = pread(b, "ve", LG0, lambda e, srcap=srcap, mx=mx: e.reduce_max(
                    mx, srcap, axis=AX.X), deps=[t_dif, war2[wkey]])
            else:
                t_mx0 = b.op("ve", lambda e, srcap=srcap, mx=mx: e.reduce_max(
                    mx, srcap, axis=AX.X), deps=[t_la, t_dif, war2[wkey]])
            t_mx = b.op("ve", lambda e, mx=mx: e.tensor_scalar(
                mx, mx, -1.0, None, op0=ALU.mult, op1=ALU.bypass), deps=[t_mx0])
            if is_psum:
                t_ex = pread(b, "se", LG0, lambda e, srcap=srcap, cc=cc, mx=mx, sm=sm: e.activation(
                    cc, srcap, AF.Exp, bias=mx, accum_out=sm),
                    deps=[t_mx, war2[wkey]])
            else:
                t_ex = b.op("se", lambda e, srcap=srcap, cc=cc, mx=mx, sm=sm: e.activation(
                    cc, srcap, AF.Exp, bias=mx, accum_out=sm),
                    deps=[t_mx, war2[wkey]])
            t_rc = b.op("ve", lambda e, sm=sm, rc=rc: e.reciprocal(
                rc, sm), deps=[t_ex])
            t_cn = b.op("ve", lambda e, cc=cc, rc=rc: e.tensor_scalar(
                cc, cc, rc, None, op0=ALU.mult, op1=ALU.bypass),
                deps=[t_rc, t_ex])
            t_cd = b.dma("sy", c_r[:, tt, :], cc, deps=[t_cn])
            war2[wkey] = t_cd
            out_toks.append(t_cd)
        war2["LA"] = t_dif

    PS = b.sbuf("PS", [128, 512], F32)
    DR = b.sbuf("DR", [128, 2 * 8 * 128], BF16)
    ps_dep = None
    dr_dep = [None, None]
    for bb in range(B_):
        for npc in range(16):
            r = npc % 2
            dra = DR[:, r * 1024:(r + 1) * 1024]
            t_dr = b.dma("sy",
                         dra.rearrange("p (j n) -> p j n", j=8),
                         d_r[:, bb * 8:(bb + 1) * 8,
                             npc * 128:(npc + 1) * 128],
                         deps=[dr_dep[r]] + d_toks)
            last = None
            for dsl in range(2):
                for j in range(8):
                    t = bb * 8 + j
                    last = mm(
                        b, LG0, LG0[:, 0:512],
                        dra[:, j * 128:(j + 1) * 128],
                        XN2[:, t * D + dsl * 512: t * D + (dsl + 1) * 512],
                        deps=[xn2_toks[t], t_dr], start=(j == 0),
                        stop=(j == 7), first=(j == 0))
                t_ev = pread(b, "ve", LG0, lambda e: e.tensor_copy(
                    PS[:, :], LG0[:, 0:512]), deps=[ps_dep])
                t_pd = b.dma("sy", o_ps[bb, npc * 128:(npc + 1) * 128,
                                        dsl * 512:(dsl + 1) * 512], PS[:, :],
                             deps=[t_ev])
                ps_dep = t_pd
                out_toks.append(t_pd)
            dr_dep[r] = last

    out_toks.append(out_gt_tok)
    nc = b.finalize(out_toks)
    return nc



def build_B2():
    b = Builder("B2")
    d_ps = b.din("d_ps", [NC, B_, D, 256], F32)
    d_wn = b.din("d_wn", [8, D, D], F32)
    d_bn = b.din("d_bn", [8, D], F32)
    o_y = b.dout("o_y", [B_, 8, 32, D], F32)
    o_slt = b.dout("o_slt", [128, B_ * 8 * 256], F32)

    SLT = b.sbuf("SLT", [128, B_ * 8 * 256], F32)
    SLB = b.sbuf("SLB", [128, B_ * 8 * 256], BF16)
    PSR = b.sbuf("PSR", [128, 2 * 256], F32)
    WN = b.sbuf("WN", [128, 2 * 8 * D], BF16)
    BN = b.sbuf("BN", [32, D], F32)
    YS = b.sbuf("YS", [32, 2 * D], F32)
    pY = b.psum("pY", [32, 1024])

    sum_toks = {}
    ps_ring = [None, None]
    for bb in range(B_):
        for dc in range(8):
            dst = SLT[:, (bb * 8 + dc) * 256:(bb * 8 + dc + 1) * 256]
            acc_tok = None
            for s in range(NC):
                r = s % 2
                pra = PSR[:, r * 256:(r + 1) * 256]
                t_l = b.dma("sy", pra,
                            d_ps[s, bb, dc * 128:(dc + 1) * 128, :],
                            deps=[ps_ring[r]])
                if s == 0:
                    acc_tok = b.op("ve", lambda e, pra=pra, dst=dst: e.tensor_copy(
                        dst, pra), deps=[t_l, acc_tok])
                else:
                    acc_tok = b.op("ve", lambda e, pra=pra, dst=dst: e.tensor_tensor(
                        dst, dst, pra, op=ALU.add), deps=[t_l, acc_tok])
                ps_ring[r] = acc_tok
            sum_toks[(bb, dc)] = b.op("ve", lambda e, dst=dst, bb=bb, dc=dc: e.tensor_copy(
                SLB[:, (bb * 8 + dc) * 256:(bb * 8 + dc + 1) * 256], dst),
                deps=[acc_tok])

    out_toks = []
    out_toks.append(b.dma("gp", o_slt[:, :], SLT[:, :],
                          deps=list(sum_toks.values())))
    wn_ring = [None, None]
    ys_ring = [None, None]
    bn_dep = None
    for j in range(8):
        r = j % 2
        wna = WN[:, r * 8 * D:(r + 1) * 8 * D]
        t_w = b.dma("gp", wna.rearrange("p (c d) -> p c d", c=8),
                    d_wn[j].rearrange("(c p) d -> p c d", p=128),
                    deps=[wn_ring[r]])
        t_b = b.dma("sy", BN[:, :], d_bn[j:j + 1, :].partition_broadcast(32),
                    deps=[wn_ring[r], bn_dep])
        for bb in range(B_):
            yr = bb % 2
            ysa = YS[:, yr * D:(yr + 1) * D]
            last = None
            for h in range(2):
                for dc in range(8):
                    last = mm(b, pY, pY[:, h * 512:(h + 1) * 512],
                              SLB[:, (bb * 8 + dc) * 256 + 32 * j:
                                   (bb * 8 + dc) * 256 + 32 * j + 32],
                              wna[:, dc * D + h * 512: dc * D + (h + 1) * 512],
                              deps=[sum_toks[(bb, dc)], t_w],
                              start=(dc == 0), stop=(dc == 7), first=(dc == 0))
            t_ev = pread(b, "ve", pY, lambda e, ysa=ysa: e.tensor_tensor(
                ysa, pY[0:32, :], BN[:, :], op=ALU.add),
                deps=[t_b, ys_ring[yr]])
            bn_dep = t_ev
            t_yd = b.dma("sy", o_y[bb, j, :, :], ysa, deps=[t_ev])
            ys_ring[yr] = t_yd
            out_toks.append(t_yd)
        wn_ring[r] = last

    nc = b.finalize(out_toks)
    return nc



def build_BC():
    b = Builder("BC")
    d_y = b.din("d_y", [B_, 2048, D], F32)
    d_c = b.din("d_c", [TLOC, 2048], BF16)
    d_x1 = b.din("d_x1", [TLOC, D], F32)
    w_attn = b.din("w_attn", [D, 3 * D], F32)
    b_attn = b.din("b_attn", [1, 3 * D], F32)
    o_x2 = b.dout("o_x2", [TLOC, D], F32)
    o_q = b.dout("o_q", [TLOC, D], BF16)
    o_k = b.dout("o_k", [TLOC, D], BF16)
    o_v = b.dout("o_v", [TLOC, D], BF16)

    ident, tid = emit_const_identity(b)
    YB = b.sbuf("YB", [128, B_ * 16 * D], BF16)
    GEN8 = b.sbuf("GEN8", [128, 16 * TLOC], BF16)
    XN5 = b.sbuf("XN5", [128, NT * D], BF16)
    X1R = b.sbuf("X1R", [128, 2 * D], F32)
    FF = b.sbuf("FF", [128, D], F32)
    TMP = b.sbuf("TMP", [128, D], F32)
    X2R = b.sbuf("X2R", [128, 2 * D], F32)
    BA = b.sbuf("BA", [128, 3 * D], F32)
    QS = b.sbuf("QS", [128, 3 * D], BF16)
    scratch = b.sbuf("scratch", [128, D], F32)
    stats = b.sbuf("stats", [128, 32], F32)

    pO = b.psum("pO", [128, 1024])
    pQ = b.psum("pQ", [128, 1024])
    pTs = [b.psum("pT0", [128, 128], BF16), b.psum("pT1", [128, 128], BF16)]

    t_y = []
    for bb in range(B_):
        t_y.append(b.dma("gp",
                         YB[:, bb * 16 * D:(bb + 1) * 16 * D].rearrange(
                             "p (n d) -> p n d", n=16),
                         d_y[bb].rearrange("(n p) d -> p n d", p=128)))
    ct_toks = []
    for npc in range(16):
        ct_toks.append(b.dma("sy", GEN8[:, npc * TLOC:(npc + 1) * TLOC],
                             d_c[:, npc * 128:(npc + 1) * 128], transpose=True))
    t_ba = b.dma("gp", BA[:, :], b_attn[0:1, :].partition_broadcast(128))
    t_baq = b.op("ve", lambda e: e.tensor_scalar(
        BA[:, 0:D], BA[:, 0:D], 0.125, None, op0=ALU.mult, op1=ALU.bypass),
        deps=[t_ba])

    x1_r = d_x1.rearrange("(t p) d -> p t d", p=128)
    x2_r = o_x2.rearrange("(t p) d -> p t d", p=128)
    ring = [None, None]
    x2ring = [None, None]
    war = {"FF": None, "TMP": None}
    scdep = None
    xn5_toks = []
    out_toks = []
    for t in range(NT):
        bb = t // 8
        r = t % 2
        x1a = X1R[:, r * D:(r + 1) * D]
        t_x1 = b.dma("sy", x1a, x1_r[:, t, :], deps=[ring[r]])
        for h in range(2):
            for npc in range(16):
                mm(b, pO, pO[:, h * 512:(h + 1) * 512],
                   GEN8[:, npc * TLOC + t * 128: npc * TLOC + (t + 1) * 128],
                   YB[:, (bb * 16 + npc) * D + h * 512:
                      (bb * 16 + npc) * D + (h + 1) * 512],
                   deps=[ct_toks[npc], t_y[bb]],
                   start=(npc == 0), stop=(npc == 15), first=(npc == 0))
        t_ff = pread(b, "ve", pO, lambda e: e.tensor_copy(FF[:, :], pO[:, :]),
                     deps=[war["FF"]])
        x2a = X2R[:, r * D:(r + 1) * D]
        t_x2, scdep = emit_ln_res(b, FF[:, :], [t_ff], x1a, [t_x1], x2a,
                                  TMP[:, :], scratch[:, :], stats, 0,
                                  scdep=scdep)
        war["FF"] = t_x2
        ring[r] = t_x2
        out_toks.append(b.dma("sy", x2_r[:, t, :], x2a, deps=[t_x2]))
        t_xn5, scdep = emit_ln_tile(b, x2a, [t_x2], XN5[:, t * D:(t + 1) * D],
                                    scratch[:, :], stats, 8, scdep=scdep)
        xn5_toks.append(t_xn5)
        x2ring[r] = t_xn5
        ring[r] = t_xn5

    WA = YB
    t_wa = b.dma("gp", WA[:, 0:8 * 3 * D].rearrange("p (c d) -> p c d", c=8),
                 w_attn.rearrange("(c p) d -> p c d", p=128),
                 deps=b.barrier_tokens())
    xt_tok = {}
    outs = {0: o_q, 1: o_k, 2: o_v}
    qs_dep = [None]
    for t in range(NT):
        for k in range(8):
            pt = pTs[k % 2]
            ttok = b.op("te", lambda e, t=t, k=k, pt=pt: e.transpose(
                pt[:, :], XN5[:, t * D + k * 128: t * D + (k + 1) * 128],
                ident[:, :]),
                deps=[xn5_toks[t], tid, t_wa] + pt.write_deps())
            pt.note_write(ttok)
            xt_tok[(t, k)] = pread(b, "ve", pt,
                lambda e, t=t, k=k, pt=pt: e.tensor_copy(
                    GEN8[:, k * TLOC + t * 128: k * TLOC + (t + 1) * 128],
                    pt[:, :]))
        ev_toks = []
        for cs in range(6):
            ph = pQ if cs % 2 == 0 else pO
            for k in range(8):
                mm(b, ph, ph[:, 0:512],
                   GEN8[:, k * TLOC + t * 128: k * TLOC + (t + 1) * 128],
                   WA[:, k * 3 * D + cs * 512: k * 3 * D + (cs + 1) * 512],
                   deps=[xt_tok[(t, k)], t_wa], start=(k == 0), stop=(k == 7),
                   first=(k == 0))
            ev_toks.append(pread(b, "ve", ph,
                lambda e, cs=cs, ph=ph: e.scalar_tensor_tensor(
                    QS[:, cs * 512:(cs + 1) * 512], ph[:, 0:512],
                    0.125 if cs < 2 else 1.0, BA[:, cs * 512:(cs + 1) * 512],
                    op0=ALU.mult, op1=ALU.add),
                deps=[t_baq, qs_dep[0]]))
        for pi in range(3):
            tok = b.dma("sy", outs[pi].rearrange("(t p) d -> p t d", p=128)[:, t, :],
                        QS[:, pi * D:(pi + 1) * D],
                        deps=[ev_toks[2 * pi], ev_toks[2 * pi + 1]])
            out_toks.append(tok)
        qs_dep[0] = out_toks[-1]


    nc = b.finalize(out_toks)
    return nc



def build_C2(nu=4, nqg=16):
    b = Builder("C2")
    d_qf = b.din("d_qf", [nu, 64, SEQ], BF16)
    d_kf = b.din("d_kf", [nu, 64, SEQ], BF16)
    d_v = b.din("d_v", [nu, SEQ, 64], BF16)
    d_o = b.dout("d_o", [nu, nqg, 65, 512], F32)

    QF = b.sbuf("QF", [64, 4 * SEQ], BF16)
    KF = b.sbuf("KF", [64, 4 * SEQ], BF16)
    V65 = b.sbuf("V65", [128, 4 * 64 * 65], BF16)
    PB = b.sbuf("PB", [128, 2 * 1024], BF16)
    PMD = b.sbuf("PMD", [128, 512], BF16)
    DM = b.sbuf("DM", [128, 4 * 512], BF16)
    OS = b.sbuf("OS", [65, 2 * 512], F32)
    B15 = b.sbuf("B15", [128, 1], F32)

    p_sc = b.psum("p_sc", [128, 1024])
    pOs = [b.psum("pO0", [65, 512]), b.psum("pO1", [65, 512])]

    ld_toks = []
    for u in range(nu):
        ld_toks.append(b.dma("sy", QF[:, u * SEQ:(u + 1) * SEQ], d_qf[u]))
        ld_toks.append(b.dma("sy", KF[:, u * SEQ:(u + 1) * SEQ], d_kf[u]))
    t_b15 = b.op("gp", lambda e: e.memset(B15[:, :], -15.0))
    t_vm = b.op("gp", lambda e: e.memset(V65[:, :], 1.0))
    v_toks = {}
    for u in range(nu):
        v_toks[u] = b.dma("gp",
                          V65[:, u * 4160:(u + 1) * 4160].rearrange(
                              "p (ch w) -> p ch w", w=65)[:, :, 0:64],
                          d_v[u].rearrange("(ch p) v -> p ch v", p=128),
                          deps=[t_vm])
    dm_toks = []
    for dd in range(4):
        t0 = b.op("gp", lambda e, dd=dd: e.memset(DM[:, dd * 512:(dd + 1) * 512], 1.0))
        dm_toks.append(b.op("gp", lambda e, dd=dd: e.affine_select(
            out=DM[:, dd * 512:(dd + 1) * 512],
            in_=DM[:, dd * 512:(dd + 1) * 512],
            pattern=[[1, 512]], compare_op=ALU.is_ge, fill=0.0,
            base=-128 * dd, channel_multiplier=-1), deps=[t0]))

    out_toks = []
    war_pb = [None, None]
    war_pmd = None
    os_dep = [None, None]
    for u in range(nu):
        for qg in range(nqg):
            nk = 4 * qg + 4
            pO = pOs[qg % 2]
            qsl = QF[0:64, u * SEQ + qg * 512: u * SEQ + (qg + 1) * 512]
            for kc2 in range(nk // 2):
                sc_toks = []
                for h2 in range(2):
                    kc = 2 * kc2 + h2
                    sc_toks.append(mm(
                        b, p_sc, p_sc[:, h2 * 512:(h2 + 1) * 512],
                        KF[0:64, u * SEQ + kc * 128: u * SEQ + (kc + 1) * 128],
                        qsl, deps=[ld_toks[2 * u], ld_toks[2 * u + 1]],
                        start=True, stop=True, first=(h2 == 0)))
                pbh = (2 * kc2) % 2
                pbh = kc2 % 2
                pba = PB[:, pbh * 1024:(pbh + 1) * 1024]
                t_exp = pread(b, "se", p_sc, lambda e, pba=pba: e.activation(
                    pba, p_sc[:, :], AF.Exp, bias=B15[:, :]),
                    deps=[war_pb[pbh], t_b15])
                for h2 in range(2):
                    kc = 2 * kc2 + h2
                    dd = kc - (nk - 4)
                    psrc = pba[:, h2 * 512:(h2 + 1) * 512]
                    extra = []
                    if dd >= 0:
                        t_msk = b.op("ve", lambda e, psrc=psrc, dd=dd: e.tensor_tensor(
                            PMD[:, :], psrc, DM[:, dd * 512:(dd + 1) * 512],
                            op=ALU.mult),
                            deps=[t_exp, dm_toks[dd], war_pmd])
                        psrc = PMD[:, :]
                        extra = [t_msk]
                    t_av = mm(b, pO, pO[:, :],
                              V65[:, u * 4160 + kc * 65: u * 4160 + (kc + 1) * 65],
                              psrc, deps=[t_exp, v_toks[u]] + extra,
                              start=(kc == 0), stop=(kc == nk - 1),
                              first=(kc == 0))
                    if dd >= 0:
                        war_pmd = t_av
                war_pb[pbh] = t_av
            osl = OS[:, (qg % 2) * 512:(qg % 2) * 512 + 512]
            t_oe = pread(b, "ve", pO, lambda e, osl=osl, pO=pO: e.tensor_copy(
                osl, pO[:, :]), deps=[os_dep[qg % 2]])
            t_od = b.dma("sy", d_o[u, qg], osl, deps=[t_oe])
            os_dep[qg % 2] = t_od
            out_toks.append(t_od)

    nc = b.finalize(out_toks)
    return nc



def build_CD():
    b = Builder("CD")
    d_yin = b.din("d_yin", [TLOC, D], F32)
    d_sums = b.din("d_sums", [TLOC, 16], F32)
    d_x2 = b.din("d_x2", [TLOC, D], F32)
    w_proj = b.din("w_proj", [D, D], F32)
    b_proj = b.din("b_proj", [1, D], F32)
    rw_T = b.din("rw_T", [D, 16], F32)
    cbias = b.din("cbias", [1, 16], F32)
    d_gw = b.din("d_gw", [24, D, 256], F32)
    d_uw = b.din("d_uw", [24, D, 256], F32)
    d_dw = b.din("d_dw", [24, 256, D], F32)
    o_x4 = b.dout("o_x4", [TLOC, D], F32)
    o_x3 = b.dout("o_x3", [TLOC, D], F32)
    o_ff = b.dout("o_ff", [TLOC, D], F32)
    o_cb = b.dout("o_cb", [128, NT * 24], F32)
    o_scb = b.dout("o_scb", [128, NT * 16], F32)
    d_x3 = b.dtmp("d_x3", [TLOC, D], F32)

    ident, tid = emit_const_identity(b)
    XT4 = b.sbuf("XT4", [128, 8 * TLOC], BF16)
    HF = b.sbuf("HF", [128, 24 * 1024], BF16)
    WP = b.sbuf("WP", [128, 8 * D], BF16)
    RWT = b.sbuf("RWT", [128, 8 * 16], BF16)
    GW = b.sbuf("GW", [128, 2 * 2048], BF16)
    UW = b.sbuf("UW", [128, 2 * 2048], BF16)
    DWB = b.sbuf("DWB", [128, 2 * 1024], BF16)
    BP = b.sbuf("BP", [128, D], F32)
    CBC = b.sbuf("CBC", [128, 16], F32)
    YR = b.sbuf("YR", [128, 2 * D], F32)
    SR = b.sbuf("SR", [128, 2 * 16], F32)
    X2R = b.sbuf("X2R", [128, 2 * D], F32)
    X3R = b.sbuf("X3R", [128, 2 * D], F32)
    YN = b.sbuf("YN", [128, D], BF16)
    YT = b.sbuf("YT", [128, D], BF16)
    AT = b.sbuf("AT", [128, D], F32)
    TMP = b.sbuf("TMP", [128, D], F32)
    XN4 = b.sbuf("XN4", [128, D], BF16)
    GS = b.sbuf("GS_", [128, 256], BF16)
    HT = b.sbuf("HT", [128, 256], BF16)
    FF4 = b.sbuf("FF4", [128, 4 * D], F32)
    CB = b.sbuf("CB", [128, NT * 24], F32)
    SCB = b.sbuf("SCB", [128, 16], F32)
    SB2 = b.sbuf("SB2", [128, 16], F32)
    SB2o = b.sbuf("SB2o", [128, 16], F32)
    GRP = b.sbuf("GRP", [128, 4 * 8], F32)
    EM = b.sbuf("EM", [128, 16], F32)
    X4R = b.sbuf("X4R", [128, 2 * D], F32)
    scratch = b.sbuf("scratch", [128, D], F32)
    stats = b.sbuf("stats", [128, 48], F32)

    T0 = b.psum("T0", [128, 1024])
    T1 = b.psum("T1", [128, 1024])
    pTs = [b.psum("pT0", [128, 128], BF16), b.psum("pT1", [128, 128], BF16)]
    pR = b.psum("pR", [128, 512])

    t_wp = b.dma("gp", WP[:, :].rearrange("p (c d) -> p c d", c=8),
                 w_proj.rearrange("(c p) d -> p c d", p=128))
    t_bp = b.dma("gp", BP[:, :], b_proj[0:1, :].partition_broadcast(128))
    t_rw = b.dma("gp", RWT[:, :].rearrange("p (c r) -> p c r", c=8),
                 rw_T.rearrange("(c p) r -> p c r", p=128))
    t_cbc = b.dma("gp", CBC[:, :], cbias[0:1, :].partition_broadcast(128))
    t_cb1 = b.op("gp", lambda e: e.memset(CB[:, :], 1.0))

    y_r = d_yin.rearrange("(t p) d -> p t d", p=128)
    s_r = d_sums.rearrange("(t p) r -> p t r", p=128)
    x2_r = d_x2.rearrange("(t p) d -> p t d", p=128)
    x3_r = d_x3.rearrange("(t p) d -> p t d", p=128)
    x4_r = o_x4.rearrange("(t p) d -> p t d", p=128)

    ring = {k: [None, None] for k in ("YR", "SR", "X2R", "X3R", "X4R")}
    war = {k: None for k in ("YN", "YT", "AT", "TMP", "XN4", "SCB", "SB2",
                             "SB2o", "GRP", "EM", "rs")}
    scdep = None
    out_toks = []
    x3_toks = []
    cb_toks = {}
    BIG = 16.0

    for t in range(NT):
        r = t % 2
        ya = YR[:, r * D:(r + 1) * D]
        sa = SR[:, r * 16:(r + 1) * 16]
        x2a = X2R[:, r * D:(r + 1) * D]
        t_y = b.dma("sy", ya, y_r[:, t, :], deps=[ring["YR"][r]])
        t_s = b.dma("sy", sa, s_r[:, t, :], deps=[ring["SR"][r]])
        t_x2 = b.dma("sy", x2a, x2_r[:, t, :], deps=[ring["X2R"][r]])
        rs = stats[:, 32:48]
        t_rs = b.op("ve", lambda e, sa=sa, rs=rs: e.reciprocal(rs, sa),
                    deps=[t_s, war["rs"]])
        t_yn = b.op("ve", lambda e, ya=ya, rs=rs: e.tensor_tensor(
            YN[:, :].rearrange("p (h w) -> p h w", h=16),
            ya.rearrange("p (h w) -> p h w", h=16),
            rs.unsqueeze(2).broadcast_to([128, 16, 64]), op=ALU.mult),
            deps=[t_rs, t_y, war["YN"]])
        ring["YR"][r] = t_yn
        war["rs"] = t_yn
        yt_toks = []
        for k in range(8):
            pt = pTs[k % 2]
            ttok = b.op("te", lambda e, k=k, pt=pt: e.transpose(
                pt[:, :], YN[:, k * 128:(k + 1) * 128], ident[:, :]),
                deps=[t_yn, tid] + pt.write_deps())
            pt.note_write(ttok)
            yt_toks.append(pread(b, "ve", pt, lambda e, k=k, pt=pt: e.tensor_copy(
                YT[:, k * 128:(k + 1) * 128], pt[:, :]), deps=[war["YT"]]))
        war["YN"] = yt_toks[-1]
        for h in range(2):
            for k in range(8):
                mm(b, T0, T0[:, h * 512:(h + 1) * 512],
                   YT[:, k * 128:(k + 1) * 128],
                   WP[:, k * D + h * 512: k * D + (h + 1) * 512],
                   deps=[yt_toks[k], t_wp], start=(k == 0), stop=(k == 7),
                   first=(k == 0))
        t_at = pread(b, "ve", T0, lambda e: e.tensor_tensor(
            AT[:, :], T0[:, :], BP[:, :], op=ALU.add), deps=[t_bp, war["AT"]])
        war["YT"] = t_at
        x3a = X3R[:, r * D:(r + 1) * D]
        t_x3, scdep = emit_ln_res(b, AT[:, :], [t_at], x2a, [t_x2], x3a,
                                  TMP[:, :], scratch[:, :], stats, 0,
                                  scdep=scdep)
        war["AT"] = t_x3
        ring["X2R"][r] = t_x3
        t_x3d = b.dma("sy", x3_r[:, t, :], x3a, deps=[t_x3])
        x3_toks.append(t_x3d)
        out_toks.append(b.dma("gp", o_x3.rearrange("(t p) d -> p t d", p=128)[:, t, :],
                              x3a, deps=[t_x3]))
        t_xn4, scdep = emit_ln_tile(b, x3a, [t_x3, war["XN4"]], XN4[:, :],
                                    scratch[:, :], stats, 8, scdep=scdep)
        ring["X3R"][r] = t_xn4
        xt_toks = []
        for k in range(8):
            pt = pTs[k % 2]
            ttok = b.op("te", lambda e, t=t, k=k, pt=pt: e.transpose(
                pt[:, :], XN4[:, k * 128:(k + 1) * 128], ident[:, :]),
                deps=[t_xn4, tid] + pt.write_deps())
            pt.note_write(ttok)
            xt_toks.append(pread(b, "ve", pt,
                lambda e, t=t, k=k, pt=pt: e.tensor_copy(
                    XT4[:, k * TLOC + t * 128: k * TLOC + (t + 1) * 128],
                    pt[:, :])))
        war["XN4"] = xt_toks[-1]
        for k in range(8):
            mm(b, pR, pR[:, 0:16],
               XT4[:, k * TLOC + t * 128: k * TLOC + (t + 1) * 128],
               RWT[:, k * 16:(k + 1) * 16],
               deps=[xt_toks[k], t_rw], start=(k == 0), stop=(k == 7),
               first=(k == 0))
        t_sc = pread(b, "se", pR, lambda e: e.activation(
            SCB[:, :], pR[:, 0:16], AF.Sigmoid), deps=[war["SCB"]])
        t_sb = b.op("ve", lambda e: e.tensor_tensor(
            SB2[:, :], SCB[:, :], CBC[:, :], op=ALU.add),
            deps=[t_sc, t_cbc, war["SB2"]])
        s3 = SB2[:, :].rearrange("p (g m) -> p g m", g=4)
        pair_toks = []
        pairs = [(0, 1), (0, 2), (0, 3), (1, 2), (1, 3), (2, 3)]
        for pi2, (i1, i2) in enumerate(pairs):
            pair_toks.append(b.op("ve", lambda e, s3=s3, pi2=pi2, i1=i1, i2=i2:
                e.tensor_tensor(GRP[:, pi2 * 4:(pi2 + 1) * 4],
                                s3[:, :, i1], s3[:, :, i2], op=ALU.add),
                deps=[t_sb, war["GRP"]]))
        t_m = b.op("ve", lambda e: e.tensor_tensor(
            GRP[:, 24:28], GRP[:, 0:4], GRP[:, 4:8], op=ALU.max),
            deps=pair_toks)
        for pi2 in range(2, 6):
            t_m = b.op("ve", lambda e, pi2=pi2: e.tensor_tensor(
                GRP[:, 24:28], GRP[:, 24:28], GRP[:, pi2 * 4:(pi2 + 1) * 4],
                op=ALU.max), deps=[t_m])
        t_m1 = b.op("ve", lambda e: e.reduce_max(
            stats[:, 16:17], GRP[:, 24:28], axis=AX.X), deps=[t_m])
        t_im = b.op("ve", lambda e: e.tensor_scalar(
            GRP[:, 28:32], GRP[:, 24:28], stats[:, 16:17], None,
            op0=ALU.is_ge, op1=ALU.bypass), deps=[t_m1])
        t_gm = b.op("ve", lambda e: e.scalar_tensor_tensor(
            GRP[:, 0:4], GRP[:, 28:32], -BIG, GRP[:, 24:28],
            op0=ALU.mult, op1=ALU.add), deps=[t_im])
        t_m2 = b.op("ve", lambda e: e.reduce_max(
            stats[:, 17:18], GRP[:, 0:4], axis=AX.X), deps=[t_gm])
        t_gmask = b.op("ve", lambda e: e.tensor_scalar(
            GRP[:, 28:32], GRP[:, 24:28], stats[:, 17:18], None,
            op0=ALU.is_ge, op1=ALU.bypass), deps=[t_m2, t_im])
        t_em = b.op("ve", lambda e: e.tensor_copy(
            EM[:, :].rearrange("p (g m) -> p g m", g=4),
            GRP[:, 28:32].unsqueeze(2).broadcast_to([128, 4, 4])),
            deps=[t_gmask, war["EM"]])
        t_mk = b.op("ve", lambda e: e.scalar_tensor_tensor(
            SB2o[:, :], EM[:, :], BIG, SB2[:, :], op0=ALU.mult, op1=ALU.add),
            deps=[t_em, war["SB2o"]])
        t_mk2 = b.op("ve", lambda e: e.tensor_scalar(
            SB2[:, :], SB2o[:, :], BIG, None, op0=ALU.subtract, op1=ALU.bypass),
            deps=[t_mk])
        t_cp = b.op("ve", lambda e: e.tensor_copy(SB2o[:, :], SB2[:, :]),
                    deps=[t_mk2])
        war["SB2"] = t_cp
        t_prev = t_cp
        for it in range(3):
            t_mx = b.op("ve", lambda e: e.reduce_max(
                stats[:, 18:19], SB2[:, :], axis=AX.X), deps=[t_prev])
            t_is = b.op("ve", lambda e: e.tensor_scalar(
                EM[:, :], SB2[:, :], stats[:, 18:19], None,
                op0=ALU.is_ge, op1=ALU.bypass), deps=[t_mx])
            t_prev = b.op("ve", lambda e: e.scalar_tensor_tensor(
                SB2[:, :], EM[:, :], -BIG, SB2[:, :], op0=ALU.mult, op1=ALU.add),
                deps=[t_is])
        t_tau = b.op("ve", lambda e: e.reduce_max(
            stats[:, 18:19], SB2[:, :], axis=AX.X), deps=[t_prev])
        war["EM"] = t_tau
        t_sel = b.op("ve", lambda e: e.tensor_scalar(
            EM[:, :], SB2o[:, :], stats[:, 18:19], None,
            op0=ALU.is_ge, op1=ALU.bypass), deps=[t_tau, t_cp])
        war["SB2o"] = t_sel
        t_w = b.op("ve", lambda e: e.tensor_tensor(
            SB2[:, :], SCB[:, :], EM[:, :], op=ALU.mult), deps=[t_sel, t_sc])
        war["SCB"] = t_w
        t_ws = b.op("ve", lambda e: e.reduce_sum(
            stats[:, 19:20], SB2[:, :], axis=AX.X), deps=[t_w])
        t_wse = b.op("ve", lambda e: e.tensor_scalar(
            stats[:, 19:20], stats[:, 19:20], 1e-20, None,
            op0=ALU.add, op1=ALU.bypass), deps=[t_ws])
        t_rc = b.op("ve", lambda e: e.reciprocal(
            stats[:, 20:21], stats[:, 19:20]), deps=[t_wse])
        t_cb = b.op("ve", lambda e, t=t: e.tensor_scalar(
            CB[:, t * 24:t * 24 + 16], SB2[:, :], stats[:, 20:21], None,
            op0=ALU.mult, op1=ALU.bypass), deps=[t_rc, t_w, t_cb1])
        cb_toks[t] = t_cb
        out_toks.append(b.dma("gp", o_cb[:, t * 24:(t + 1) * 24],
                              CB[:, t * 24:(t + 1) * 24], deps=[t_cb]))
        out_toks.append(b.dma("gp", o_scb[:, t * 16:(t + 1) * 16],
                              SCB[:, :], deps=[t_sc, t_w]))
        war["GRP"] = t_gmask

    gw_ring = [None, None]
    uw_ring = [None, None]
    dw_ring = [None, None]
    hf_guard = None
    gs_war = None
    ht_war = None
    ff_war = {}
    for tg in range(4):
        hf_toks = []
        for e2 in range(24):
            rr = e2 % 2
            gwa = GW[:, rr * 2048:(rr + 1) * 2048]
            uwa = UW[:, rr * 2048:(rr + 1) * 2048]
            t_gw = b.dma("gp", gwa.rearrange("p (c h) -> p c h", c=8),
                         d_gw[e2].rearrange("(c p) h -> p c h", p=128),
                         deps=[gw_ring[rr]])
            t_uw = b.dma("gp", uwa.rearrange("p (c h) -> p c h", c=8),
                         d_uw[e2].rearrange("(c p) h -> p c h", p=128),
                         deps=[uw_ring[rr]])
            last_g = last_u = None
            for ti in range(4):
                t_abs = tg * 4 + ti
                T = T0 if ti % 2 == 0 else T1
                for k in range(8):
                    last_g = mm(b, T, T[:, 0:256],
                                XT4[:, k * TLOC + t_abs * 128:
                                    k * TLOC + (t_abs + 1) * 128],
                                gwa[:, k * 256:(k + 1) * 256],
                                deps=[t_gw], start=(k == 0), stop=(k == 7),
                                first=(k == 0))
                for k in range(8):
                    last_u = mm(b, T, T[:, 512:768],
                                XT4[:, k * TLOC + t_abs * 128:
                                    k * TLOC + (t_abs + 1) * 128],
                                uwa[:, k * 256:(k + 1) * 256],
                                deps=[t_uw], start=(k == 0), stop=(k == 7),
                                first=(k == 0))
                t_si = pread(b, "se", T, lambda e, T=T: e.activation(
                    GS[:, :], T[:, 0:256], AF.Silu), deps=[gs_war])
                t_h = pread(b, "ve", T, lambda e, T=T, t_abs=t_abs, e2=e2:
                    e.scalar_tensor_tensor(
                        HT[:, :], T[:, 512:768], CB[:, t_abs * 24 + e2:
                                                    t_abs * 24 + e2 + 1],
                        GS[:, :], op0=ALU.mult, op1=ALU.mult),
                    deps=[t_si, cb_toks[t_abs], ht_war])
                gs_war = t_h
                tr_toks = []
                for hc in range(2):
                    pt = pTs[hc]
                    ttok = b.op("te", lambda e, hc=hc, pt=pt: e.transpose(
                        pt[:, :], HT[:, hc * 128:(hc + 1) * 128], ident[:, :]),
                        deps=[t_h, tid] + pt.write_deps())
                    pt.note_write(ttok)
                    tr_toks.append(pread(b, "ve", pt,
                        lambda e, e2=e2, hc=hc, ti=ti, pt=pt: e.tensor_copy(
                            HF[:, e2 * 1024 + hc * 512 + ti * 128:
                               e2 * 1024 + hc * 512 + (ti + 1) * 128],
                            pt[:, :]), deps=[hf_guard]))
                ht_war = tr_toks[-1]
                hf_toks += tr_toks
            gw_ring[rr] = last_g
            uw_ring[rr] = last_u
        dw_last = None
        for dsl in range(2):
            ev_toks = []
            for e2 in range(24):
                rr = e2 % 2
                dwa = DWB[:, rr * 1024:(rr + 1) * 1024]
                t_dw = b.dma("gp", dwa.rearrange("p (hc d) -> p hc d", hc=2),
                             d_dw[e2].rearrange("(hc p) d -> p hc d", p=128)
                             [:, :, dsl * 512:(dsl + 1) * 512],
                             deps=[dw_ring[rr]])
                last_dw = None
                for ti in range(4):
                    T = T0 if ti < 2 else T1
                    col = (ti % 2) * 512
                    for hc in range(2):
                        last_dw = mm(
                            b, T, T[:, col:col + 512],
                            HF[:, e2 * 1024 + hc * 512 + ti * 128:
                               e2 * 1024 + hc * 512 + (ti + 1) * 128],
                            dwa[:, hc * 512:(hc + 1) * 512],
                            deps=[t_dw] + hf_toks,
                            start=(e2 == 0 and hc == 0),
                            stop=(e2 == 23 and hc == 1),
                            first=(e2 == 0 and hc == 0))
                dw_ring[rr] = last_dw
            for ti in range(4):
                T = T0 if ti < 2 else T1
                col = (ti % 2) * 512
                ev_toks.append(pread(b, "ve", T,
                    lambda e, ti=ti, T=T, col=col, tg=tg, dsl=dsl: e.tensor_copy(
                        FF4[:, ti * D + dsl * 512: ti * D + (dsl + 1) * 512],
                        T[:, col:col + 512]),
                    deps=[ff_war.get((tg - 1, ti))]))
                ff_war[(tg, ti, dsl)] = ev_toks[-1]
        for ti in range(4):
            t_abs = tg * 4 + ti
            r = t_abs % 2
            x3a = X3R[:, r * D:(r + 1) * D]
            t_x3l = b.dma("sy", x3a, x3_r[:, t_abs, :],
                          deps=[ring["X3R"][r], x3_toks[t_abs]])
            x4a = X4R[:, r * D:(r + 1) * D]
            t_x4, scdep = emit_ln_res(
                b, FF4[:, ti * D:(ti + 1) * D],
                [ff_war[(tg, ti, 0)], ff_war[(tg, ti, 1)]],
                x3a, [t_x3l], x4a, TMP[:, :], scratch[:, :], stats, 24,
                scdep=scdep)
            war["TMP"] = t_x4
            ring["X3R"][r] = t_x4
            out_toks.append(b.dma("gp",
                o_ff.rearrange("(t p) d -> p t d", p=128)[:, t_abs, :],
                FF4[:, ti * D:(ti + 1) * D],
                deps=[ff_war[(tg, ti, 0)], ff_war[(tg, ti, 1)], t_x4]))
            t_od = b.dma("sy", x4_r[:, t_abs, :], x4a, deps=[t_x4, ring["X4R"][r]])
            ring["X4R"][r] = t_od
            out_toks.append(t_od)
            ff_war[(tg, ti)] = t_x4
        hf_guard = max(dw_ring[0], dw_ring[1], key=lambda tk: tk[1])

    nc = b.finalize(out_toks)
    return nc



def _shard_rows(a):
    out = []
    for c in range(NC):
        sl = a[:, c * MLOC:(c + 1) * MLOC]
        out.append(np.ascontiguousarray(sl.reshape((TLOC,) + a.shape[2:])))
    return out


def kernel(L0_attn_x, L0_attn_Wq, L0_attn_Wk, L0_attn_Wv, L0_attn_Wg,
           L0_attn_Wgk1, L0_attn_Wgk2, L0_attn_bgk2, L0_attn_gamma,
           L0_attn_beta, L0_attn_Wout, L0_ffn_phi, L0_ffn_W, L0_ffn_b,
           L1_attn_w_attn, L1_attn_b_attn, L1_attn_w_proj, L1_attn_b_proj,
           L1_ffn_router_w, L1_ffn_correction_bias, L1_ffn_gate_w,
           L1_ffn_up_w, L1_ffn_down_w, L1_ffn_shared_gate_w,
           L1_ffn_shared_up_w, L1_ffn_shared_down_w, _upto=None, _dbg=None):
    x = np.asarray(L0_attn_x, dtype=np.float32)
    xs = _shard_rows(x)

    nc = build_A1()
    in_maps = []
    for c in range(NC):
        in_maps.append({
            "x": xs[c],
            "wq": np.asarray(L0_attn_Wq, np.float32),
            "wk": np.asarray(L0_attn_Wk, np.float32),
            "wv": np.asarray(L0_attn_Wv, np.float32),
            "wg": np.asarray(L0_attn_Wg, np.float32),
            "wgk1": np.asarray(L0_attn_Wgk1, np.float32),
            "wgk2": np.asarray(L0_attn_Wgk2, np.float32),
            "bgk2": np.asarray(L0_attn_bgk2, np.float32).reshape(1, D),
        })
    res = run_bass_kernel_spmd(nc, in_maps, core_ids=list(range(NC)))
    if _upto == "A1":
        return [r for r in res.results]
    rA1 = res.results

    def asm(key, dt):
        full = np.empty((B_, SEQ, D), dt)
        for c in range(NC):
            full[:, c * MLOC:(c + 1) * MLOC] = rA1[c][key].reshape(B_, MLOC, D)
        return full
    qf = asm("o_q", BF)
    kf = asm("o_k", BF)
    vf = asm("o_v", BF)
    gkf = asm("o_gk", np.float32)

    nc2 = build_A2()
    in_maps2 = []
    for c in range(NC):
        bsel = c // 4
        heads = [4 * (c % 4) + u for u in range(4)]
        qr = qf[bsel].reshape(NCH, 128, 16, 64)[:, :, heads]
        kr = kf[bsel].reshape(NCH, 128, 16, 64)[:, :, heads]
        vr = vf[bsel].reshape(NCH, 128, 16, 64)[:, :, heads]
        gr = gkf[bsel].reshape(NCH, 128, 16, 64)[:, :, heads]
        d_fm = np.zeros((NCH, 64, 1024), BF)
        for u in range(4):
            d_fm[:, :, u * 128:(u + 1) * 128] = qr[:, :, u].transpose(0, 2, 1)
            d_fm[:, :, 512 + u * 128:512 + (u + 1) * 128] = kr[:, :, u].transpose(0, 2, 1)
        d_tv = np.zeros((NCH, 128, 512), BF)
        d_tv[:, :, 0:256] = kr.reshape(NCH, 128, 256)
        d_tv[:, :, 256:512] = vr.reshape(NCH, 128, 256)
        d_gk = np.ascontiguousarray(gr.reshape(NCH, 128, 256))
        in_maps2.append({"d_fm": d_fm, "d_tv": d_tv, "d_gk": d_gk})
    res2 = run_bass_kernel_spmd(nc2, in_maps2, core_ids=list(range(NC)))
    o_gla = np.empty((B_, SEQ, D), np.float32)
    for c in range(NC):
        bsel = c // 4
        heads = [4 * (c % 4) + u for u in range(4)]
        do = res2.results[c]["d_o"].reshape(NCH, 128, 4, 64)
        for u in range(4):
            o_gla[bsel, :, heads[u] * 64:(heads[u] + 1) * 64] = \
                do[:, :, u].reshape(SEQ, 64)
    if _upto == "A2":
        return o_gla

    nc3 = build_AB()
    ogs = _shard_rows(o_gla)
    in_maps3 = []
    for c in range(NC):
        in_maps3.append({
            "x": xs[c],
            "og": ogs[c],
            "gsil": np.ascontiguousarray(rA1[c]["o_g"]),
            "gamma": np.asarray(L0_attn_gamma, np.float32).reshape(1, D),
            "beta": np.asarray(L0_attn_beta, np.float32).reshape(1, D),
            "wout": np.asarray(L0_attn_Wout, np.float32),
            "phi": np.ascontiguousarray(np.asarray(L0_ffn_phi, np.float32).reshape(D, 2048)),
        })
    res3 = run_bass_kernel_spmd(nc3, in_maps3, core_ids=list(range(NC)))
    rAB = res3.results
    if _upto == "AB":
        return rAB

    nc4 = build_B2()
    W4 = np.asarray(L0_ffn_W, np.float32)
    bn4 = np.asarray(L0_ffn_b, np.float32)
    in_maps4 = []
    for c in range(NC):
        d_ps = np.stack([
            np.ascontiguousarray(rAB[s]["o_ps"][:, c * 256:(c + 1) * 256, :]
                                 .transpose(0, 2, 1))
            for s in range(NC)])
        in_maps4.append({
            "d_ps": d_ps,
            "d_wn": np.ascontiguousarray(W4[8 * c:8 * c + 8]),
            "d_bn": np.ascontiguousarray(bn4[8 * c:8 * c + 8]),
        })
    res4 = run_bass_kernel_spmd(nc4, in_maps4, core_ids=list(range(NC)))
    rB2 = res4.results
    if _upto == "B2":
        return rB2

    nc5 = build_BC()
    Yfull = np.concatenate([rB2[c]["o_y"].reshape(B_, 256, D) for c in range(NC)],
                           axis=1)
    in_maps5 = []
    for c in range(NC):
        in_maps5.append({
            "d_y": Yfull,
            "d_c": rAB[c]["o_C"],
            "d_x1": rAB[c]["o_x1"],
            "w_attn": np.asarray(L1_attn_w_attn, np.float32),
            "b_attn": np.asarray(L1_attn_b_attn, np.float32).reshape(1, 3 * D),
        })
    res5 = run_bass_kernel_spmd(nc5, in_maps5, core_ids=list(range(NC)))
    rBC = res5.results
    if _upto == "BC":
        return rBC

    qf2 = asm2 = None
    def asmBC(key):
        full = np.empty((B_, SEQ, D), BF)
        for c in range(NC):
            full[:, c * MLOC:(c + 1) * MLOC] = rBC[c][key].reshape(B_, MLOC, D)
        return full
    qf2 = asmBC("o_q")
    kf2 = asmBC("o_k")
    vf2 = asmBC("o_v")
    nc6 = build_C2()
    in_maps6 = []
    for c in range(NC):
        bsel = c // 4
        heads = [4 * (c % 4) + u for u in range(4)]
        d_qf = np.stack([np.ascontiguousarray(qf2[bsel, :, h * 64:(h + 1) * 64].T)
                         for h in heads])
        d_kf = np.stack([np.ascontiguousarray(kf2[bsel, :, h * 64:(h + 1) * 64].T)
                         for h in heads])
        d_v = np.stack([np.ascontiguousarray(vf2[bsel, :, h * 64:(h + 1) * 64])
                        for h in heads])
        in_maps6.append({"d_qf": d_qf, "d_kf": d_kf, "d_v": d_v})
    res6 = run_bass_kernel_spmd(nc6, in_maps6, core_ids=list(range(NC)))
    rC2 = res6.results
    if _upto == "C2":
        return rC2

    Yall = np.empty((B_, SEQ, D), np.float32)
    Sall = np.empty((B_, SEQ, 16), np.float32)
    for bb in range(B_):
        for h in range(16):
            qc = bb * 4 + h // 4
            u = h % 4
            ot = rC2[qc]["d_o"][u].transpose(1, 0, 2).reshape(65, SEQ)
            Yall[bb, :, h * 64:(h + 1) * 64] = ot[0:64].T
            Sall[bb, :, h] = ot[64]
    gw24 = np.concatenate([
        np.asarray(L1_ffn_gate_w, np.float32),
        np.ascontiguousarray(np.asarray(L1_ffn_shared_gate_w, np.float32).T
                             .reshape(D, 8, 256).transpose(1, 0, 2))], axis=0)
    uw24 = np.concatenate([
        np.asarray(L1_ffn_up_w, np.float32),
        np.ascontiguousarray(np.asarray(L1_ffn_shared_up_w, np.float32).T
                             .reshape(D, 8, 256).transpose(1, 0, 2))], axis=0)
    dw24 = np.concatenate([
        np.asarray(L1_ffn_down_w, np.float32),
        np.ascontiguousarray(np.asarray(L1_ffn_shared_down_w, np.float32).T
                             .reshape(8, 256, D))], axis=0)
    nc7 = build_CD()
    in_maps7 = []
    for c in range(NC):
        in_maps7.append({
            "d_yin": np.ascontiguousarray(
                Yall[:, c * MLOC:(c + 1) * MLOC].reshape(TLOC, D)),
            "d_sums": np.ascontiguousarray(
                Sall[:, c * MLOC:(c + 1) * MLOC].reshape(TLOC, 16)),
            "d_x2": rBC[c]["o_x2"],
            "w_proj": np.asarray(L1_attn_w_proj, np.float32),
            "b_proj": np.asarray(L1_attn_b_proj, np.float32).reshape(1, D),
            "rw_T": np.ascontiguousarray(np.asarray(L1_ffn_router_w, np.float32).T),
            "cbias": np.asarray(L1_ffn_correction_bias, np.float32).reshape(1, 16),
            "d_gw": gw24, "d_uw": uw24, "d_dw": dw24,
        })
    res7 = run_bass_kernel_spmd(nc7, in_maps7, core_ids=list(range(NC)))
    if _dbg == "CD":
        return res7.results
    out = np.empty((B_, SEQ, D), np.float32)
    for c in range(NC):
        out[:, c * MLOC:(c + 1) * MLOC] = res7.results[c]["o_x4"].reshape(B_, MLOC, D)
    return out


# version 03
# speedup vs baseline: 1.0340x; 1.0340x over previous
import numpy as np
import ml_dtypes
from contextlib import ExitStack

import concourse.bass as bass
import concourse.mybir as mybir
from concourse.bass_utils import run_bass_kernel_spmd

F32 = mybir.dt.float32
BF16 = mybir.dt.bfloat16
AF = mybir.ActivationFunctionType
ALU = mybir.AluOpType
AX = mybir.AxisListType
BF = ml_dtypes.bfloat16

NC = 8
B_ = 2
SEQ = 8192
D = 1024
MLOC = SEQ // NC
TLOC = B_ * MLOC
NT = TLOC // 128
EPS = 1e-5


ENGS = ("te", "ve", "se", "gp", "sy")


class Builder:
    K_DMA = 6

    def __init__(self, name):
        self.name = name
        self.nc = bass.Bass("TRN2", target_bir_lowering=False)
        self.ctx = ExitStack()
        self.counts = {e: 0 for e in ENGS}
        self.dma_i = {e: 0 for e in ENGS}
        self.marks = {e: {} for e in ENGS}
        self.records = {e: [] for e in ENGS}

    def sbuf(self, name, shape, dt):
        return self.ctx.enter_context(self.nc.sbuf_tensor(name, list(shape), dt))

    def psum(self, name, shape, dt=F32):
        h = self.ctx.enter_context(self.nc.psum_tensor(name, list(shape), dt))
        return PsumTile(h)

    def din(self, name, shape, dt):
        return self.nc.dram_tensor(name, list(shape), dt, kind="ExternalInput").ap()

    def dout(self, name, shape, dt):
        return self.nc.dram_tensor(name, list(shape), dt, kind="ExternalOutput").ap()

    def dtmp(self, name, shape, dt):
        return self.nc.dram_tensor(name, list(shape), dt).ap()

    def _want(self, eng, deps):
        want = {}
        for tok in deps:
            if tok is None:
                continue
            semkey, val = tok
            if want.get(semkey, 0) < val:
                want[semkey] = val
        waits = []
        for semkey, val in want.items():
            if self.marks[eng].get(semkey, 0) < val:
                waits.append((semkey, val))
                self.marks[eng][semkey] = val
        return waits

    def op(self, eng, fn, deps=(), dma=False):
        waits = self._want(eng, deps)
        semkey = ("e", eng)
        self.counts[eng] += 1
        self.records[eng].append((waits, fn, semkey, 1, False))
        return (semkey, self.counts[eng])

    def dma(self, eng, out, in_, deps=(), **kw):
        i = self.dma_i[eng]
        self.dma_i[eng] = i + 1
        j = i % self.K_DMA
        semkey = ("d", eng, j)
        slot_use = i // self.K_DMA
        deps = list(deps) + [(semkey, 16 * slot_use)]
        waits = self._want(eng, deps)
        fn = lambda e: e.dma_start(out=out, in_=in_, **kw)
        self.records[eng].append((waits, fn, semkey, 16, True))
        return (semkey, 16 * (slot_use + 1))

    def barrier_tokens(self):
        toks = [(("e", e), self.counts[e]) for e in ENGS if self.counts[e] > 0]
        for e in ENGS:
            n = self.dma_i[e]
            for j in range(self.K_DMA):
                full = n // self.K_DMA + (1 if j < n % self.K_DMA else 0)
                if full > 0:
                    toks.append((("d", e, j), 16 * full))
        return toks

    def finalize(self, final_deps):
        gw = self._want("gp", list(final_deps) + self.barrier_tokens())
        self.records["gp"].append((gw, None, None, 0, False))
        for e in ENGS:
            if e != "gp" and self.counts[e] > 0:
                w = self._want(e, [(("e", e), self.counts[e])])
                if w:
                    self.records[e].append((w, None, None, 0, False))
        nc = self.nc
        sems = {}
        for e in ENGS:
            sems[("e", e)] = self.ctx.enter_context(nc.semaphore(f"s_{e}"))
            for j in range(self.K_DMA):
                if any(r[2] == ("d", e, j) or any(w[0] == ("d", e, j) for w in r[0])
                       for recs in self.records.values() for r in recs):
                    sems[("d", e, j)] = self.ctx.enter_context(
                        nc.semaphore(f"d_{e}{j}"))
        block = self.ctx.enter_context(nc.Block())

        def ninstr():
            return sum(len(bb.instructions) for bb in nc.m.functions[0].blocks)

        def section(engname):
            def body(engobj):
                for waits, fn, semkey, inc, isdma in self.records[engname]:
                    for (sk, val) in waits:
                        engobj.wait_ge(sems[sk], val)
                    if fn is None:
                        continue
                    n0 = ninstr()
                    instr = fn(engobj)
                    if isdma:
                        nparts = ninstr() - n0
                        assert nparts == 1, (
                            f"{self.name}/{engname}: dma_start emitted "
                            f"{nparts} instructions; split it manually")
                    instr.then_inc(sems[semkey], inc)
            return body

        for engname, deco in (("te", block.tensor), ("ve", block.vector),
                              ("se", block.scalar), ("gp", block.gpsimd),
                              ("sy", block.sync)):
            if self.records[engname]:
                deco(section(engname))
        self.ctx.close()
        return nc


class PsumTile:
    def __init__(self, h):
        self.h = h
        self.wtoks = []
        self.rtoks = []

    def __getitem__(self, idx):
        return self.h[idx]

    def write_deps(self):
        return list(self.rtoks) + list(self.wtoks)

    def read_deps(self):
        return list(self.wtoks)

    def note_write(self, tok):
        if self.rtoks:
            self.rtoks = []
            self.wtoks = []
        self.wtoks.append(tok)

    def note_read(self, tok):
        self.rtoks.append(tok)


def mm(b, pt, out_ap, lhsT, rhs, deps=(), start=True, stop=True, first=False):
    d = list(deps)
    if first or start:
        d += pt.write_deps()
    tok = b.op("te", lambda e: e.matmul(out_ap, lhsT, rhs, start=start, stop=stop), deps=d)
    pt.note_write(tok)
    return tok


def pread(b, eng, pt, fn, deps=()):
    tok = b.op(eng, fn, deps=list(deps) + pt.read_deps())
    pt.note_read(tok)
    return tok



def emit_const_identity(b, name="ident"):
    ident = b.sbuf(name, [128, 128], BF16)
    t0 = b.op("gp", lambda e: e.memset(ident[:, :], 0.0))
    t1 = b.op("gp", lambda e: e.affine_select(
        out=ident[:, :], in_=ident[:, :], pattern=[[-1, 128]],
        compare_op=ALU.not_equal, fill=1.0, base=0, channel_multiplier=1),
        deps=[t0])
    return ident, t1


def emit_ln_tile(b, x_ap, xdeps, out_ap, scratch, stats, sc, scdep=None):
    s_sum = stats[:, sc:sc + 1]
    s_sq = stats[:, sc + 1:sc + 2]
    s_mean = stats[:, sc + 2:sc + 3]
    s_rstd = stats[:, sc + 3:sc + 4]
    t_sum = b.op("ve", lambda e: e.reduce_sum(s_sum, x_ap, axis=AX.X), deps=xdeps)
    t_sq = b.op("se", lambda e: e.activation(scratch, x_ap, AF.Square,
                accum_out=s_sq), deps=list(xdeps) + ([scdep] if scdep else []))
    t_mean = b.op("ve", lambda e: e.tensor_scalar(
        s_mean, s_sum, 1.0 / D, None, op0=ALU.mult, op1=ALU.bypass), deps=[t_sum])
    t_m2 = b.op("ve", lambda e: e.tensor_tensor(
        s_rstd, s_mean, s_mean, op=ALU.mult), deps=[t_mean])
    t_var = b.op("ve", lambda e: e.scalar_tensor_tensor(
        s_rstd, s_sq, 1.0 / D, s_rstd, op0=ALU.mult, op1=ALU.subtract),
        deps=[t_sq, t_m2])
    t_vare = b.op("ve", lambda e: e.tensor_scalar(
        s_rstd, s_rstd, EPS, None, op0=ALU.add, op1=ALU.bypass), deps=[t_var])
    t_ln = b.op("se", lambda e: e.activation(s_rstd, s_rstd, AF.Ln), deps=[t_vare])
    t_rstd = b.op("se", lambda e: e.activation(s_rstd, s_rstd, AF.Exp, scale=-0.5),
                  deps=[t_ln])
    t_out = b.op("ve", lambda e: e.tensor_scalar(
        out_ap, x_ap, s_mean, s_rstd, op0=ALU.subtract, op1=ALU.mult),
        deps=[t_rstd] + list(xdeps))
    return t_out, t_sq



def build_A1():
    b = Builder("A1")
    x_in = b.din("x", [TLOC, D], F32)
    wq = b.din("wq", [D, D], F32)
    wk = b.din("wk", [D, D], F32)
    wv = b.din("wv", [D, D], F32)
    wg = b.din("wg", [D, D], F32)
    wgk1 = b.din("wgk1", [D, 16], F32)
    wgk2 = b.din("wgk2", [16, D], F32)
    bgk2 = b.din("bgk2", [1, D], F32)

    o_q = b.dout("o_q", [TLOC, D], BF16)
    o_k = b.dout("o_k", [TLOC, D], BF16)
    o_v = b.dout("o_v", [TLOC, D], BF16)
    o_g = b.dout("o_g", [TLOC, D], BF16)
    o_gk = b.dout("o_gk", [TLOC, D], F32)
    o_xn = b.dout("o_xn", [TLOC, D], BF16)

    ident, tid = emit_const_identity(b)

    W = {}
    wtok = {}
    for nm, src in (("wq", wq), ("wk", wk), ("wv", wv), ("wg", wg)):
        W[nm] = b.sbuf(f"W_{nm}", [128, 8 * D], BF16)
        wtok[nm] = b.dma("gp", W[nm][:, :].rearrange("p (c d) -> p c d", c=8),
                         src.rearrange("(c p) d -> p c d", p=128))
    Wgk1 = b.sbuf("W_wgk1", [128, 8 * 16], BF16)
    t_wgk1 = b.dma("gp", Wgk1[:, :].rearrange("p (c r) -> p c r", c=8),
                   wgk1.rearrange("(c p) r -> p c r", p=128))
    Wgk2 = b.sbuf("W_wgk2", [16, D], BF16)
    t_wgk2 = b.dma("gp", Wgk2[:, :], wgk2[:, :])
    BG = b.sbuf("BG", [128, D], F32)
    t_bg = b.dma("gp", BG[:, :], bgk2[0:1, :].partition_broadcast(128))

    XR = b.sbuf("XR", [128, 2 * D], F32)
    XN = b.sbuf("XN", [128, NT * D], BF16)
    scratch = b.sbuf("scratch", [128, D], F32)
    stats = b.sbuf("stats", [128, 4 * NT], F32)
    x_r = x_in.rearrange("(t p) d -> p t d", p=128)
    ln_toks = []
    ring_dep = [None, None]
    scdep = None
    for t in range(NT):
        r = t % 2
        xa = XR[:, r * D:(r + 1) * D]
        t_x = b.dma("sy", xa, x_r[:, t, :],
                    deps=[ring_dep[r]] if ring_dep[r] else [])
        t_out, scdep = emit_ln_tile(b, xa, [t_x], XN[:, t * D:(t + 1) * D],
                                    scratch[:, :], stats, 4 * t, scdep=scdep)
        ring_dep[r] = t_out
        ln_toks.append(t_out)

    t_oxn = b.dma("sy", o_xn.rearrange("(t p) d -> p t d", p=128),
                  XN[:, :].rearrange("p (t d) -> p t d", t=NT), deps=ln_toks)

    XT = b.sbuf("XT", [128, 8 * TLOC], BF16)
    pT = [b.psum(f"pT{i}", [128, 128], BF16) for i in range(2)]
    xt_tok = {}
    xt_all = []
    for t in range(NT):
        for k in range(8):
            pt = pT[(t * 8 + k) % 2]
            ttok = b.op("te", lambda e, t=t, k=k, pt=pt: e.transpose(
                pt[:, :], XN[:, t * D + k * 128: t * D + (k + 1) * 128],
                ident[:, :]), deps=[ln_toks[t], tid] + pt.write_deps())
            pt.note_write(ttok)
            etok = pread(b, "ve", pt, lambda e, t=t, k=k, pt=pt: e.tensor_copy(
                XT[:, k * TLOC + t * 128: k * TLOC + (t + 1) * 128], pt[:, :]))
            xt_all.append(etok)
            xt_tok[(t, k)] = etok

    pj = [b.psum(f"pj{i}", [128, 512]) for i in range(4)]
    stage = b.sbuf("stage", [128, 2 * D], BF16)
    out_toks = []
    st_dep = [None, None]
    outs = {"wq": o_q, "wk": o_k, "wv": o_v, "wg": o_g}
    for t in range(NT):
        for pi, nm in enumerate(("wq", "wk", "wv", "wg")):
            sbase = (t * 4 + pi) % 2
            ev_toks = []
            for h in range(2):
                pt = pj[(t * 8 + pi * 2 + h) % 4]
                for k in range(8):
                    tok = mm(b, pt, pt[:, :],
                             XT[:, k * TLOC + t * 128: k * TLOC + (t + 1) * 128],
                             W[nm][:, k * D + h * 512: k * D + (h + 1) * 512],
                             deps=[xt_tok[(t, k)], wtok[nm]],
                             start=(k == 0), stop=(k == 7), first=(k == 0))
                dst = stage[:, sbase * D + h * 512: sbase * D + (h + 1) * 512]
                dd = [st_dep[sbase]] if st_dep[sbase] else []
                if nm == "wg":
                    ev = pread(b, "se", pt, lambda e, pt=pt, dst=dst: e.activation(
                        dst, pt[:, :], AF.Silu), deps=dd)
                else:
                    ev = pread(b, "ve", pt, lambda e, pt=pt, dst=dst: e.tensor_copy(
                        dst, pt[:, :]), deps=dd)
                ev_toks.append(ev)
            otok = b.dma("sy",
                         outs[nm].rearrange("(t p) d -> p t d", p=128)[:, t, :],
                         stage[:, sbase * D:(sbase + 1) * D], deps=ev_toks)
            st_dep[sbase] = otok
            out_toks.append(otok)

    pr1 = b.psum("pr1", [16, 512])
    R1 = b.sbuf("R1", [16, TLOC], BF16)
    r1_toks = []
    for s in range(TLOC // 512):
        for k in range(8):
            tok = mm(b, pr1, pr1[:, :],
                     Wgk1[:, k * 16:(k + 1) * 16],
                     XT[:, k * TLOC + s * 512: k * TLOC + (s + 1) * 512],
                     deps=xt_all + [t_wgk1], start=(k == 0), stop=(k == 7),
                     first=(k == 0))
        r1_toks.append(pread(b, "ve", pr1, lambda e, s=s: e.tensor_copy(
            R1[:, s * 512:(s + 1) * 512], pr1[:, :])))

    pz = [pj[0], pj[1]]
    ZA = b.sbuf("ZA", [128, 2 * D], F32)
    GK = b.sbuf("GK", [128, 2 * D], F32)
    gk_dep = [None, None]
    for t in range(NT):
        sbase = t % 2
        ev = []
        for h in range(2):
            pt = pz[h]
            tok = mm(b, pt, pt[:, :], R1[:, t * 128:(t + 1) * 128],
                     Wgk2[:, h * 512:(h + 1) * 512],
                     deps=r1_toks + [t_wgk2], start=True, stop=True, first=True)
            dd = [gk_dep[sbase]] if gk_dep[sbase] else []
            ev.append(pread(b, "ve", pt, lambda e, pt=pt, h=h, sbase=sbase: e.tensor_tensor(
                ZA[:, sbase * D + h * 512: sbase * D + (h + 1) * 512], pt[:, :],
                BG[:, h * 512:(h + 1) * 512], op=ALU.add), deps=dd + [t_bg]))
        zs = ZA[:, sbase * D:(sbase + 1) * D]
        gs = GK[:, sbase * D:(sbase + 1) * D]
        t1 = b.op("se", lambda e, zs=zs, gs=gs: e.activation(gs, zs, AF.Abs),
                  deps=ev)
        t2 = b.op("se", lambda e, gs=gs: e.activation(gs, gs, AF.Exp, scale=-1.0),
                  deps=[t1])
        t3 = b.op("se", lambda e, gs=gs: e.activation(gs, gs, AF.Ln, bias=1.0),
                  deps=[t2])
        t4 = b.op("se", lambda e, zs=zs: e.activation(zs, zs, AF.Relu, scale=-1.0),
                  deps=ev + [t1])
        t5 = b.op("ve", lambda e, zs=zs, gs=gs: e.tensor_tensor(
            gs, gs, zs, op=ALU.add), deps=[t3, t4])
        otok = b.dma("sy", o_gk.rearrange("(t p) d -> p t d", p=128)[:, t, :],
                     gs, deps=[t5])
        gk_dep[sbase] = otok
        out_toks.append(otok)

    nc = b.finalize(out_toks + [t_oxn])
    return nc



NCH = SEQ // 128


def build_A2():
    b = Builder("A2")
    d_fm = b.din("d_fm", [NCH, 64, 1024], BF16)
    d_tv = b.din("d_tv", [NCH, 128, 512], BF16)
    d_gk = b.din("d_gk", [NCH, 128, 256], F32)
    d_o = b.dout("d_o", [NCH, 128, 256], F32)

    ident, tid = emit_const_identity(b)

    TRI = b.sbuf("TRI", [128, 128], F32)
    t0 = b.op("gp", lambda e: e.memset(TRI[:, :], -1.0 / 16.0))
    t_tri = b.op("gp", lambda e: e.affine_select(
        out=TRI[:, :], in_=TRI[:, :], pattern=[[1, 128]], compare_op=ALU.is_ge,
        fill=0.0, base=0, channel_multiplier=-1), deps=[t0])
    ONESC = b.sbuf("ONESC", [128, 1], F32)
    t_ones = b.op("gp", lambda e: e.memset(ONESC[:, :], -1.0 / 16.0))
    U4 = b.sbuf("U4", [128, 512], BF16)
    t1 = b.op("gp", lambda e: e.memset(U4[:, :], 1.0))
    t_u4 = b.op("gp", lambda e: e.affine_select(
        out=U4[:, :], in_=U4[:, :], pattern=[[0, 4], [1, 128]],
        compare_op=ALU.is_ge, fill=0.0, base=0, channel_multiplier=-1),
        deps=[t1])

    FMR = b.sbuf("FMR", [64, 2 * 1024], BF16)
    TVR = b.sbuf("TVR", [128, 2 * 512], BF16)
    GKR = b.sbuf("GKR", [128, 2 * 256], F32)
    CUMP = b.sbuf("CUMP", [128, 256], F32)
    CUMN = b.sbuf("CUMN", [128, 256], F32)
    VT = b.sbuf("VT", [128, 256], BF16)
    BM = b.sbuf("BM", [128, 512], BF16)
    OS = b.sbuf("OS", [128, 2 * 256], F32)
    S = b.sbuf("S", [64, 256], F32)
    SB = b.sbuf("SB", [64, 256], BF16)
    STB = b.sbuf("STB", [64, 256], BF16)
    GEXP = b.sbuf("GEXP", [64, 4], F32)

    p_cs = b.psum("p_cs", [128, 256])
    p_sc = b.psum("p_sc", [128, 512])
    p_O = b.psum("p_O", [128, 256])
    p_VTK = b.psum("p_VTK", [64, 256])
    p_G = b.psum("p_G", [64, 4])
    p_ST = b.psum("p_ST", [64, 256], BF16)

    t_s0 = b.op("gp", lambda e: e.memset(S[:, :], 0.0))
    t_stb0 = b.op("gp", lambda e: e.memset(STB[:, :], 0.0))

    ring_dep = [None, None]
    gk_ring_dep = [None, None]
    os_dep = [None, None]
    s_prev = t_s0
    stb_prev = t_stb0
    out_toks = []
    war = {"CUMP": None, "CUMN": None, "VT": None, "BM": None, "GEXP": None,
           "SB": None}

    for c in range(NCH):
        r = c % 2
        fm = FMR[:, r * 1024:(r + 1) * 1024]
        tv = TVR[:, r * 512:(r + 1) * 512]
        gk = GKR[:, r * 256:(r + 1) * 256]
        t_fm = b.dma("sy", fm, d_fm[c, :, :],
                     deps=[ring_dep[r]] if ring_dep[r] else [])
        t_tv = b.dma("sy", tv, d_tv[c, :, :],
                     deps=[ring_dep[r]] if ring_dep[r] else [])
        t_bf = (t_tv[0], t_tv[1])
        t_gk = b.dma("gp", gk, d_gk[c, :, :],
                     deps=[gk_ring_dep[r]] if gk_ring_dep[r] else [])
        qfm = lambda u: fm[0:64, u * 128:(u + 1) * 128]
        kfm = lambda u: fm[0:64, 512 + u * 128: 512 + (u + 1) * 128]
        ktm = lambda u: tv[:, u * 64:(u + 1) * 64]
        vtm = tv[:, 256:512]

        t_cs = mm(b, p_cs, p_cs[:, :], TRI[:, :], gk, deps=[t_tri, t_gk],
                  start=True, stop=True, first=True)
        g_toks = []
        for u in range(4):
            g_toks.append(mm(b, p_G, p_G[:, u:u + 1],
                             gk[:, u * 64:(u + 1) * 64], ONESC[:, :],
                             deps=[t_gk, t_ones], start=True, stop=True,
                             first=(u == 0)))
        t_cp = pread(b, "se", p_cs, lambda e: e.activation(
            CUMP[:, :], p_cs[:, :], AF.Exp), deps=[war["CUMP"]])
        t_cn = pread(b, "se", p_cs, lambda e: e.activation(
            CUMN[:, :], p_cs[:, :], AF.Exp, scale=-1.0), deps=[war["CUMN"]])
        t_g = pread(b, "se", p_G, lambda e: e.activation(
            GEXP[:, :], p_G[:, :], AF.Exp), deps=[war["GEXP"]])
        t_vt = b.op("ve", lambda e, vtm=vtm: e.tensor_tensor(
            VT[:, :], vtm, CUMN[:, :], op=ALU.mult),
            deps=[t_tv, t_cn, war["VT"]])
        war["CUMN"] = t_vt
        sc_toks = []
        for u in range(4):
            sc_toks.append(mm(b, p_sc, p_sc[:, u * 128:(u + 1) * 128],
                              kfm(u), qfm(u), deps=[t_fm],
                              start=True, stop=True, first=(u == 0)))
        t_bm = pread(b, "ve", p_sc, lambda e: e.tensor_tensor(
            BM[:, :], p_sc[:, :], U4[:, :], op=ALU.mult),
            deps=[t_u4, war["BM"]])
        o_mm = []
        for u in range(4):
            o_mm.append(mm(b, p_O, p_O[:, u * 64:(u + 1) * 64],
                           BM[:, u * 128:(u + 1) * 128],
                           VT[:, u * 64:(u + 1) * 64],
                           deps=[t_bm, t_vt], start=True, stop=False,
                           first=(u == 0)))
            o_mm.append(mm(b, p_O, p_O[:, u * 64:(u + 1) * 64],
                           qfm(u), STB[0:64, u * 64:(u + 1) * 64],
                           deps=[t_fm, stb_prev], start=False, stop=True))
        stb_read_o = o_mm[-1]
        war["BM"] = o_mm[-1]
        osl = OS[:, (c % 2) * 256:(c % 2) * 256 + 256]
        t_oe = pread(b, "ve", p_O, lambda e, osl=osl: e.tensor_tensor(
            osl, p_O[:, :], CUMP[:, :], op=ALU.mult),
            deps=[t_cp] + ([os_dep[c % 2]] if os_dep[c % 2] else []))
        t_od = b.dma("sy", d_o[c, :, :], osl, deps=[t_oe])
        os_dep[c % 2] = t_od
        out_toks.append(t_od)
        war["CUMP"] = t_oe
        vtk_toks = []
        for u in range(4):
            vtk_toks.append(mm(b, p_VTK, p_VTK[:, u * 64:(u + 1) * 64],
                               VT[:, u * 64:(u + 1) * 64], ktm(u),
                               deps=[t_vt, t_tv], start=True, stop=True,
                               first=(u == 0)))
        war["VT"] = vtk_toks[-1]
        t_sa = pread(b, "ve", p_VTK, lambda e: e.tensor_tensor(
            S[:, :], S[:, :], p_VTK[:, :], op=ALU.add), deps=[s_prev])
        gb = GEXP[:, :].unsqueeze(2).broadcast_to([64, 4, 64])
        s3 = S[:, :].rearrange("p (u k) -> p u k", u=4)
        t_sm = b.op("ve", lambda e: e.tensor_tensor(s3, s3, gb, op=ALU.mult),
                    deps=[t_sa, t_g])
        s_prev = t_sm
        war["GEXP"] = t_sm
        t_sb = b.op("se", lambda e: e.copy(SB[:, :], S[:, :]),
                    deps=[t_sm, war["SB"]])
        tr_toks = []
        for u in range(4):
            tr_toks.append(b.op("te", lambda e, u=u: e.transpose(
                p_ST[:, u * 64:(u + 1) * 64], SB[:, u * 64:(u + 1) * 64],
                ident[0:64, 0:64]),
                deps=[t_sb, tid] + (p_ST.write_deps() if u == 0 else [])))
            p_ST.note_write(tr_toks[-1])
        t_stb = pread(b, "ve", p_ST, lambda e: e.tensor_copy(
            STB[:, :], p_ST[:, :]), deps=[stb_read_o])
        stb_prev = t_stb
        war["SB"] = tr_toks[-1]
        ring_dep[r] = vtk_toks[-1]
        gk_ring_dep[r] = g_toks[-1] if g_toks[-1][1] > t_cs[1] else t_cs

    nc = b.finalize(out_toks)
    return nc



def emit_ln_res(b, a_ap, adeps, x_ap, xdeps, out_ap, tmp, scratch, stats, sc,
                scdep=None):
    t_ln, t_sq = emit_ln_tile(b, a_ap, adeps, tmp, scratch, stats, sc,
                              scdep=scdep)
    t_out = b.op("ve", lambda e: e.tensor_tensor(out_ap, x_ap, tmp, op=ALU.add),
                 deps=[t_ln] + list(xdeps))
    return t_out, t_sq


def build_AB():
    b = Builder("AB")
    x_in = b.din("x", [TLOC, D], F32)
    og_in = b.din("og", [TLOC, D], F32)
    gs_in = b.din("gsil", [TLOC, D], BF16)
    gamma = b.din("gamma", [1, D], F32)
    beta = b.din("beta", [1, D], F32)
    wout = b.din("wout", [D, D], F32)
    phi = b.din("phi", [D, 2048], F32)
    o_x1 = b.dout("o_x1", [TLOC, D], F32)
    o_C = b.dout("o_C", [TLOC, 2048], BF16)
    o_ps = b.dout("o_ps", [B_, 2048, D], F32)
    d_D = b.dtmp("d_D", [TLOC, 2048], BF16)

    ident, tid = emit_const_identity(b)
    GT = b.sbuf("GT", [128, D], F32)
    BT = b.sbuf("BT", [128, D], F32)
    t_gt = b.dma("gp", GT[:, :], gamma[0:1, :].partition_broadcast(128))
    t_bt = b.dma("gp", BT[:, :], beta[0:1, :].partition_broadcast(128))
    WBUF = b.sbuf("WBUF", [128, 8 * 2048], BF16)
    t_wo = b.dma("gp", WBUF[:, 0:8 * D].rearrange("p (c d) -> p c d", c=8),
                 wout.rearrange("(c p) d -> p c d", p=128))

    OG = b.sbuf("OG", [128, 2 * D], F32)
    
    GS = b.sbuf("GS", [128, 2 * D], BF16)
    XR = b.sbuf("XR", [128, 2 * D], F32)
    ON = b.sbuf("ON", [128, D], F32)
    PB = b.sbuf("PB", [128, D], BF16)
    PTS = b.sbuf("PTS", [128, 8 * 128], BF16)
    AT = b.sbuf("AT", [128, D], F32)
    TMP = b.sbuf("TMP", [128, D], F32)
    X1R = b.sbuf("X1R", [128, 2 * D], F32)
    XN2 = b.sbuf("XN2", [128, NT * D], BF16)
    XT2 = b.sbuf("XT2", [128, 8 * TLOC], BF16)
    scratch = b.sbuf("scratch", [128, D], F32)
    stats = b.sbuf("stats", [128, 64], F32)

    LG0 = b.psum("LG0", [128, 2048])
    pTs = [b.psum("pT0", [128, 128], BF16), b.psum("pT1", [128, 128], BF16)]

    ring = {"OG": [None, None], "GS": [None, None], "XR": [None, None],
            "X1R": [None, None]}
    war = {"ON": None, "PB": None, "PTS": None, "AT": None, "TMP": None}
    scdep = None
    out_toks = []
    xt2_tok = {}
    xn2_toks = []
    x_r = x_in.rearrange("(t p) d -> p t d", p=128)
    og_r = og_in.rearrange("(t p) d -> p t d", p=128)
    gs_r = gs_in.rearrange("(t p) d -> p t d", p=128)
    x1_r = o_x1.rearrange("(t p) d -> p t d", p=128)

    for t in range(NT):
        r = t % 2
        oga = OG[:, r * D:(r + 1) * D]
        gsa = GS[:, r * D:(r + 1) * D]
        xra = XR[:, r * D:(r + 1) * D]
        t_og = b.dma("sy", oga, og_r[:, t, :], deps=[ring["OG"][r]])
        t_gs = b.dma("sy", gsa, gs_r[:, t, :], deps=[ring["GS"][r]])
        t_xr = b.dma("sy", xra, x_r[:, t, :], deps=[ring["XR"][r]])
        t_on, scdep = emit_ln_tile(b, oga, [t_og, war["ON"]], ON[:, :],
                                   scratch[:, :], stats, 0, scdep=scdep)
        t_g1 = b.op("ve", lambda e: e.tensor_tensor(
            ON[:, :], ON[:, :], GT[:, :], op=ALU.mult), deps=[t_on, t_gt])
        t_g2 = b.op("ve", lambda e: e.tensor_tensor(
            ON[:, :], ON[:, :], BT[:, :], op=ALU.add), deps=[t_g1, t_bt])
        t_pb = b.op("ve", lambda e, gsa=gsa: e.tensor_tensor(
            PB[:, :], ON[:, :], gsa, op=ALU.mult),
            deps=[t_g2, t_gs, war["PB"]])
        ring["OG"][r] = t_on
        ring["GS"][r] = t_pb
        war["ON"] = t_pb
        tr_evs = []
        for k in range(8):
            pt = pTs[k % 2]
            ttok = b.op("te", lambda e, k=k, pt=pt: e.transpose(
                pt[:, :], PB[:, k * 128:(k + 1) * 128], ident[:, :]),
                deps=[t_pb, tid] + pt.write_deps())
            pt.note_write(ttok)
            tr_evs.append(pread(b, "ve", pt, lambda e, k=k, pt=pt: e.tensor_copy(
                PTS[:, k * 128:(k + 1) * 128], pt[:, :]), deps=[war["PTS"]]))
        war["PB"] = tr_evs[-1]
        wo_ev = []
        for h in range(2):
            for k in range(8):
                mm(b, LG0, LG0[:, h * 512:(h + 1) * 512],
                   PTS[:, k * 128:(k + 1) * 128],
                   WBUF[:, k * D + h * 512: k * D + (h + 1) * 512],
                   deps=[tr_evs[k], t_wo], start=(k == 0), stop=(k == 7),
                   first=(k == 0))
        t_at = pread(b, "ve", LG0, lambda e: e.tensor_copy(
            AT[:, :], LG0[:, 0:D]), deps=[war["AT"]])
        war["PTS"] = t_at
        x1a = X1R[:, r * D:(r + 1) * D]
        t_x1, scdep = emit_ln_res(b, AT[:, :], [t_at], xra, [t_xr],
                                  x1a, TMP[:, :], scratch[:, :], stats, 8,
                                  scdep=scdep)
        war["AT"] = t_x1
        war["TMP"] = t_x1
        ring["XR"][r] = t_x1
        t_x1d = b.dma("sy", x1_r[:, t, :], x1a, deps=[t_x1])
        out_toks.append(t_x1d)
        t_xn2, scdep = emit_ln_tile(b, x1a, [t_x1], XN2[:, t * D:(t + 1) * D],
                                    scratch[:, :], stats, 16, scdep=scdep)
        xn2_toks.append(t_xn2)
        ring["X1R"][r] = t_xn2
        for k in range(8):
            pt = pTs[k % 2]
            ttok = b.op("te", lambda e, t=t, k=k, pt=pt: e.transpose(
                pt[:, :], XN2[:, t * D + k * 128: t * D + (k + 1) * 128],
                ident[:, :]), deps=[t_xn2, tid] + pt.write_deps())
            pt.note_write(ttok)
            xt2_tok[(t, k)] = pread(b, "ve", pt,
                lambda e, t=t, k=k, pt=pt: e.tensor_copy(
                    XT2[:, k * TLOC + t * 128: k * TLOC + (t + 1) * 128], pt[:, :]))
        ring["X1R"][r] = xt2_tok[(t, 7)]

    phi_dep = [tok for tok in out_toks]
    t_phi = b.dma("gp", WBUF[:, :].rearrange("p (c n) -> p c n", c=8),
                  phi.rearrange("(c p) n -> p c n", p=128),
                  deps=b.barrier_tokens())
    DS = b.sbuf("DS", [128, 2 * 2048], BF16)
    LA = b.sbuf("LA", [128, 2048], F32)
    DIF = b.sbuf("DIF", [128, 2048], F32)
    CC = b.sbuf("CC", [128, 2 * 2048], BF16)
    c_r = o_C.rearrange("(t p) n -> p t n", p=128)
    d_r = d_D.rearrange("(t p) n -> p t n", p=128)
    d_toks = []
    war2 = {"LA": None, "DIF": None, "CC0": None, "CC1": None,
            "DS0": None, "DS1": None}

    def logits_tile(t, deps):
        mms = []
        for h in range(4):
            for k in range(8):
                mms.append(mm(b, LG0, LG0[:, h * 512:(h + 1) * 512],
                              XT2[:, k * TLOC + t * 128: k * TLOC + (t + 1) * 128],
                              WBUF[:, k * 2048 + h * 512: k * 2048 + (h + 1) * 512],
                              deps=[xt2_tok[(t, k)], t_phi] + deps,
                              start=(k == 0), stop=(k == 7),
                              first=(k == 0)))
        return mms

    for i in range(NT // 2):
        m0 = logits_tile(i, [])
        t_la = pread(b, "ve", LG0, lambda e: e.tensor_copy(LA[:, :], LG0[:, :]),
                     deps=[war2["LA"]])
        m1 = logits_tile(i + 8, [t_la])
        t_dif = pread(b, "ve", LG0, lambda e: e.tensor_tensor(
            DIF[:, :], LA[:, :], LG0[:, :], op=ALU.subtract),
            deps=[t_la, war2["DIF"]])
        t_d0 = b.op("se", lambda e: e.activation(
            DS[:, 0:2048], DIF[:, :], AF.Sigmoid), deps=[t_dif, war2["DS0"]])
        t_d1 = b.op("se", lambda e: e.activation(
            DS[:, 2048:4096], DIF[:, :], AF.Sigmoid,
            scale=-1.0), deps=[t_dif, war2["DS1"]])
        t_dd0 = b.dma("sy", d_r[:, i, :], DS[:, 0:2048], deps=[t_d0])
        t_dd1 = b.dma("sy", d_r[:, i + 8, :], DS[:, 2048:4096], deps=[t_d1])
        war2["DS0"] = t_dd0
        war2["DS1"] = t_dd1
        d_toks += [t_dd0, t_dd1]
        war2["DIF"] = t_d1
        for (tt, srcap, is_psum, wkey) in ((i, LA[:, :], False, "CC0"),
                                           (i + 8, LG0[:, :], True, "CC1")):
            sb0 = 24 if not is_psum else 28
            cc = CC[:, (0 if not is_psum else 2048):(2048 if not is_psum else 4096)]
            mx = stats[:, sb0:sb0 + 1]
            sm = stats[:, sb0 + 1:sb0 + 2]
            rc = stats[:, sb0 + 2:sb0 + 3]
            if is_psum:
                t_mx0 = pread(b, "ve", LG0, lambda e, srcap=srcap, mx=mx: e.reduce_max(
                    mx, srcap, axis=AX.X), deps=[t_dif, war2[wkey]])
            else:
                t_mx0 = b.op("ve", lambda e, srcap=srcap, mx=mx: e.reduce_max(
                    mx, srcap, axis=AX.X), deps=[t_la, t_dif, war2[wkey]])
            t_mx = b.op("ve", lambda e, mx=mx: e.tensor_scalar(
                mx, mx, -1.0, None, op0=ALU.mult, op1=ALU.bypass), deps=[t_mx0])
            if is_psum:
                t_ex = pread(b, "se", LG0, lambda e, srcap=srcap, cc=cc, mx=mx, sm=sm: e.activation(
                    cc, srcap, AF.Exp, bias=mx, accum_out=sm),
                    deps=[t_mx, war2[wkey]])
            else:
                t_ex = b.op("se", lambda e, srcap=srcap, cc=cc, mx=mx, sm=sm: e.activation(
                    cc, srcap, AF.Exp, bias=mx, accum_out=sm),
                    deps=[t_mx, war2[wkey]])
            t_rc = b.op("ve", lambda e, sm=sm, rc=rc: e.reciprocal(
                rc, sm), deps=[t_ex])
            t_cn = b.op("ve", lambda e, cc=cc, rc=rc: e.tensor_scalar(
                cc, cc, rc, None, op0=ALU.mult, op1=ALU.bypass),
                deps=[t_rc, t_ex])
            t_cd = b.dma("sy", c_r[:, tt, :], cc, deps=[t_cn])
            war2[wkey] = t_cd
            out_toks.append(t_cd)
        war2["LA"] = t_dif

    PS = b.sbuf("PS", [128, 512], F32)
    DR = b.sbuf("DR", [128, 2 * 8 * 128], BF16)
    ps_dep = None
    dr_dep = [None, None]
    for bb in range(B_):
        for npc in range(16):
            r = npc % 2
            dra = DR[:, r * 1024:(r + 1) * 1024]
            t_dr = b.dma("sy",
                         dra.rearrange("p (j n) -> p j n", j=8),
                         d_r[:, bb * 8:(bb + 1) * 8,
                             npc * 128:(npc + 1) * 128],
                         deps=[dr_dep[r]] + d_toks)
            last = None
            for dsl in range(2):
                for j in range(8):
                    t = bb * 8 + j
                    last = mm(
                        b, LG0, LG0[:, 0:512],
                        dra[:, j * 128:(j + 1) * 128],
                        XN2[:, t * D + dsl * 512: t * D + (dsl + 1) * 512],
                        deps=[xn2_toks[t], t_dr], start=(j == 0),
                        stop=(j == 7), first=(j == 0))
                t_ev = pread(b, "ve", LG0, lambda e: e.tensor_copy(
                    PS[:, :], LG0[:, 0:512]), deps=[ps_dep])
                t_pd = b.dma("sy", o_ps[bb, npc * 128:(npc + 1) * 128,
                                        dsl * 512:(dsl + 1) * 512], PS[:, :],
                             deps=[t_ev])
                ps_dep = t_pd
                out_toks.append(t_pd)
            dr_dep[r] = last

    nc = b.finalize(out_toks)
    return nc



def build_B2():
    b = Builder("B2")
    d_ps = b.din("d_ps", [NC, B_, D, 256], F32)
    d_wn = b.din("d_wn", [8, D, D], F32)
    d_bn = b.din("d_bn", [8, D], F32)
    o_y = b.dout("o_y", [B_, 8, 32, D], F32)

    SLT = b.sbuf("SLT", [128, B_ * 8 * 256], F32)
    SLB = b.sbuf("SLB", [128, B_ * 8 * 256], BF16)
    PSR = b.sbuf("PSR", [128, 2 * 256], F32)
    WN = b.sbuf("WN", [128, 2 * 8 * D], BF16)
    BN = b.sbuf("BN", [32, D], F32)
    YS = b.sbuf("YS", [32, 2 * D], F32)
    pY = b.psum("pY", [32, 1024])

    sum_toks = {}
    ps_ring = [None, None]
    for bb in range(B_):
        for dc in range(8):
            dst = SLT[:, (bb * 8 + dc) * 256:(bb * 8 + dc + 1) * 256]
            acc_tok = None
            for s in range(NC):
                r = s % 2
                pra = PSR[:, r * 256:(r + 1) * 256]
                t_l = b.dma("sy", pra,
                            d_ps[s, bb, dc * 128:(dc + 1) * 128, :],
                            deps=[ps_ring[r]])
                if s == 0:
                    acc_tok = b.op("ve", lambda e, pra=pra, dst=dst: e.tensor_copy(
                        dst, pra), deps=[t_l, acc_tok])
                else:
                    acc_tok = b.op("ve", lambda e, pra=pra, dst=dst: e.tensor_tensor(
                        dst, dst, pra, op=ALU.add), deps=[t_l, acc_tok])
                ps_ring[r] = acc_tok
            sum_toks[(bb, dc)] = b.op("ve", lambda e, dst=dst, bb=bb, dc=dc: e.tensor_copy(
                SLB[:, (bb * 8 + dc) * 256:(bb * 8 + dc + 1) * 256], dst),
                deps=[acc_tok])

    out_toks = []
    wn_ring = [None, None]
    ys_ring = [None, None]
    bn_dep = None
    for j in range(8):
        r = j % 2
        wna = WN[:, r * 8 * D:(r + 1) * 8 * D]
        t_w = b.dma("gp", wna.rearrange("p (c d) -> p c d", c=8),
                    d_wn[j].rearrange("(c p) d -> p c d", p=128),
                    deps=[wn_ring[r]])
        t_b = b.dma("sy", BN[:, :], d_bn[j:j + 1, :].partition_broadcast(32),
                    deps=[wn_ring[r], bn_dep])
        for bb in range(B_):
            yr = bb % 2
            ysa = YS[:, yr * D:(yr + 1) * D]
            last = None
            for h in range(2):
                for dc in range(8):
                    last = mm(b, pY, pY[:, h * 512:(h + 1) * 512],
                              SLB[:, (bb * 8 + dc) * 256 + 32 * j:
                                   (bb * 8 + dc) * 256 + 32 * j + 32],
                              wna[:, dc * D + h * 512: dc * D + (h + 1) * 512],
                              deps=[sum_toks[(bb, dc)], t_w],
                              start=(dc == 0), stop=(dc == 7), first=(dc == 0))
            t_ev = pread(b, "ve", pY, lambda e, ysa=ysa: e.tensor_tensor(
                ysa, pY[0:32, :], BN[:, :], op=ALU.add),
                deps=[t_b, ys_ring[yr]])
            bn_dep = t_ev
            t_yd = b.dma("sy", o_y[bb, j, :, :], ysa, deps=[t_ev])
            ys_ring[yr] = t_yd
            out_toks.append(t_yd)
        wn_ring[r] = last

    nc = b.finalize(out_toks)
    return nc



def build_BC():
    b = Builder("BC")
    d_y = b.din("d_y", [B_, 2048, D], F32)
    d_c = b.din("d_c", [TLOC, 2048], BF16)
    d_x1 = b.din("d_x1", [TLOC, D], F32)
    w_attn = b.din("w_attn", [D, 3 * D], F32)
    b_attn = b.din("b_attn", [1, 3 * D], F32)
    o_x2 = b.dout("o_x2", [TLOC, D], F32)
    o_q = b.dout("o_q", [TLOC, D], BF16)
    o_k = b.dout("o_k", [TLOC, D], BF16)
    o_v = b.dout("o_v", [TLOC, D], BF16)

    ident, tid = emit_const_identity(b)
    YB = b.sbuf("YB", [128, B_ * 16 * D], BF16)
    GEN8 = b.sbuf("GEN8", [128, 16 * TLOC], BF16)
    XN5 = b.sbuf("XN5", [128, NT * D], BF16)
    X1R = b.sbuf("X1R", [128, 2 * D], F32)
    FF = b.sbuf("FF", [128, D], F32)
    TMP = b.sbuf("TMP", [128, D], F32)
    X2R = b.sbuf("X2R", [128, 2 * D], F32)
    BA = b.sbuf("BA", [128, 3 * D], F32)
    QS = b.sbuf("QS", [128, 3 * D], BF16)
    scratch = b.sbuf("scratch", [128, D], F32)
    stats = b.sbuf("stats", [128, 32], F32)

    pO = b.psum("pO", [128, 1024])
    pQ = b.psum("pQ", [128, 1024])
    pTs = [b.psum("pT0", [128, 128], BF16), b.psum("pT1", [128, 128], BF16)]

    t_y = []
    for bb in range(B_):
        t_y.append(b.dma("gp",
                         YB[:, bb * 16 * D:(bb + 1) * 16 * D].rearrange(
                             "p (n d) -> p n d", n=16),
                         d_y[bb].rearrange("(n p) d -> p n d", p=128)))
    ct_toks = []
    for npc in range(16):
        ct_toks.append(b.dma("sy", GEN8[:, npc * TLOC:(npc + 1) * TLOC],
                             d_c[:, npc * 128:(npc + 1) * 128], transpose=True))
    t_ba = b.dma("gp", BA[:, :], b_attn[0:1, :].partition_broadcast(128))
    t_baq = b.op("ve", lambda e: e.tensor_scalar(
        BA[:, 0:D], BA[:, 0:D], 0.125, None, op0=ALU.mult, op1=ALU.bypass),
        deps=[t_ba])

    x1_r = d_x1.rearrange("(t p) d -> p t d", p=128)
    x2_r = o_x2.rearrange("(t p) d -> p t d", p=128)
    ring = [None, None]
    x2ring = [None, None]
    war = {"FF": None, "TMP": None}
    scdep = None
    xn5_toks = []
    out_toks = []
    for t in range(NT):
        bb = t // 8
        r = t % 2
        x1a = X1R[:, r * D:(r + 1) * D]
        t_x1 = b.dma("sy", x1a, x1_r[:, t, :], deps=[ring[r]])
        for h in range(2):
            for npc in range(16):
                mm(b, pO, pO[:, h * 512:(h + 1) * 512],
                   GEN8[:, npc * TLOC + t * 128: npc * TLOC + (t + 1) * 128],
                   YB[:, (bb * 16 + npc) * D + h * 512:
                      (bb * 16 + npc) * D + (h + 1) * 512],
                   deps=[ct_toks[npc], t_y[bb]],
                   start=(npc == 0), stop=(npc == 15), first=(npc == 0))
        t_ff = pread(b, "ve", pO, lambda e: e.tensor_copy(FF[:, :], pO[:, :]),
                     deps=[war["FF"]])
        x2a = X2R[:, r * D:(r + 1) * D]
        t_x2, scdep = emit_ln_res(b, FF[:, :], [t_ff], x1a, [t_x1], x2a,
                                  TMP[:, :], scratch[:, :], stats, 0,
                                  scdep=scdep)
        war["FF"] = t_x2
        ring[r] = t_x2
        out_toks.append(b.dma("sy", x2_r[:, t, :], x2a, deps=[t_x2]))
        t_xn5, scdep = emit_ln_tile(b, x2a, [t_x2], XN5[:, t * D:(t + 1) * D],
                                    scratch[:, :], stats, 8, scdep=scdep)
        xn5_toks.append(t_xn5)
        x2ring[r] = t_xn5
        ring[r] = t_xn5

    WA = YB
    t_wa = b.dma("gp", WA[:, 0:8 * 3 * D].rearrange("p (c d) -> p c d", c=8),
                 w_attn.rearrange("(c p) d -> p c d", p=128),
                 deps=b.barrier_tokens())
    xt_tok = {}
    outs = {0: o_q, 1: o_k, 2: o_v}
    qs_dep = [None]
    for t in range(NT):
        for k in range(8):
            pt = pTs[k % 2]
            ttok = b.op("te", lambda e, t=t, k=k, pt=pt: e.transpose(
                pt[:, :], XN5[:, t * D + k * 128: t * D + (k + 1) * 128],
                ident[:, :]),
                deps=[xn5_toks[t], tid, t_wa] + pt.write_deps())
            pt.note_write(ttok)
            xt_tok[(t, k)] = pread(b, "ve", pt,
                lambda e, t=t, k=k, pt=pt: e.tensor_copy(
                    GEN8[:, k * TLOC + t * 128: k * TLOC + (t + 1) * 128],
                    pt[:, :]))
        ev_toks = []
        for cs in range(6):
            ph = pQ if cs % 2 == 0 else pO
            for k in range(8):
                mm(b, ph, ph[:, 0:512],
                   GEN8[:, k * TLOC + t * 128: k * TLOC + (t + 1) * 128],
                   WA[:, k * 3 * D + cs * 512: k * 3 * D + (cs + 1) * 512],
                   deps=[xt_tok[(t, k)], t_wa], start=(k == 0), stop=(k == 7),
                   first=(k == 0))
            ev_toks.append(pread(b, "ve", ph,
                lambda e, cs=cs, ph=ph: e.scalar_tensor_tensor(
                    QS[:, cs * 512:(cs + 1) * 512], ph[:, 0:512],
                    0.125 if cs < 2 else 1.0, BA[:, cs * 512:(cs + 1) * 512],
                    op0=ALU.mult, op1=ALU.add),
                deps=[t_baq, qs_dep[0]]))
        for pi in range(3):
            tok = b.dma("sy", outs[pi].rearrange("(t p) d -> p t d", p=128)[:, t, :],
                        QS[:, pi * D:(pi + 1) * D],
                        deps=[ev_toks[2 * pi], ev_toks[2 * pi + 1]])
            out_toks.append(tok)
        qs_dep[0] = out_toks[-1]


    nc = b.finalize(out_toks)
    return nc



def build_C2(nu=4, nqg=16):
    b = Builder("C2")
    d_qf = b.din("d_qf", [nu, 64, SEQ], BF16)
    d_kf = b.din("d_kf", [nu, 64, SEQ], BF16)
    d_v = b.din("d_v", [nu, SEQ, 64], BF16)
    d_o = b.dout("d_o", [nu, nqg, 65, 512], F32)

    QF = b.sbuf("QF", [64, 4 * SEQ], BF16)
    KF = b.sbuf("KF", [64, 4 * SEQ], BF16)
    V65 = b.sbuf("V65", [128, 4 * 64 * 65], BF16)
    PB = b.sbuf("PB", [128, 2 * 1024], BF16)
    PMD = b.sbuf("PMD", [128, 2 * 512], BF16)
    DM = b.sbuf("DM", [128, 4 * 512], BF16)
    OS = b.sbuf("OS", [65, 2 * 512], F32)
    B15 = b.sbuf("B15", [128, 1], F32)

    p_scs = [b.psum("p_sc0", [128, 1024]), b.psum("p_sc1", [128, 1024])]
    pOs = [b.psum("pO0", [65, 512]), b.psum("pO1", [65, 512])]

    ld_toks = []
    for u in range(nu):
        ld_toks.append(b.dma("sy", QF[:, u * SEQ:(u + 1) * SEQ], d_qf[u]))
        ld_toks.append(b.dma("sy", KF[:, u * SEQ:(u + 1) * SEQ], d_kf[u]))
    t_b15 = b.op("gp", lambda e: e.memset(B15[:, :], -15.0))
    t_vm = b.op("gp", lambda e: e.memset(V65[:, :], 1.0))
    v_toks = {}
    for u in range(nu):
        v_toks[u] = b.dma("gp",
                          V65[:, u * 4160:(u + 1) * 4160].rearrange(
                              "p (ch w) -> p ch w", w=65)[:, :, 0:64],
                          d_v[u].rearrange("(ch p) v -> p ch v", p=128),
                          deps=[t_vm])
    dm_toks = []
    for dd in range(4):
        t0 = b.op("gp", lambda e, dd=dd: e.memset(DM[:, dd * 512:(dd + 1) * 512], 1.0))
        dm_toks.append(b.op("gp", lambda e, dd=dd: e.affine_select(
            out=DM[:, dd * 512:(dd + 1) * 512],
            in_=DM[:, dd * 512:(dd + 1) * 512],
            pattern=[[1, 512]], compare_op=ALU.is_ge, fill=0.0,
            base=-128 * dd, channel_multiplier=-1), deps=[t0]))

    out_toks = []
    war_pb = [None, None]
    war_pmd = [None, None]
    os_dep = [None, None]
    steps = []
    for u in range(nu):
        for qg in range(nqg):
            for kc2 in range((4 * qg + 4) // 2):
                steps.append((u, qg, kc2))

    exp_toks = [None] * len(steps)

    def emit_scores(i):
        u, qg, kc2 = steps[i]
        pp = p_scs[i % 2]
        qsl = QF[0:64, u * SEQ + qg * 512: u * SEQ + (qg + 1) * 512]
        for h2 in range(2):
            kc = 2 * kc2 + h2
            mm(b, pp, pp[:, h2 * 512:(h2 + 1) * 512],
               KF[0:64, u * SEQ + kc * 128: u * SEQ + (kc + 1) * 128],
               qsl, deps=[ld_toks[2 * u], ld_toks[2 * u + 1]],
               start=True, stop=True, first=(h2 == 0))
        pba = PB[:, (i % 2) * 1024:(i % 2) * 1024 + 1024]
        exp_toks[i] = pread(b, "se", pp, lambda e, pba=pba: e.activation(
            pba, pp[:, :], AF.Exp, bias=B15[:, :]),
            deps=[war_pb[i % 2], t_b15])

    def emit_av(i):
        u, qg, kc2 = steps[i]
        nk = 4 * qg + 4
        pO = pOs[qg % 2]
        pba = PB[:, (i % 2) * 1024:(i % 2) * 1024 + 1024]
        t_av = None
        for h2 in range(2):
            kc = 2 * kc2 + h2
            dd = kc - (nk - 4)
            psrc = pba[:, h2 * 512:(h2 + 1) * 512]
            extra = []
            if dd >= 0:
                pm = PMD[:, (dd % 2) * 512:(dd % 2) * 512 + 512]
                t_msk = b.op("ve", lambda e, psrc=psrc, dd=dd, pm=pm: e.tensor_tensor(
                    pm, psrc, DM[:, dd * 512:(dd + 1) * 512], op=ALU.mult),
                    deps=[exp_toks[i], dm_toks[dd], war_pmd[dd % 2]])
                psrc = pm
                extra = [t_msk]
            t_av = mm(b, pO, pO[:, :],
                      V65[:, u * 4160 + kc * 65: u * 4160 + (kc + 1) * 65],
                      psrc, deps=[exp_toks[i], v_toks[u]] + extra,
                      start=(kc == 0), stop=(kc == nk - 1),
                      first=(kc == 0))
            if dd >= 0:
                war_pmd[dd % 2] = t_av
        war_pb[i % 2] = t_av
        if kc2 == nk // 2 - 1:
            osl = OS[:, (qg % 2) * 512:(qg % 2) * 512 + 512]
            t_oe = pread(b, "ve", pO, lambda e, osl=osl, pO=pO: e.tensor_copy(
                osl, pO[:, :]), deps=[os_dep[qg % 2]])
            t_od = b.dma("sy", d_o[u, qg], osl, deps=[t_oe])
            os_dep[qg % 2] = t_od
            out_toks.append(t_od)

    for i in range(len(steps) + 1):
        if i < len(steps):
            emit_scores(i)
        if i > 0:
            emit_av(i - 1)

    nc = b.finalize(out_toks)
    return nc



def build_CD():
    b = Builder("CD")
    d_yin = b.din("d_yin", [TLOC, D], F32)
    d_sums = b.din("d_sums", [TLOC, 16], F32)
    d_x2 = b.din("d_x2", [TLOC, D], F32)
    w_proj = b.din("w_proj", [D, D], F32)
    b_proj = b.din("b_proj", [1, D], F32)
    rw_T = b.din("rw_T", [D, 16], F32)
    cbias = b.din("cbias", [1, 16], F32)
    d_gw = b.din("d_gw", [12, D, 512], F32)
    d_uw = b.din("d_uw", [12, D, 512], F32)
    d_dw = b.din("d_dw", [24, 256, D], F32)
    o_x4 = b.dout("o_x4", [TLOC, D], F32)
    d_x3 = b.dtmp("d_x3", [TLOC, D], F32)

    ident, tid = emit_const_identity(b)
    XT4 = b.sbuf("XT4", [128, 8 * TLOC], BF16)
    HF = b.sbuf("HF", [128, 24 * 1024], BF16)
    WP = b.sbuf("WP", [128, 8 * D], BF16)
    RWT = b.sbuf("RWT", [128, 8 * 16], BF16)
    GW = b.sbuf("GW", [128, 2 * 4096], BF16)
    UW = b.sbuf("UW", [128, 2 * 4096], BF16)
    DWB = b.sbuf("DWB", [128, 2 * 1024], BF16)
    BP = b.sbuf("BP", [128, D], F32)
    CBC = b.sbuf("CBC", [128, 16], F32)
    YR = b.sbuf("YR", [128, 2 * D], F32)
    SR = b.sbuf("SR", [128, 2 * 16], F32)
    X2R = b.sbuf("X2R", [128, 2 * D], F32)
    X3R = b.sbuf("X3R", [128, 2 * D], F32)
    YN = b.sbuf("YN", [128, D], BF16)
    YT = b.sbuf("YT", [128, D], BF16)
    AT = b.sbuf("AT", [128, D], F32)
    TMP = b.sbuf("TMP", [128, D], F32)
    XN4 = b.sbuf("XN4", [128, D], BF16)
    GS = b.sbuf("GS_", [128, 512], BF16)
    HT = b.sbuf("HT", [128, 512], BF16)
    FF4 = b.sbuf("FF4", [128, 4 * D], F32)
    CB = b.sbuf("CB", [128, NT * 24], F32)
    SCB = b.sbuf("SCB", [128, 16], F32)
    SB2 = b.sbuf("SB2", [128, 16], F32)
    SB2o = b.sbuf("SB2o", [128, 16], F32)
    GRP = b.sbuf("GRP", [128, 4 * 8], F32)
    EM = b.sbuf("EM", [128, 16], F32)
    X4R = b.sbuf("X4R", [128, 2 * D], F32)
    scratch = b.sbuf("scratch", [128, D], F32)
    stats = b.sbuf("stats", [128, 48], F32)

    T0 = b.psum("T0", [128, 1024])
    T1 = b.psum("T1", [128, 1024])
    pTs = [b.psum("pT0", [128, 128], BF16), b.psum("pT1", [128, 128], BF16)]
    pR = b.psum("pR", [128, 512])

    t_wp = b.dma("gp", WP[:, :].rearrange("p (c d) -> p c d", c=8),
                 w_proj.rearrange("(c p) d -> p c d", p=128))
    t_bp = b.dma("gp", BP[:, :], b_proj[0:1, :].partition_broadcast(128))
    t_rw = b.dma("gp", RWT[:, :].rearrange("p (c r) -> p c r", c=8),
                 rw_T.rearrange("(c p) r -> p c r", p=128))
    t_cbc = b.dma("gp", CBC[:, :], cbias[0:1, :].partition_broadcast(128))
    t_cb1 = b.op("gp", lambda e: e.memset(CB[:, :], 1.0))

    y_r = d_yin.rearrange("(t p) d -> p t d", p=128)
    s_r = d_sums.rearrange("(t p) r -> p t r", p=128)
    x2_r = d_x2.rearrange("(t p) d -> p t d", p=128)
    x3_r = d_x3.rearrange("(t p) d -> p t d", p=128)
    x4_r = o_x4.rearrange("(t p) d -> p t d", p=128)

    ring = {k: [None, None] for k in ("YR", "SR", "X2R", "X3R", "X4R")}
    war = {k: None for k in ("YN", "YT", "AT", "TMP", "XN4", "SCB", "SB2",
                             "SB2o", "GRP", "EM", "rs")}
    scdep = None
    out_toks = []
    x3_toks = []
    cb_toks = {}
    BIG = 16.0

    for t in range(NT):
        r = t % 2
        ya = YR[:, r * D:(r + 1) * D]
        sa = SR[:, r * 16:(r + 1) * 16]
        x2a = X2R[:, r * D:(r + 1) * D]
        t_y = b.dma("sy", ya, y_r[:, t, :], deps=[ring["YR"][r]])
        t_s = b.dma("sy", sa, s_r[:, t, :], deps=[ring["SR"][r]])
        t_x2 = b.dma("sy", x2a, x2_r[:, t, :], deps=[ring["X2R"][r]])
        rs = stats[:, 32:48]
        t_rs = b.op("ve", lambda e, sa=sa, rs=rs: e.reciprocal(rs, sa),
                    deps=[t_s, war["rs"]])
        t_yn = b.op("ve", lambda e, ya=ya, rs=rs: e.tensor_tensor(
            YN[:, :].rearrange("p (h w) -> p h w", h=16),
            ya.rearrange("p (h w) -> p h w", h=16),
            rs.unsqueeze(2).broadcast_to([128, 16, 64]), op=ALU.mult),
            deps=[t_rs, t_y, war["YN"]])
        ring["YR"][r] = t_yn
        war["rs"] = t_yn
        yt_toks = []
        for k in range(8):
            pt = pTs[k % 2]
            ttok = b.op("te", lambda e, k=k, pt=pt: e.transpose(
                pt[:, :], YN[:, k * 128:(k + 1) * 128], ident[:, :]),
                deps=[t_yn, tid] + pt.write_deps())
            pt.note_write(ttok)
            yt_toks.append(pread(b, "ve", pt, lambda e, k=k, pt=pt: e.tensor_copy(
                YT[:, k * 128:(k + 1) * 128], pt[:, :]), deps=[war["YT"]]))
        war["YN"] = yt_toks[-1]
        for h in range(2):
            for k in range(8):
                mm(b, T0, T0[:, h * 512:(h + 1) * 512],
                   YT[:, k * 128:(k + 1) * 128],
                   WP[:, k * D + h * 512: k * D + (h + 1) * 512],
                   deps=[yt_toks[k], t_wp], start=(k == 0), stop=(k == 7),
                   first=(k == 0))
        t_at = pread(b, "ve", T0, lambda e: e.tensor_tensor(
            AT[:, :], T0[:, :], BP[:, :], op=ALU.add), deps=[t_bp, war["AT"]])
        war["YT"] = t_at
        x3a = X3R[:, r * D:(r + 1) * D]
        t_x3, scdep = emit_ln_res(b, AT[:, :], [t_at], x2a, [t_x2], x3a,
                                  TMP[:, :], scratch[:, :], stats, 0,
                                  scdep=scdep)
        war["AT"] = t_x3
        ring["X2R"][r] = t_x3
        t_x3d = b.dma("sy", x3_r[:, t, :], x3a, deps=[t_x3])
        x3_toks.append(t_x3d)
        t_xn4, scdep = emit_ln_tile(b, x3a, [t_x3, war["XN4"]], XN4[:, :],
                                    scratch[:, :], stats, 8, scdep=scdep)
        ring["X3R"][r] = t_xn4
        xt_toks = []
        for k in range(8):
            pt = pTs[k % 2]
            ttok = b.op("te", lambda e, t=t, k=k, pt=pt: e.transpose(
                pt[:, :], XN4[:, k * 128:(k + 1) * 128], ident[:, :]),
                deps=[t_xn4, tid] + pt.write_deps())
            pt.note_write(ttok)
            xt_toks.append(pread(b, "ve", pt,
                lambda e, t=t, k=k, pt=pt: e.tensor_copy(
                    XT4[:, k * TLOC + t * 128: k * TLOC + (t + 1) * 128],
                    pt[:, :])))
        war["XN4"] = xt_toks[-1]
        for k in range(8):
            mm(b, pR, pR[:, 0:16],
               XT4[:, k * TLOC + t * 128: k * TLOC + (t + 1) * 128],
               RWT[:, k * 16:(k + 1) * 16],
               deps=[xt_toks[k], t_rw], start=(k == 0), stop=(k == 7),
               first=(k == 0))
        t_sc = pread(b, "se", pR, lambda e: e.activation(
            SCB[:, :], pR[:, 0:16], AF.Sigmoid), deps=[war["SCB"]])
        t_sb = b.op("ve", lambda e: e.tensor_tensor(
            SB2[:, :], SCB[:, :], CBC[:, :], op=ALU.add),
            deps=[t_sc, t_cbc, war["SB2"]])
        s3 = SB2[:, :].rearrange("p (g m) -> p g m", g=4)
        pair_toks = []
        pairs = [(0, 1), (0, 2), (0, 3), (1, 2), (1, 3), (2, 3)]
        for pi2, (i1, i2) in enumerate(pairs):
            pair_toks.append(b.op("ve", lambda e, s3=s3, pi2=pi2, i1=i1, i2=i2:
                e.tensor_tensor(GRP[:, pi2 * 4:(pi2 + 1) * 4],
                                s3[:, :, i1], s3[:, :, i2], op=ALU.add),
                deps=[t_sb, war["GRP"]]))
        t_m = b.op("ve", lambda e: e.tensor_tensor(
            GRP[:, 24:28], GRP[:, 0:4], GRP[:, 4:8], op=ALU.max),
            deps=pair_toks)
        for pi2 in range(2, 6):
            t_m = b.op("ve", lambda e, pi2=pi2: e.tensor_tensor(
                GRP[:, 24:28], GRP[:, 24:28], GRP[:, pi2 * 4:(pi2 + 1) * 4],
                op=ALU.max), deps=[t_m])
        t_m1 = b.op("ve", lambda e: e.reduce_max(
            stats[:, 16:17], GRP[:, 24:28], axis=AX.X), deps=[t_m])
        t_im = b.op("ve", lambda e: e.tensor_scalar(
            GRP[:, 28:32], GRP[:, 24:28], stats[:, 16:17], None,
            op0=ALU.is_ge, op1=ALU.bypass), deps=[t_m1])
        t_gm = b.op("ve", lambda e: e.scalar_tensor_tensor(
            GRP[:, 0:4], GRP[:, 28:32], -BIG, GRP[:, 24:28],
            op0=ALU.mult, op1=ALU.add), deps=[t_im])
        t_m2 = b.op("ve", lambda e: e.reduce_max(
            stats[:, 17:18], GRP[:, 0:4], axis=AX.X), deps=[t_gm])
        t_gmask = b.op("ve", lambda e: e.tensor_scalar(
            GRP[:, 28:32], GRP[:, 24:28], stats[:, 17:18], None,
            op0=ALU.is_ge, op1=ALU.bypass), deps=[t_m2, t_im])
        t_em = b.op("ve", lambda e: e.tensor_copy(
            EM[:, :].rearrange("p (g m) -> p g m", g=4),
            GRP[:, 28:32].unsqueeze(2).broadcast_to([128, 4, 4])),
            deps=[t_gmask, war["EM"]])
        t_mk = b.op("ve", lambda e: e.scalar_tensor_tensor(
            SB2o[:, :], EM[:, :], BIG, SB2[:, :], op0=ALU.mult, op1=ALU.add),
            deps=[t_em, war["SB2o"]])
        t_mk2 = b.op("ve", lambda e: e.tensor_scalar(
            SB2[:, :], SB2o[:, :], BIG, None, op0=ALU.subtract, op1=ALU.bypass),
            deps=[t_mk])
        t_cp = b.op("ve", lambda e: e.tensor_copy(SB2o[:, :], SB2[:, :]),
                    deps=[t_mk2])
        war["SB2"] = t_cp
        t_prev = t_cp
        for it in range(3):
            t_mx = b.op("ve", lambda e: e.reduce_max(
                stats[:, 18:19], SB2[:, :], axis=AX.X), deps=[t_prev])
            t_is = b.op("ve", lambda e: e.tensor_scalar(
                EM[:, :], SB2[:, :], stats[:, 18:19], None,
                op0=ALU.is_ge, op1=ALU.bypass), deps=[t_mx])
            t_prev = b.op("ve", lambda e: e.scalar_tensor_tensor(
                SB2[:, :], EM[:, :], -BIG, SB2[:, :], op0=ALU.mult, op1=ALU.add),
                deps=[t_is])
        t_tau = b.op("ve", lambda e: e.reduce_max(
            stats[:, 18:19], SB2[:, :], axis=AX.X), deps=[t_prev])
        war["EM"] = t_tau
        t_sel = b.op("ve", lambda e: e.tensor_scalar(
            EM[:, :], SB2o[:, :], stats[:, 18:19], None,
            op0=ALU.is_ge, op1=ALU.bypass), deps=[t_tau, t_cp])
        war["SB2o"] = t_sel
        t_w = b.op("ve", lambda e: e.tensor_tensor(
            SB2[:, :], SCB[:, :], EM[:, :], op=ALU.mult), deps=[t_sel, t_sc])
        war["SCB"] = t_w
        t_ws = b.op("ve", lambda e: e.reduce_sum(
            stats[:, 19:20], SB2[:, :], axis=AX.X), deps=[t_w])
        t_wse = b.op("ve", lambda e: e.tensor_scalar(
            stats[:, 19:20], stats[:, 19:20], 1e-20, None,
            op0=ALU.add, op1=ALU.bypass), deps=[t_ws])
        t_rc = b.op("ve", lambda e: e.reciprocal(
            stats[:, 20:21], stats[:, 19:20]), deps=[t_wse])
        t_cb = b.op("ve", lambda e, t=t: e.tensor_scalar(
            CB[:, t * 24:t * 24 + 16], SB2[:, :], stats[:, 20:21], None,
            op0=ALU.mult, op1=ALU.bypass), deps=[t_rc, t_w, t_cb1])
        cb_toks[t] = t_cb
        war["GRP"] = t_gmask

    gw_ring = [None, None]
    uw_ring = [None, None]
    dw_ring = [None, None]
    hf_guard = None
    gs_war = None
    ht_war = None
    ff_war = {}
    for tg in range(4):
        hf_toks = []
        for pe in range(12):
            rr = pe % 2
            gwa = GW[:, rr * 4096:(rr + 1) * 4096]
            uwa = UW[:, rr * 4096:(rr + 1) * 4096]
            t_gw = b.dma("gp", gwa.rearrange("p (c h) -> p c h", c=8),
                         d_gw[pe].rearrange("(c p) h -> p c h", p=128),
                         deps=[gw_ring[rr]])
            t_uw = b.dma("gp", uwa.rearrange("p (c h) -> p c h", c=8),
                         d_uw[pe].rearrange("(c p) h -> p c h", p=128),
                         deps=[uw_ring[rr]])
            last_g = last_u = None
            for ti in range(4):
                t_abs = tg * 4 + ti
                T = T0 if ti % 2 == 0 else T1
                for k in range(8):
                    last_g = mm(b, T, T[:, 0:512],
                                XT4[:, k * TLOC + t_abs * 128:
                                    k * TLOC + (t_abs + 1) * 128],
                                gwa[:, k * 512:(k + 1) * 512],
                                deps=[t_gw], start=(k == 0), stop=(k == 7),
                                first=(k == 0))
                for k in range(8):
                    last_u = mm(b, T, T[:, 512:1024],
                                XT4[:, k * TLOC + t_abs * 128:
                                    k * TLOC + (t_abs + 1) * 128],
                                uwa[:, k * 512:(k + 1) * 512],
                                deps=[t_uw], start=(k == 0), stop=(k == 7),
                                first=(k == 0))
                t_si = pread(b, "se", T, lambda e, T=T: e.activation(
                    GS[:, :], T[:, 0:512], AF.Silu), deps=[gs_war])
                t_hs = []
                for eip in range(2):
                    e2 = 2 * pe + eip
                    t_hs.append(pread(b, "ve", T,
                        lambda e, T=T, t_abs=t_abs, e2=e2, eip=eip:
                        e.scalar_tensor_tensor(
                            HT[:, eip * 256:(eip + 1) * 256],
                            T[:, 512 + eip * 256: 512 + (eip + 1) * 256],
                            CB[:, t_abs * 24 + e2: t_abs * 24 + e2 + 1],
                            GS[:, eip * 256:(eip + 1) * 256],
                            op0=ALU.mult, op1=ALU.mult),
                        deps=[t_si, cb_toks[t_abs], ht_war]))
                gs_war = t_hs[-1]
                tr_toks = []
                for q4 in range(4):
                    eip, hc = q4 // 2, q4 % 2
                    e2 = 2 * pe + eip
                    pt = pTs[q4 % 2]
                    ttok = b.op("te", lambda e, q4=q4, pt=pt: e.transpose(
                        pt[:, :], HT[:, q4 * 128:(q4 + 1) * 128], ident[:, :]),
                        deps=[t_hs[eip], tid] + pt.write_deps())
                    pt.note_write(ttok)
                    tr_toks.append(pread(b, "ve", pt,
                        lambda e, e2=e2, hc=hc, ti=ti, pt=pt: e.tensor_copy(
                            HF[:, e2 * 1024 + hc * 512 + ti * 128:
                               e2 * 1024 + hc * 512 + (ti + 1) * 128],
                            pt[:, :]), deps=[hf_guard]))
                ht_war = tr_toks[-1]
                hf_toks += tr_toks
            gw_ring[rr] = last_g
            uw_ring[rr] = last_u
        dw_last = None
        for dsl in range(2):
            ev_toks = []
            for e2 in range(24):
                rr = e2 % 2
                dwa = DWB[:, rr * 1024:(rr + 1) * 1024]
                t_dw = b.dma("gp", dwa.rearrange("p (hc d) -> p hc d", hc=2),
                             d_dw[e2].rearrange("(hc p) d -> p hc d", p=128)
                             [:, :, dsl * 512:(dsl + 1) * 512],
                             deps=[dw_ring[rr]])
                last_dw = None
                for ti in range(4):
                    T = T0 if ti < 2 else T1
                    col = (ti % 2) * 512
                    for hc in range(2):
                        last_dw = mm(
                            b, T, T[:, col:col + 512],
                            HF[:, e2 * 1024 + hc * 512 + ti * 128:
                               e2 * 1024 + hc * 512 + (ti + 1) * 128],
                            dwa[:, hc * 512:(hc + 1) * 512],
                            deps=[t_dw] + hf_toks,
                            start=(e2 == 0 and hc == 0),
                            stop=(e2 == 23 and hc == 1),
                            first=(e2 == 0 and hc == 0))
                dw_ring[rr] = last_dw
            for ti in range(4):
                T = T0 if ti < 2 else T1
                col = (ti % 2) * 512
                ev_toks.append(pread(b, "ve", T,
                    lambda e, ti=ti, T=T, col=col, tg=tg, dsl=dsl: e.tensor_copy(
                        FF4[:, ti * D + dsl * 512: ti * D + (dsl + 1) * 512],
                        T[:, col:col + 512]),
                    deps=[ff_war.get((tg - 1, ti))]))
                ff_war[(tg, ti, dsl)] = ev_toks[-1]
        for ti in range(4):
            t_abs = tg * 4 + ti
            r = t_abs % 2
            x3a = X3R[:, r * D:(r + 1) * D]
            t_x3l = b.dma("sy", x3a, x3_r[:, t_abs, :],
                          deps=[ring["X3R"][r], x3_toks[t_abs]])
            x4a = X4R[:, r * D:(r + 1) * D]
            t_x4, scdep = emit_ln_res(
                b, FF4[:, ti * D:(ti + 1) * D],
                [ff_war[(tg, ti, 0)], ff_war[(tg, ti, 1)]],
                x3a, [t_x3l], x4a, TMP[:, :], scratch[:, :], stats, 24,
                scdep=scdep)
            war["TMP"] = t_x4
            ring["X3R"][r] = t_x4
            t_od = b.dma("sy", x4_r[:, t_abs, :], x4a, deps=[t_x4, ring["X4R"][r]])
            ring["X4R"][r] = t_od
            out_toks.append(t_od)
            ff_war[(tg, ti)] = t_x4
        hf_guard = max(dw_ring[0], dw_ring[1], key=lambda tk: tk[1])

    nc = b.finalize(out_toks)
    return nc



def _shard_rows(a):
    out = []
    for c in range(NC):
        sl = a[:, c * MLOC:(c + 1) * MLOC]
        out.append(np.ascontiguousarray(sl.reshape((TLOC,) + a.shape[2:])))
    return out


def kernel(L0_attn_x, L0_attn_Wq, L0_attn_Wk, L0_attn_Wv, L0_attn_Wg,
           L0_attn_Wgk1, L0_attn_Wgk2, L0_attn_bgk2, L0_attn_gamma,
           L0_attn_beta, L0_attn_Wout, L0_ffn_phi, L0_ffn_W, L0_ffn_b,
           L1_attn_w_attn, L1_attn_b_attn, L1_attn_w_proj, L1_attn_b_proj,
           L1_ffn_router_w, L1_ffn_correction_bias, L1_ffn_gate_w,
           L1_ffn_up_w, L1_ffn_down_w, L1_ffn_shared_gate_w,
           L1_ffn_shared_up_w, L1_ffn_shared_down_w, _upto=None, _dbg=None):
    x = np.asarray(L0_attn_x, dtype=np.float32)
    xs = _shard_rows(x)

    nc = build_A1()
    in_maps = []
    for c in range(NC):
        in_maps.append({
            "x": xs[c],
            "wq": np.asarray(L0_attn_Wq, np.float32),
            "wk": np.asarray(L0_attn_Wk, np.float32),
            "wv": np.asarray(L0_attn_Wv, np.float32),
            "wg": np.asarray(L0_attn_Wg, np.float32),
            "wgk1": np.asarray(L0_attn_Wgk1, np.float32),
            "wgk2": np.asarray(L0_attn_Wgk2, np.float32),
            "bgk2": np.asarray(L0_attn_bgk2, np.float32).reshape(1, D),
        })
    res = run_bass_kernel_spmd(nc, in_maps, core_ids=list(range(NC)))
    if _upto == "A1":
        return [r for r in res.results]
    rA1 = res.results

    def asm(key, dt):
        full = np.empty((B_, SEQ, D), dt)
        for c in range(NC):
            full[:, c * MLOC:(c + 1) * MLOC] = rA1[c][key].reshape(B_, MLOC, D)
        return full
    qf = asm("o_q", BF)
    kf = asm("o_k", BF)
    vf = asm("o_v", BF)
    gkf = asm("o_gk", np.float32)

    nc2 = build_A2()
    in_maps2 = []
    for c in range(NC):
        bsel = c // 4
        heads = [4 * (c % 4) + u for u in range(4)]
        qr = qf[bsel].reshape(NCH, 128, 16, 64)[:, :, heads]
        kr = kf[bsel].reshape(NCH, 128, 16, 64)[:, :, heads]
        vr = vf[bsel].reshape(NCH, 128, 16, 64)[:, :, heads]
        gr = gkf[bsel].reshape(NCH, 128, 16, 64)[:, :, heads]
        d_fm = np.zeros((NCH, 64, 1024), BF)
        for u in range(4):
            d_fm[:, :, u * 128:(u + 1) * 128] = qr[:, :, u].transpose(0, 2, 1)
            d_fm[:, :, 512 + u * 128:512 + (u + 1) * 128] = kr[:, :, u].transpose(0, 2, 1)
        d_tv = np.zeros((NCH, 128, 512), BF)
        d_tv[:, :, 0:256] = kr.reshape(NCH, 128, 256)
        d_tv[:, :, 256:512] = vr.reshape(NCH, 128, 256)
        d_gk = np.ascontiguousarray(gr.reshape(NCH, 128, 256))
        in_maps2.append({"d_fm": d_fm, "d_tv": d_tv, "d_gk": d_gk})
    res2 = run_bass_kernel_spmd(nc2, in_maps2, core_ids=list(range(NC)))
    o_gla = np.empty((B_, SEQ, D), np.float32)
    for c in range(NC):
        bsel = c // 4
        heads = [4 * (c % 4) + u for u in range(4)]
        do = res2.results[c]["d_o"].reshape(NCH, 128, 4, 64)
        for u in range(4):
            o_gla[bsel, :, heads[u] * 64:(heads[u] + 1) * 64] = \
                do[:, :, u].reshape(SEQ, 64)
    if _upto == "A2":
        return o_gla

    nc3 = build_AB()
    ogs = _shard_rows(o_gla)
    in_maps3 = []
    for c in range(NC):
        in_maps3.append({
            "x": xs[c],
            "og": ogs[c],
            "gsil": np.ascontiguousarray(rA1[c]["o_g"]),
            "gamma": np.asarray(L0_attn_gamma, np.float32).reshape(1, D),
            "beta": np.asarray(L0_attn_beta, np.float32).reshape(1, D),
            "wout": np.asarray(L0_attn_Wout, np.float32),
            "phi": np.ascontiguousarray(np.asarray(L0_ffn_phi, np.float32).reshape(D, 2048)),
        })
    res3 = run_bass_kernel_spmd(nc3, in_maps3, core_ids=list(range(NC)))
    rAB = res3.results
    if _upto == "AB":
        return rAB

    nc4 = build_B2()
    W4 = np.asarray(L0_ffn_W, np.float32)
    bn4 = np.asarray(L0_ffn_b, np.float32)
    in_maps4 = []
    for c in range(NC):
        d_ps = np.stack([
            np.ascontiguousarray(rAB[s]["o_ps"][:, c * 256:(c + 1) * 256, :]
                                 .transpose(0, 2, 1))
            for s in range(NC)])
        in_maps4.append({
            "d_ps": d_ps,
            "d_wn": np.ascontiguousarray(W4[8 * c:8 * c + 8]),
            "d_bn": np.ascontiguousarray(bn4[8 * c:8 * c + 8]),
        })
    res4 = run_bass_kernel_spmd(nc4, in_maps4, core_ids=list(range(NC)))
    rB2 = res4.results
    if _upto == "B2":
        return rB2

    nc5 = build_BC()
    Yfull = np.concatenate([rB2[c]["o_y"].reshape(B_, 256, D) for c in range(NC)],
                           axis=1)
    in_maps5 = []
    for c in range(NC):
        in_maps5.append({
            "d_y": Yfull,
            "d_c": rAB[c]["o_C"],
            "d_x1": rAB[c]["o_x1"],
            "w_attn": np.asarray(L1_attn_w_attn, np.float32),
            "b_attn": np.asarray(L1_attn_b_attn, np.float32).reshape(1, 3 * D),
        })
    res5 = run_bass_kernel_spmd(nc5, in_maps5, core_ids=list(range(NC)))
    rBC = res5.results
    if _upto == "BC":
        return rBC

    qf2 = asm2 = None
    def asmBC(key):
        full = np.empty((B_, SEQ, D), BF)
        for c in range(NC):
            full[:, c * MLOC:(c + 1) * MLOC] = rBC[c][key].reshape(B_, MLOC, D)
        return full
    qf2 = asmBC("o_q")
    kf2 = asmBC("o_k")
    vf2 = asmBC("o_v")
    nc6 = build_C2()
    in_maps6 = []
    for c in range(NC):
        bsel = c // 4
        heads = [4 * (c % 4) + u for u in range(4)]
        d_qf = np.stack([np.ascontiguousarray(qf2[bsel, :, h * 64:(h + 1) * 64].T)
                         for h in heads])
        d_kf = np.stack([np.ascontiguousarray(kf2[bsel, :, h * 64:(h + 1) * 64].T)
                         for h in heads])
        d_v = np.stack([np.ascontiguousarray(vf2[bsel, :, h * 64:(h + 1) * 64])
                        for h in heads])
        in_maps6.append({"d_qf": d_qf, "d_kf": d_kf, "d_v": d_v})
    res6 = run_bass_kernel_spmd(nc6, in_maps6, core_ids=list(range(NC)))
    rC2 = res6.results
    if _upto == "C2":
        return rC2

    Yall = np.empty((B_, SEQ, D), np.float32)
    Sall = np.empty((B_, SEQ, 16), np.float32)
    for bb in range(B_):
        for h in range(16):
            qc = bb * 4 + h // 4
            u = h % 4
            ot = rC2[qc]["d_o"][u].transpose(1, 0, 2).reshape(65, SEQ)
            Yall[bb, :, h * 64:(h + 1) * 64] = ot[0:64].T
            Sall[bb, :, h] = ot[64]
    gw24 = np.concatenate([
        np.asarray(L1_ffn_gate_w, np.float32),
        np.ascontiguousarray(np.asarray(L1_ffn_shared_gate_w, np.float32).T
                             .reshape(D, 8, 256).transpose(1, 0, 2))], axis=0)
    uw24 = np.concatenate([
        np.asarray(L1_ffn_up_w, np.float32),
        np.ascontiguousarray(np.asarray(L1_ffn_shared_up_w, np.float32).T
                             .reshape(D, 8, 256).transpose(1, 0, 2))], axis=0)
    dw24 = np.concatenate([
        np.asarray(L1_ffn_down_w, np.float32),
        np.ascontiguousarray(np.asarray(L1_ffn_shared_down_w, np.float32).T
                             .reshape(8, 256, D))], axis=0)
    gwp = np.ascontiguousarray(gw24.reshape(12, 2, D, 256).transpose(0, 2, 1, 3)
                               .reshape(12, D, 512))
    uwp = np.ascontiguousarray(uw24.reshape(12, 2, D, 256).transpose(0, 2, 1, 3)
                               .reshape(12, D, 512))
    nc7 = build_CD()
    in_maps7 = []
    for c in range(NC):
        in_maps7.append({
            "d_yin": np.ascontiguousarray(
                Yall[:, c * MLOC:(c + 1) * MLOC].reshape(TLOC, D)),
            "d_sums": np.ascontiguousarray(
                Sall[:, c * MLOC:(c + 1) * MLOC].reshape(TLOC, 16)),
            "d_x2": rBC[c]["o_x2"],
            "w_proj": np.asarray(L1_attn_w_proj, np.float32),
            "b_proj": np.asarray(L1_attn_b_proj, np.float32).reshape(1, D),
            "rw_T": np.ascontiguousarray(np.asarray(L1_ffn_router_w, np.float32).T),
            "cbias": np.asarray(L1_ffn_correction_bias, np.float32).reshape(1, 16),
            "d_gw": gwp, "d_uw": uwp, "d_dw": dw24,
        })
    res7 = run_bass_kernel_spmd(nc7, in_maps7, core_ids=list(range(NC)))
    if _dbg == "CD":
        return res7.results
    out = np.empty((B_, SEQ, D), np.float32)
    for c in range(NC):
        out[:, c * MLOC:(c + 1) * MLOC] = res7.results[c]["o_x4"].reshape(B_, MLOC, D)
    return out


# version 05 (final)
# speedup vs baseline: 1.2470x; 1.2061x over previous
import numpy as np
import ml_dtypes
from contextlib import ExitStack

import concourse.bass as bass
import concourse.mybir as mybir
from concourse.bass_utils import run_bass_kernel_spmd

F32 = mybir.dt.float32
BF16 = mybir.dt.bfloat16
AF = mybir.ActivationFunctionType
ALU = mybir.AluOpType
AX = mybir.AxisListType
BF = ml_dtypes.bfloat16

NC = 8
B_ = 2
SEQ = 8192
D = 1024
MLOC = SEQ // NC
TLOC = B_ * MLOC
NT = TLOC // 128
EPS = 1e-5


ENGS = ("te", "ve", "se", "gp", "sy")


class Builder:
    K_DMA = 6

    def __init__(self, name):
        self.name = name
        self.nc = bass.Bass("TRN2", target_bir_lowering=False)
        self.ctx = ExitStack()
        self.counts = {e: 0 for e in ENGS}
        self.dma_i = {e: 0 for e in ENGS}
        self.marks = {e: {} for e in ENGS}
        self.records = {e: [] for e in ENGS}

    def sbuf(self, name, shape, dt):
        return self.ctx.enter_context(self.nc.sbuf_tensor(name, list(shape), dt))

    def psum(self, name, shape, dt=F32):
        h = self.ctx.enter_context(self.nc.psum_tensor(name, list(shape), dt))
        return PsumTile(h)

    def din(self, name, shape, dt):
        return self.nc.dram_tensor(name, list(shape), dt, kind="ExternalInput").ap()

    def dout(self, name, shape, dt):
        return self.nc.dram_tensor(name, list(shape), dt, kind="ExternalOutput").ap()

    def dtmp(self, name, shape, dt):
        return self.nc.dram_tensor(name, list(shape), dt).ap()

    def _want(self, eng, deps):
        want = {}
        for tok in deps:
            if tok is None:
                continue
            semkey, val = tok
            if want.get(semkey, 0) < val:
                want[semkey] = val
        waits = []
        for semkey, val in want.items():
            if self.marks[eng].get(semkey, 0) < val:
                waits.append((semkey, val))
                self.marks[eng][semkey] = val
        return waits

    def op(self, eng, fn, deps=(), dma=False):
        waits = self._want(eng, deps)
        semkey = ("e", eng)
        self.counts[eng] += 1
        self.records[eng].append((waits, fn, semkey, 1, False))
        return (semkey, self.counts[eng])

    def dma(self, eng, out, in_, deps=(), **kw):
        i = self.dma_i[eng]
        self.dma_i[eng] = i + 1
        j = i % self.K_DMA
        semkey = ("d", eng, j)
        slot_use = i // self.K_DMA
        deps = list(deps) + [(semkey, 16 * slot_use)]
        waits = self._want(eng, deps)
        fn = lambda e: e.dma_start(out=out, in_=in_, **kw)
        self.records[eng].append((waits, fn, semkey, 16, True))
        return (semkey, 16 * (slot_use + 1))

    def barrier_tokens(self):
        toks = [(("e", e), self.counts[e]) for e in ENGS if self.counts[e] > 0]
        for e in ENGS:
            n = self.dma_i[e]
            for j in range(self.K_DMA):
                full = n // self.K_DMA + (1 if j < n % self.K_DMA else 0)
                if full > 0:
                    toks.append((("d", e, j), 16 * full))
        return toks

    def finalize(self, final_deps):
        gw = self._want("gp", list(final_deps) + self.barrier_tokens())
        self.records["gp"].append((gw, None, None, 0, False))
        for e in ENGS:
            if e != "gp" and self.counts[e] > 0:
                w = self._want(e, [(("e", e), self.counts[e])])
                if w:
                    self.records[e].append((w, None, None, 0, False))
        nc = self.nc
        sems = {}
        for e in ENGS:
            sems[("e", e)] = self.ctx.enter_context(nc.semaphore(f"s_{e}"))
            for j in range(self.K_DMA):
                if any(r[2] == ("d", e, j) or any(w[0] == ("d", e, j) for w in r[0])
                       for recs in self.records.values() for r in recs):
                    sems[("d", e, j)] = self.ctx.enter_context(
                        nc.semaphore(f"d_{e}{j}"))
        block = self.ctx.enter_context(nc.Block())

        def ninstr():
            return sum(len(bb.instructions) for bb in nc.m.functions[0].blocks)

        def section(engname):
            def body(engobj):
                for waits, fn, semkey, inc, isdma in self.records[engname]:
                    for (sk, val) in waits:
                        engobj.wait_ge(sems[sk], val)
                    if fn is None:
                        continue
                    n0 = ninstr()
                    instr = fn(engobj)
                    if isdma:
                        nparts = ninstr() - n0
                        assert nparts == 1, (
                            f"{self.name}/{engname}: dma_start emitted "
                            f"{nparts} instructions; split it manually")
                    instr.then_inc(sems[semkey], inc)
            return body

        for engname, deco in (("te", block.tensor), ("ve", block.vector),
                              ("se", block.scalar), ("gp", block.gpsimd),
                              ("sy", block.sync)):
            if self.records[engname]:
                deco(section(engname))
        self.ctx.close()
        return nc


class PsumTile:
    def __init__(self, h):
        self.h = h
        self.wtoks = []
        self.rtoks = []

    def __getitem__(self, idx):
        return self.h[idx]

    def write_deps(self):
        return list(self.rtoks) + list(self.wtoks)

    def read_deps(self):
        return list(self.wtoks)

    def note_write(self, tok):
        if self.rtoks:
            self.rtoks = []
            self.wtoks = []
        self.wtoks.append(tok)

    def note_read(self, tok):
        self.rtoks.append(tok)


def mm(b, pt, out_ap, lhsT, rhs, deps=(), start=True, stop=True, first=False):
    d = list(deps)
    if first or start:
        d += pt.write_deps()
    tok = b.op("te", lambda e: e.matmul(out_ap, lhsT, rhs, start=start, stop=stop), deps=d)
    pt.note_write(tok)
    return tok


def pread(b, eng, pt, fn, deps=()):
    tok = b.op(eng, fn, deps=list(deps) + pt.read_deps())
    pt.note_read(tok)
    return tok



def emit_const_identity(b, name="ident"):
    ident = b.sbuf(name, [128, 128], BF16)
    t0 = b.op("gp", lambda e: e.memset(ident[:, :], 0.0))
    t1 = b.op("gp", lambda e: e.affine_select(
        out=ident[:, :], in_=ident[:, :], pattern=[[-1, 128]],
        compare_op=ALU.not_equal, fill=1.0, base=0, channel_multiplier=1),
        deps=[t0])
    return ident, t1


def emit_ln_tile(b, x_ap, xdeps, out_ap, scratch, stats, sc, scdep=None):
    s_sum = stats[:, sc:sc + 1]
    s_sq = stats[:, sc + 1:sc + 2]
    s_mean = stats[:, sc + 2:sc + 3]
    s_rstd = stats[:, sc + 3:sc + 4]
    t_sum = b.op("ve", lambda e: e.reduce_sum(s_sum, x_ap, axis=AX.X), deps=xdeps)
    t_sq = b.op("se", lambda e: e.activation(scratch, x_ap, AF.Square,
                accum_out=s_sq), deps=list(xdeps) + ([scdep] if scdep else []))
    t_mean = b.op("ve", lambda e: e.tensor_scalar(
        s_mean, s_sum, 1.0 / D, None, op0=ALU.mult, op1=ALU.bypass), deps=[t_sum])
    t_m2 = b.op("ve", lambda e: e.tensor_tensor(
        s_rstd, s_mean, s_mean, op=ALU.mult), deps=[t_mean])
    t_var = b.op("ve", lambda e: e.scalar_tensor_tensor(
        s_rstd, s_sq, 1.0 / D, s_rstd, op0=ALU.mult, op1=ALU.subtract),
        deps=[t_sq, t_m2])
    t_vare = b.op("ve", lambda e: e.tensor_scalar(
        s_rstd, s_rstd, EPS, None, op0=ALU.add, op1=ALU.bypass), deps=[t_var])
    t_ln = b.op("se", lambda e: e.activation(s_rstd, s_rstd, AF.Ln), deps=[t_vare])
    t_rstd = b.op("se", lambda e: e.activation(s_rstd, s_rstd, AF.Exp, scale=-0.5),
                  deps=[t_ln])
    t_out = b.op("ve", lambda e: e.tensor_scalar(
        out_ap, x_ap, s_mean, s_rstd, op0=ALU.subtract, op1=ALU.mult),
        deps=[t_rstd] + list(xdeps))
    return t_out, t_sq



def build_A1():
    b = Builder("A1")
    x_in = b.din("x", [TLOC, D], F32)
    wq = b.din("wq", [D, D], F32)
    wk = b.din("wk", [D, D], F32)
    wv = b.din("wv", [D, D], F32)
    wg = b.din("wg", [D, D], F32)
    wgk1 = b.din("wgk1", [D, 16], F32)
    wgk2 = b.din("wgk2", [16, D], F32)
    bgk2 = b.din("bgk2", [1, D], F32)

    o_q = b.dout("o_q", [TLOC, D], BF16)
    o_k = b.dout("o_k", [TLOC, D], BF16)
    o_v = b.dout("o_v", [TLOC, D], BF16)
    o_g = b.dout("o_g", [TLOC, D], BF16)
    o_gk = b.dout("o_gk", [TLOC, D], F32)
    o_xn = b.dout("o_xn", [TLOC, D], BF16)

    ident, tid = emit_const_identity(b)

    W = {}
    wtok = {}
    for nm, src in (("wq", wq), ("wk", wk), ("wv", wv), ("wg", wg)):
        W[nm] = b.sbuf(f"W_{nm}", [128, 8 * D], BF16)
        wtok[nm] = b.dma("gp", W[nm][:, :].rearrange("p (c d) -> p c d", c=8),
                         src.rearrange("(c p) d -> p c d", p=128))
    Wgk1 = b.sbuf("W_wgk1", [128, 8 * 16], BF16)
    t_wgk1 = b.dma("gp", Wgk1[:, :].rearrange("p (c r) -> p c r", c=8),
                   wgk1.rearrange("(c p) r -> p c r", p=128))
    Wgk2 = b.sbuf("W_wgk2", [16, D], BF16)
    t_wgk2 = b.dma("gp", Wgk2[:, :], wgk2[:, :])
    BG = b.sbuf("BG", [128, D], F32)
    t_bg = b.dma("gp", BG[:, :], bgk2[0:1, :].partition_broadcast(128))

    XR = b.sbuf("XR", [128, 2 * D], F32)
    XN = b.sbuf("XN", [128, NT * D], BF16)
    scratch = b.sbuf("scratch", [128, D], F32)
    stats = b.sbuf("stats", [128, 4 * NT], F32)
    x_r = x_in.rearrange("(t p) d -> p t d", p=128)
    ln_toks = []
    ring_dep = [None, None]
    scdep = None
    for t in range(NT):
        r = t % 2
        xa = XR[:, r * D:(r + 1) * D]
        t_x = b.dma("sy", xa, x_r[:, t, :],
                    deps=[ring_dep[r]] if ring_dep[r] else [])
        t_out, scdep = emit_ln_tile(b, xa, [t_x], XN[:, t * D:(t + 1) * D],
                                    scratch[:, :], stats, 4 * t, scdep=scdep)
        ring_dep[r] = t_out
        ln_toks.append(t_out)

    t_oxn = b.dma("sy", o_xn.rearrange("(t p) d -> p t d", p=128),
                  XN[:, :].rearrange("p (t d) -> p t d", t=NT), deps=ln_toks)

    XT = b.sbuf("XT", [128, 8 * TLOC], BF16)
    pT = [b.psum(f"pT{i}", [128, 128], BF16) for i in range(2)]
    xt_tok = {}
    xt_all = []
    for t in range(NT):
        for k in range(8):
            pt = pT[(t * 8 + k) % 2]
            ttok = b.op("te", lambda e, t=t, k=k, pt=pt: e.transpose(
                pt[:, :], XN[:, t * D + k * 128: t * D + (k + 1) * 128],
                ident[:, :]), deps=[ln_toks[t], tid] + pt.write_deps())
            pt.note_write(ttok)
            etok = pread(b, "ve", pt, lambda e, t=t, k=k, pt=pt: e.tensor_copy(
                XT[:, k * TLOC + t * 128: k * TLOC + (t + 1) * 128], pt[:, :]))
            xt_all.append(etok)
            xt_tok[(t, k)] = etok

    pj = [b.psum(f"pj{i}", [128, 512]) for i in range(4)]
    stage = b.sbuf("stage", [128, 2 * D], BF16)
    out_toks = []
    st_dep = [None, None]
    outs = {"wq": o_q, "wk": o_k, "wv": o_v, "wg": o_g}
    for t in range(NT):
        for pi, nm in enumerate(("wq", "wk", "wv", "wg")):
            sbase = (t * 4 + pi) % 2
            ev_toks = []
            for h in range(2):
                pt = pj[(t * 8 + pi * 2 + h) % 4]
                for k in range(8):
                    tok = mm(b, pt, pt[:, :],
                             XT[:, k * TLOC + t * 128: k * TLOC + (t + 1) * 128],
                             W[nm][:, k * D + h * 512: k * D + (h + 1) * 512],
                             deps=[xt_tok[(t, k)], wtok[nm]],
                             start=(k == 0), stop=(k == 7), first=(k == 0))
                dst = stage[:, sbase * D + h * 512: sbase * D + (h + 1) * 512]
                dd = [st_dep[sbase]] if st_dep[sbase] else []
                if nm == "wg":
                    ev = pread(b, "se", pt, lambda e, pt=pt, dst=dst: e.activation(
                        dst, pt[:, :], AF.Silu), deps=dd)
                else:
                    ev = pread(b, "ve", pt, lambda e, pt=pt, dst=dst: e.tensor_copy(
                        dst, pt[:, :]), deps=dd)
                ev_toks.append(ev)
            otok = b.dma("sy",
                         outs[nm].rearrange("(t p) d -> p t d", p=128)[:, t, :],
                         stage[:, sbase * D:(sbase + 1) * D], deps=ev_toks)
            st_dep[sbase] = otok
            out_toks.append(otok)

    pr1 = b.psum("pr1", [16, 512])
    R1 = b.sbuf("R1", [16, TLOC], BF16)
    r1_toks = []
    for s in range(TLOC // 512):
        for k in range(8):
            tok = mm(b, pr1, pr1[:, :],
                     Wgk1[:, k * 16:(k + 1) * 16],
                     XT[:, k * TLOC + s * 512: k * TLOC + (s + 1) * 512],
                     deps=xt_all + [t_wgk1], start=(k == 0), stop=(k == 7),
                     first=(k == 0))
        r1_toks.append(pread(b, "ve", pr1, lambda e, s=s: e.tensor_copy(
            R1[:, s * 512:(s + 1) * 512], pr1[:, :])))

    pz = [pj[0], pj[1]]
    ZA = b.sbuf("ZA", [128, 2 * D], F32)
    GK = b.sbuf("GK", [128, 2 * D], F32)
    gk_dep = [None, None]
    for t in range(NT):
        sbase = t % 2
        ev = []
        for h in range(2):
            pt = pz[h]
            tok = mm(b, pt, pt[:, :], R1[:, t * 128:(t + 1) * 128],
                     Wgk2[:, h * 512:(h + 1) * 512],
                     deps=r1_toks + [t_wgk2], start=True, stop=True, first=True)
            dd = [gk_dep[sbase]] if gk_dep[sbase] else []
            ev.append(pread(b, "ve", pt, lambda e, pt=pt, h=h, sbase=sbase: e.tensor_tensor(
                ZA[:, sbase * D + h * 512: sbase * D + (h + 1) * 512], pt[:, :],
                BG[:, h * 512:(h + 1) * 512], op=ALU.add), deps=dd + [t_bg]))
        zs = ZA[:, sbase * D:(sbase + 1) * D]
        gs = GK[:, sbase * D:(sbase + 1) * D]
        t1 = b.op("se", lambda e, zs=zs, gs=gs: e.activation(gs, zs, AF.Abs),
                  deps=ev)
        t2 = b.op("se", lambda e, gs=gs: e.activation(gs, gs, AF.Exp, scale=-1.0),
                  deps=[t1])
        t3 = b.op("se", lambda e, gs=gs: e.activation(gs, gs, AF.Ln, bias=1.0),
                  deps=[t2])
        t4 = b.op("se", lambda e, zs=zs: e.activation(zs, zs, AF.Relu, scale=-1.0),
                  deps=ev + [t1])
        t5 = b.op("ve", lambda e, zs=zs, gs=gs: e.tensor_tensor(
            gs, gs, zs, op=ALU.add), deps=[t3, t4])
        otok = b.dma("sy", o_gk.rearrange("(t p) d -> p t d", p=128)[:, t, :],
                     gs, deps=[t5])
        gk_dep[sbase] = otok
        out_toks.append(otok)

    nc = b.finalize(out_toks + [t_oxn])
    return nc



NCH = SEQ // 128


def build_A2():
    b = Builder("A2")
    d_fm = b.din("d_fm", [NCH, 64, 1024], BF16)
    d_tv = b.din("d_tv", [NCH, 128, 512], BF16)
    d_gk = b.din("d_gk", [NCH, 128, 256], F32)
    d_o = b.dout("d_o", [NCH, 128, 256], F32)

    ident, tid = emit_const_identity(b)

    TRI = b.sbuf("TRI", [128, 128], F32)
    t0 = b.op("gp", lambda e: e.memset(TRI[:, :], -1.0 / 16.0))
    t_tri = b.op("gp", lambda e: e.affine_select(
        out=TRI[:, :], in_=TRI[:, :], pattern=[[1, 128]], compare_op=ALU.is_ge,
        fill=0.0, base=0, channel_multiplier=-1), deps=[t0])
    ONESC = b.sbuf("ONESC", [128, 1], F32)
    t_ones = b.op("gp", lambda e: e.memset(ONESC[:, :], -1.0 / 16.0))
    U4 = b.sbuf("U4", [128, 512], BF16)
    t1 = b.op("gp", lambda e: e.memset(U4[:, :], 1.0))
    t_u4 = b.op("gp", lambda e: e.affine_select(
        out=U4[:, :], in_=U4[:, :], pattern=[[0, 4], [1, 128]],
        compare_op=ALU.is_ge, fill=0.0, base=0, channel_multiplier=-1),
        deps=[t1])

    FMR = b.sbuf("FMR", [64, 2 * 1024], BF16)
    TVR = b.sbuf("TVR", [128, 2 * 512], BF16)
    GKR = b.sbuf("GKR", [128, 2 * 256], F32)
    CUMP = b.sbuf("CUMP", [128, 256], F32)
    CUMN = b.sbuf("CUMN", [128, 256], F32)
    VT = b.sbuf("VT", [128, 256], BF16)
    BM = b.sbuf("BM", [128, 512], BF16)
    OS = b.sbuf("OS", [128, 2 * 256], F32)
    S = b.sbuf("S", [64, 256], F32)
    SB = b.sbuf("SB", [64, 256], BF16)
    STB = b.sbuf("STB", [64, 256], BF16)
    GEXP = b.sbuf("GEXP", [64, 4], F32)

    p_cs = b.psum("p_cs", [128, 256])
    p_sc = b.psum("p_sc", [128, 512])
    p_O = b.psum("p_O", [128, 256])
    p_VTK = b.psum("p_VTK", [64, 256])
    p_G = b.psum("p_G", [64, 4])
    p_ST = b.psum("p_ST", [64, 256], BF16)

    t_s0 = b.op("gp", lambda e: e.memset(S[:, :], 0.0))
    t_stb0 = b.op("gp", lambda e: e.memset(STB[:, :], 0.0))

    ring_dep = [None, None]
    gk_ring_dep = [None, None]
    os_dep = [None, None]
    s_prev = t_s0
    stb_prev = t_stb0
    out_toks = []
    war = {"CUMP": None, "CUMN": None, "VT": None, "BM": None, "GEXP": None,
           "SB": None}

    for c in range(NCH):
        r = c % 2
        fm = FMR[:, r * 1024:(r + 1) * 1024]
        tv = TVR[:, r * 512:(r + 1) * 512]
        gk = GKR[:, r * 256:(r + 1) * 256]
        t_fm = b.dma("sy", fm, d_fm[c, :, :],
                     deps=[ring_dep[r]] if ring_dep[r] else [])
        t_tv = b.dma("sy", tv, d_tv[c, :, :],
                     deps=[ring_dep[r]] if ring_dep[r] else [])
        t_bf = (t_tv[0], t_tv[1])
        t_gk = b.dma("gp", gk, d_gk[c, :, :],
                     deps=[gk_ring_dep[r]] if gk_ring_dep[r] else [])
        qfm = lambda u: fm[0:64, u * 128:(u + 1) * 128]
        kfm = lambda u: fm[0:64, 512 + u * 128: 512 + (u + 1) * 128]
        ktm = lambda u: tv[:, u * 64:(u + 1) * 64]
        vtm = tv[:, 256:512]

        t_cs = mm(b, p_cs, p_cs[:, :], TRI[:, :], gk, deps=[t_tri, t_gk],
                  start=True, stop=True, first=True)
        g_toks = []
        for u in range(4):
            g_toks.append(mm(b, p_G, p_G[:, u:u + 1],
                             gk[:, u * 64:(u + 1) * 64], ONESC[:, :],
                             deps=[t_gk, t_ones], start=True, stop=True,
                             first=(u == 0)))
        t_cp = pread(b, "se", p_cs, lambda e: e.activation(
            CUMP[:, :], p_cs[:, :], AF.Exp), deps=[war["CUMP"]])
        t_cn = pread(b, "se", p_cs, lambda e: e.activation(
            CUMN[:, :], p_cs[:, :], AF.Exp, scale=-1.0), deps=[war["CUMN"]])
        t_g = pread(b, "se", p_G, lambda e: e.activation(
            GEXP[:, :], p_G[:, :], AF.Exp), deps=[war["GEXP"]])
        t_vt = b.op("ve", lambda e, vtm=vtm: e.tensor_tensor(
            VT[:, :], vtm, CUMN[:, :], op=ALU.mult),
            deps=[t_tv, t_cn, war["VT"]])
        war["CUMN"] = t_vt
        sc_toks = []
        for u in range(4):
            sc_toks.append(mm(b, p_sc, p_sc[:, u * 128:(u + 1) * 128],
                              kfm(u), qfm(u), deps=[t_fm],
                              start=True, stop=True, first=(u == 0)))
        t_bm = pread(b, "ve", p_sc, lambda e: e.tensor_tensor(
            BM[:, :], p_sc[:, :], U4[:, :], op=ALU.mult),
            deps=[t_u4, war["BM"]])
        o_mm = []
        for u in range(4):
            o_mm.append(mm(b, p_O, p_O[:, u * 64:(u + 1) * 64],
                           BM[:, u * 128:(u + 1) * 128],
                           VT[:, u * 64:(u + 1) * 64],
                           deps=[t_bm, t_vt], start=True, stop=False,
                           first=(u == 0)))
            o_mm.append(mm(b, p_O, p_O[:, u * 64:(u + 1) * 64],
                           qfm(u), STB[0:64, u * 64:(u + 1) * 64],
                           deps=[t_fm, stb_prev], start=False, stop=True))
        stb_read_o = o_mm[-1]
        war["BM"] = o_mm[-1]
        osl = OS[:, (c % 2) * 256:(c % 2) * 256 + 256]
        t_oe = pread(b, "ve", p_O, lambda e, osl=osl: e.tensor_tensor(
            osl, p_O[:, :], CUMP[:, :], op=ALU.mult),
            deps=[t_cp] + ([os_dep[c % 2]] if os_dep[c % 2] else []))
        t_od = b.dma("sy", d_o[c, :, :], osl, deps=[t_oe])
        os_dep[c % 2] = t_od
        out_toks.append(t_od)
        war["CUMP"] = t_oe
        vtk_toks = []
        for u in range(4):
            vtk_toks.append(mm(b, p_VTK, p_VTK[:, u * 64:(u + 1) * 64],
                               VT[:, u * 64:(u + 1) * 64], ktm(u),
                               deps=[t_vt, t_tv], start=True, stop=True,
                               first=(u == 0)))
        war["VT"] = vtk_toks[-1]
        t_sa = pread(b, "ve", p_VTK, lambda e: e.tensor_tensor(
            S[:, :], S[:, :], p_VTK[:, :], op=ALU.add), deps=[s_prev])
        gb = GEXP[:, :].unsqueeze(2).broadcast_to([64, 4, 64])
        s3 = S[:, :].rearrange("p (u k) -> p u k", u=4)
        t_sm = b.op("ve", lambda e: e.tensor_tensor(s3, s3, gb, op=ALU.mult),
                    deps=[t_sa, t_g])
        s_prev = t_sm
        war["GEXP"] = t_sm
        t_sb = b.op("se", lambda e: e.copy(SB[:, :], S[:, :]),
                    deps=[t_sm, war["SB"]])
        tr_toks = []
        for u in range(4):
            tr_toks.append(b.op("te", lambda e, u=u: e.transpose(
                p_ST[:, u * 64:(u + 1) * 64], SB[:, u * 64:(u + 1) * 64],
                ident[0:64, 0:64]),
                deps=[t_sb, tid] + (p_ST.write_deps() if u == 0 else [])))
            p_ST.note_write(tr_toks[-1])
        t_stb = pread(b, "ve", p_ST, lambda e: e.tensor_copy(
            STB[:, :], p_ST[:, :]), deps=[stb_read_o])
        stb_prev = t_stb
        war["SB"] = tr_toks[-1]
        ring_dep[r] = vtk_toks[-1]
        gk_ring_dep[r] = g_toks[-1] if g_toks[-1][1] > t_cs[1] else t_cs

    nc = b.finalize(out_toks)
    return nc



def emit_ln_res(b, a_ap, adeps, x_ap, xdeps, out_ap, tmp, scratch, stats, sc,
                scdep=None):
    t_ln, t_sq = emit_ln_tile(b, a_ap, adeps, tmp, scratch, stats, sc,
                              scdep=scdep)
    t_out = b.op("ve", lambda e: e.tensor_tensor(out_ap, x_ap, tmp, op=ALU.add),
                 deps=[t_ln] + list(xdeps))
    return t_out, t_sq


def build_AB():
    b = Builder("AB")
    x_in = b.din("x", [TLOC, D], F32)
    og_in = b.din("og", [TLOC, D], F32)
    gs_in = b.din("gsil", [TLOC, D], BF16)
    gamma = b.din("gamma", [1, D], F32)
    beta = b.din("beta", [1, D], F32)
    wout = b.din("wout", [D, D], F32)
    phi = b.din("phi", [D, 2048], F32)
    o_x1 = b.dout("o_x1", [TLOC, D], F32)
    o_C = b.dout("o_C", [TLOC, 2048], BF16)
    o_ps = b.dout("o_ps", [B_, 2048, D], F32)
    d_D = b.dtmp("d_D", [TLOC, 2048], BF16)

    ident, tid = emit_const_identity(b)
    GT = b.sbuf("GT", [128, D], F32)
    BT = b.sbuf("BT", [128, D], F32)
    t_gt = b.dma("gp", GT[:, :], gamma[0:1, :].partition_broadcast(128))
    t_bt = b.dma("gp", BT[:, :], beta[0:1, :].partition_broadcast(128))
    WBUF = b.sbuf("WBUF", [128, 8 * 2048], BF16)
    t_wo = b.dma("gp", WBUF[:, 0:8 * D].rearrange("p (c d) -> p c d", c=8),
                 wout.rearrange("(c p) d -> p c d", p=128))

    OG = b.sbuf("OG", [128, 2 * D], F32)
    
    GS = b.sbuf("GS", [128, 2 * D], BF16)
    XR = b.sbuf("XR", [128, 2 * D], F32)
    ON = b.sbuf("ON", [128, D], F32)
    PB = b.sbuf("PB", [128, D], BF16)
    PTS = b.sbuf("PTS", [128, 8 * 128], BF16)
    AT = b.sbuf("AT", [128, D], F32)
    TMP = b.sbuf("TMP", [128, D], F32)
    X1R = b.sbuf("X1R", [128, 2 * D], F32)
    XN2 = b.sbuf("XN2", [128, NT * D], BF16)
    XT2 = b.sbuf("XT2", [128, 8 * TLOC], BF16)
    scratch = b.sbuf("scratch", [128, D], F32)
    stats = b.sbuf("stats", [128, 64], F32)

    LG0 = b.psum("LG0", [128, 2048])
    pTs = [b.psum("pT0", [128, 128], BF16), b.psum("pT1", [128, 128], BF16)]

    ring = {"OG": [None, None], "GS": [None, None], "XR": [None, None],
            "X1R": [None, None]}
    war = {"ON": None, "PB": None, "PTS": None, "AT": None, "TMP": None}
    scdep = None
    out_toks = []
    xt2_tok = {}
    xn2_toks = []
    x_r = x_in.rearrange("(t p) d -> p t d", p=128)
    og_r = og_in.rearrange("(t p) d -> p t d", p=128)
    gs_r = gs_in.rearrange("(t p) d -> p t d", p=128)
    x1_r = o_x1.rearrange("(t p) d -> p t d", p=128)

    for t in range(NT):
        r = t % 2
        oga = OG[:, r * D:(r + 1) * D]
        gsa = GS[:, r * D:(r + 1) * D]
        xra = XR[:, r * D:(r + 1) * D]
        t_og = b.dma("sy", oga, og_r[:, t, :], deps=[ring["OG"][r]])
        t_gs = b.dma("sy", gsa, gs_r[:, t, :], deps=[ring["GS"][r]])
        t_xr = b.dma("sy", xra, x_r[:, t, :], deps=[ring["XR"][r]])
        t_on, scdep = emit_ln_tile(b, oga, [t_og, war["ON"]], ON[:, :],
                                   scratch[:, :], stats, 0, scdep=scdep)
        t_g1 = b.op("ve", lambda e: e.tensor_tensor(
            ON[:, :], ON[:, :], GT[:, :], op=ALU.mult), deps=[t_on, t_gt])
        t_g2 = b.op("ve", lambda e: e.tensor_tensor(
            ON[:, :], ON[:, :], BT[:, :], op=ALU.add), deps=[t_g1, t_bt])
        t_pb = b.op("ve", lambda e, gsa=gsa: e.tensor_tensor(
            PB[:, :], ON[:, :], gsa, op=ALU.mult),
            deps=[t_g2, t_gs, war["PB"]])
        ring["OG"][r] = t_on
        ring["GS"][r] = t_pb
        war["ON"] = t_pb
        tr_evs = []
        for k in range(8):
            pt = pTs[k % 2]
            ttok = b.op("te", lambda e, k=k, pt=pt: e.transpose(
                pt[:, :], PB[:, k * 128:(k + 1) * 128], ident[:, :]),
                deps=[t_pb, tid] + pt.write_deps())
            pt.note_write(ttok)
            tr_evs.append(pread(b, "ve", pt, lambda e, k=k, pt=pt: e.tensor_copy(
                PTS[:, k * 128:(k + 1) * 128], pt[:, :]), deps=[war["PTS"]]))
        war["PB"] = tr_evs[-1]
        wo_ev = []
        for h in range(2):
            for k in range(8):
                mm(b, LG0, LG0[:, h * 512:(h + 1) * 512],
                   PTS[:, k * 128:(k + 1) * 128],
                   WBUF[:, k * D + h * 512: k * D + (h + 1) * 512],
                   deps=[tr_evs[k], t_wo], start=(k == 0), stop=(k == 7),
                   first=(k == 0))
        t_at = pread(b, "ve", LG0, lambda e: e.tensor_copy(
            AT[:, :], LG0[:, 0:D]), deps=[war["AT"]])
        war["PTS"] = t_at
        x1a = X1R[:, r * D:(r + 1) * D]
        t_x1, scdep = emit_ln_res(b, AT[:, :], [t_at], xra, [t_xr],
                                  x1a, TMP[:, :], scratch[:, :], stats, 8,
                                  scdep=scdep)
        war["AT"] = t_x1
        war["TMP"] = t_x1
        ring["XR"][r] = t_x1
        t_x1d = b.dma("sy", x1_r[:, t, :], x1a, deps=[t_x1])
        out_toks.append(t_x1d)
        t_xn2, scdep = emit_ln_tile(b, x1a, [t_x1], XN2[:, t * D:(t + 1) * D],
                                    scratch[:, :], stats, 16, scdep=scdep)
        xn2_toks.append(t_xn2)
        ring["X1R"][r] = t_xn2
        for k in range(8):
            pt = pTs[k % 2]
            ttok = b.op("te", lambda e, t=t, k=k, pt=pt: e.transpose(
                pt[:, :], XN2[:, t * D + k * 128: t * D + (k + 1) * 128],
                ident[:, :]), deps=[t_xn2, tid] + pt.write_deps())
            pt.note_write(ttok)
            xt2_tok[(t, k)] = pread(b, "ve", pt,
                lambda e, t=t, k=k, pt=pt: e.tensor_copy(
                    XT2[:, k * TLOC + t * 128: k * TLOC + (t + 1) * 128], pt[:, :]))
        ring["X1R"][r] = xt2_tok[(t, 7)]

    phi_dep = [tok for tok in out_toks]
    t_phi = b.dma("gp", WBUF[:, :].rearrange("p (c n) -> p c n", c=8),
                  phi.rearrange("(c p) n -> p c n", p=128),
                  deps=b.barrier_tokens())
    DS = b.sbuf("DS", [128, 2 * 2048], BF16)
    LA = b.sbuf("LA", [128, 2048], F32)
    DIF = b.sbuf("DIF", [128, 2048], F32)
    CC = b.sbuf("CC", [128, 2 * 2048], BF16)
    c_r = o_C.rearrange("(t p) n -> p t n", p=128)
    d_r = d_D.rearrange("(t p) n -> p t n", p=128)
    d_toks = []
    war2 = {"LA": None, "DIF": None, "CC0": None, "CC1": None,
            "DS0": None, "DS1": None}

    def logits_tile(t, deps):
        mms = []
        for h in range(4):
            for k in range(8):
                mms.append(mm(b, LG0, LG0[:, h * 512:(h + 1) * 512],
                              XT2[:, k * TLOC + t * 128: k * TLOC + (t + 1) * 128],
                              WBUF[:, k * 2048 + h * 512: k * 2048 + (h + 1) * 512],
                              deps=[xt2_tok[(t, k)], t_phi] + deps,
                              start=(k == 0), stop=(k == 7),
                              first=(k == 0)))
        return mms

    for i in range(NT // 2):
        m0 = logits_tile(i, [])
        t_la = pread(b, "ve", LG0, lambda e: e.tensor_copy(LA[:, :], LG0[:, :]),
                     deps=[war2["LA"]])
        m1 = logits_tile(i + 8, [t_la])
        t_dif = pread(b, "ve", LG0, lambda e: e.tensor_tensor(
            DIF[:, :], LA[:, :], LG0[:, :], op=ALU.subtract),
            deps=[t_la, war2["DIF"]])
        t_d0 = b.op("se", lambda e: e.activation(
            DS[:, 0:2048], DIF[:, :], AF.Sigmoid), deps=[t_dif, war2["DS0"]])
        t_d1 = b.op("se", lambda e: e.activation(
            DS[:, 2048:4096], DIF[:, :], AF.Sigmoid,
            scale=-1.0), deps=[t_dif, war2["DS1"]])
        t_dd0 = b.dma("sy", d_r[:, i, :], DS[:, 0:2048], deps=[t_d0])
        t_dd1 = b.dma("sy", d_r[:, i + 8, :], DS[:, 2048:4096], deps=[t_d1])
        war2["DS0"] = t_dd0
        war2["DS1"] = t_dd1
        d_toks += [t_dd0, t_dd1]
        war2["DIF"] = t_d1
        for (tt, srcap, is_psum, wkey) in ((i, LA[:, :], False, "CC0"),
                                           (i + 8, LG0[:, :], True, "CC1")):
            sb0 = 24 if not is_psum else 28
            cc = CC[:, (0 if not is_psum else 2048):(2048 if not is_psum else 4096)]
            mx = stats[:, sb0:sb0 + 1]
            sm = stats[:, sb0 + 1:sb0 + 2]
            rc = stats[:, sb0 + 2:sb0 + 3]
            if is_psum:
                t_mx0 = pread(b, "ve", LG0, lambda e, srcap=srcap, mx=mx: e.reduce_max(
                    mx, srcap, axis=AX.X), deps=[t_dif, war2[wkey]])
            else:
                t_mx0 = b.op("ve", lambda e, srcap=srcap, mx=mx: e.reduce_max(
                    mx, srcap, axis=AX.X), deps=[t_la, t_dif, war2[wkey]])
            t_mx = b.op("ve", lambda e, mx=mx: e.tensor_scalar(
                mx, mx, -1.0, None, op0=ALU.mult, op1=ALU.bypass), deps=[t_mx0])
            if is_psum:
                t_ex = pread(b, "se", LG0, lambda e, srcap=srcap, cc=cc, mx=mx, sm=sm: e.activation(
                    cc, srcap, AF.Exp, bias=mx, accum_out=sm),
                    deps=[t_mx, war2[wkey]])
            else:
                t_ex = b.op("se", lambda e, srcap=srcap, cc=cc, mx=mx, sm=sm: e.activation(
                    cc, srcap, AF.Exp, bias=mx, accum_out=sm),
                    deps=[t_mx, war2[wkey]])
            t_rc = b.op("ve", lambda e, sm=sm, rc=rc: e.reciprocal(
                rc, sm), deps=[t_ex])
            t_cn = b.op("ve", lambda e, cc=cc, rc=rc: e.tensor_scalar(
                cc, cc, rc, None, op0=ALU.mult, op1=ALU.bypass),
                deps=[t_rc, t_ex])
            t_cd = b.dma("sy", c_r[:, tt, :], cc, deps=[t_cn])
            war2[wkey] = t_cd
            out_toks.append(t_cd)
        war2["LA"] = t_dif

    PS = b.sbuf("PS", [128, 512], F32)
    DR = b.sbuf("DR", [128, 2 * 8 * 128], BF16)
    ps_dep = None
    dr_dep = [None, None]
    for bb in range(B_):
        for npc in range(16):
            r = npc % 2
            dra = DR[:, r * 1024:(r + 1) * 1024]
            t_dr = b.dma("sy",
                         dra.rearrange("p (j n) -> p j n", j=8),
                         d_r[:, bb * 8:(bb + 1) * 8,
                             npc * 128:(npc + 1) * 128],
                         deps=[dr_dep[r]] + d_toks)
            last = None
            for dsl in range(2):
                for j in range(8):
                    t = bb * 8 + j
                    last = mm(
                        b, LG0, LG0[:, 0:512],
                        dra[:, j * 128:(j + 1) * 128],
                        XN2[:, t * D + dsl * 512: t * D + (dsl + 1) * 512],
                        deps=[xn2_toks[t], t_dr], start=(j == 0),
                        stop=(j == 7), first=(j == 0))
                t_ev = pread(b, "ve", LG0, lambda e: e.tensor_copy(
                    PS[:, :], LG0[:, 0:512]), deps=[ps_dep])
                t_pd = b.dma("sy", o_ps[bb, npc * 128:(npc + 1) * 128,
                                        dsl * 512:(dsl + 1) * 512], PS[:, :],
                             deps=[t_ev])
                ps_dep = t_pd
                out_toks.append(t_pd)
            dr_dep[r] = last

    nc = b.finalize(out_toks)
    return nc



def build_B2():
    b = Builder("B2")
    d_ps = b.din("d_ps", [NC, B_, D, 256], F32)
    d_wn = b.din("d_wn", [8, D, D], F32)
    d_bn = b.din("d_bn", [8, D], F32)
    o_y = b.dout("o_y", [B_, 8, 32, D], F32)

    SLT = b.sbuf("SLT", [128, B_ * 8 * 256], F32)
    SLB = b.sbuf("SLB", [128, B_ * 8 * 256], BF16)
    PSR = b.sbuf("PSR", [128, 2 * 256], F32)
    WN = b.sbuf("WN", [128, 2 * 8 * D], BF16)
    BN = b.sbuf("BN", [32, D], F32)
    YS = b.sbuf("YS", [32, 2 * D], F32)
    pY = b.psum("pY", [32, 1024])

    sum_toks = {}
    ps_ring = [None, None]
    for bb in range(B_):
        for dc in range(8):
            dst = SLT[:, (bb * 8 + dc) * 256:(bb * 8 + dc + 1) * 256]
            acc_tok = None
            for s in range(NC):
                r = s % 2
                pra = PSR[:, r * 256:(r + 1) * 256]
                t_l = b.dma("sy", pra,
                            d_ps[s, bb, dc * 128:(dc + 1) * 128, :],
                            deps=[ps_ring[r]])
                if s == 0:
                    acc_tok = b.op("ve", lambda e, pra=pra, dst=dst: e.tensor_copy(
                        dst, pra), deps=[t_l, acc_tok])
                else:
                    acc_tok = b.op("ve", lambda e, pra=pra, dst=dst: e.tensor_tensor(
                        dst, dst, pra, op=ALU.add), deps=[t_l, acc_tok])
                ps_ring[r] = acc_tok
            sum_toks[(bb, dc)] = b.op("ve", lambda e, dst=dst, bb=bb, dc=dc: e.tensor_copy(
                SLB[:, (bb * 8 + dc) * 256:(bb * 8 + dc + 1) * 256], dst),
                deps=[acc_tok])

    out_toks = []
    wn_ring = [None, None]
    ys_ring = [None, None]
    bn_dep = None
    for j in range(8):
        r = j % 2
        wna = WN[:, r * 8 * D:(r + 1) * 8 * D]
        t_w = b.dma("gp", wna.rearrange("p (c d) -> p c d", c=8),
                    d_wn[j].rearrange("(c p) d -> p c d", p=128),
                    deps=[wn_ring[r]])
        t_b = b.dma("sy", BN[:, :], d_bn[j:j + 1, :].partition_broadcast(32),
                    deps=[wn_ring[r], bn_dep])
        for bb in range(B_):
            yr = bb % 2
            ysa = YS[:, yr * D:(yr + 1) * D]
            last = None
            for h in range(2):
                for dc in range(8):
                    last = mm(b, pY, pY[:, h * 512:(h + 1) * 512],
                              SLB[:, (bb * 8 + dc) * 256 + 32 * j:
                                   (bb * 8 + dc) * 256 + 32 * j + 32],
                              wna[:, dc * D + h * 512: dc * D + (h + 1) * 512],
                              deps=[sum_toks[(bb, dc)], t_w],
                              start=(dc == 0), stop=(dc == 7), first=(dc == 0))
            t_ev = pread(b, "ve", pY, lambda e, ysa=ysa: e.tensor_tensor(
                ysa, pY[0:32, :], BN[:, :], op=ALU.add),
                deps=[t_b, ys_ring[yr]])
            bn_dep = t_ev
            t_yd = b.dma("sy", o_y[bb, j, :, :], ysa, deps=[t_ev])
            ys_ring[yr] = t_yd
            out_toks.append(t_yd)
        wn_ring[r] = last

    nc = b.finalize(out_toks)
    return nc



def build_BC():
    b = Builder("BC")
    d_y = b.din("d_y", [B_, 2048, D], F32)
    d_c = b.din("d_c", [TLOC, 2048], BF16)
    d_x1 = b.din("d_x1", [TLOC, D], F32)
    w_attn = b.din("w_attn", [D, 3 * D], F32)
    b_attn = b.din("b_attn", [1, 3 * D], F32)
    o_x2 = b.dout("o_x2", [TLOC, D], F32)
    o_q = b.dout("o_q", [TLOC, D], BF16)
    o_k = b.dout("o_k", [TLOC, D], BF16)
    o_v = b.dout("o_v", [TLOC, D], BF16)

    ident, tid = emit_const_identity(b)
    YB = b.sbuf("YB", [128, B_ * 16 * D], BF16)
    GEN8 = b.sbuf("GEN8", [128, 16 * TLOC], BF16)
    XN5 = b.sbuf("XN5", [128, NT * D], BF16)
    X1R = b.sbuf("X1R", [128, 2 * D], F32)
    FF = b.sbuf("FF", [128, D], F32)
    TMP = b.sbuf("TMP", [128, D], F32)
    X2R = b.sbuf("X2R", [128, 2 * D], F32)
    BA = b.sbuf("BA", [128, 3 * D], F32)
    QS = b.sbuf("QS", [128, 3 * D], BF16)
    scratch = b.sbuf("scratch", [128, D], F32)
    stats = b.sbuf("stats", [128, 32], F32)

    pO = b.psum("pO", [128, 1024])
    pQ = b.psum("pQ", [128, 1024])
    pTs = [b.psum("pT0", [128, 128], BF16), b.psum("pT1", [128, 128], BF16)]

    t_y = []
    for bb in range(B_):
        t_y.append(b.dma("gp",
                         YB[:, bb * 16 * D:(bb + 1) * 16 * D].rearrange(
                             "p (n d) -> p n d", n=16),
                         d_y[bb].rearrange("(n p) d -> p n d", p=128)))
    ct_toks = []
    for npc in range(16):
        ct_toks.append(b.dma("sy", GEN8[:, npc * TLOC:(npc + 1) * TLOC],
                             d_c[:, npc * 128:(npc + 1) * 128], transpose=True))
    t_ba = b.dma("gp", BA[:, :], b_attn[0:1, :].partition_broadcast(128))
    t_baq = b.op("ve", lambda e: e.tensor_scalar(
        BA[:, 0:D], BA[:, 0:D], 0.125, None, op0=ALU.mult, op1=ALU.bypass),
        deps=[t_ba])

    x1_r = d_x1.rearrange("(t p) d -> p t d", p=128)
    x2_r = o_x2.rearrange("(t p) d -> p t d", p=128)
    ring = [None, None]
    x2ring = [None, None]
    war = {"FF": None, "TMP": None}
    scdep = None
    xn5_toks = []
    out_toks = []
    for t in range(NT):
        bb = t // 8
        r = t % 2
        x1a = X1R[:, r * D:(r + 1) * D]
        t_x1 = b.dma("sy", x1a, x1_r[:, t, :], deps=[ring[r]])
        for h in range(2):
            for npc in range(16):
                mm(b, pO, pO[:, h * 512:(h + 1) * 512],
                   GEN8[:, npc * TLOC + t * 128: npc * TLOC + (t + 1) * 128],
                   YB[:, (bb * 16 + npc) * D + h * 512:
                      (bb * 16 + npc) * D + (h + 1) * 512],
                   deps=[ct_toks[npc], t_y[bb]],
                   start=(npc == 0), stop=(npc == 15), first=(npc == 0))
        t_ff = pread(b, "ve", pO, lambda e: e.tensor_copy(FF[:, :], pO[:, :]),
                     deps=[war["FF"]])
        x2a = X2R[:, r * D:(r + 1) * D]
        t_x2, scdep = emit_ln_res(b, FF[:, :], [t_ff], x1a, [t_x1], x2a,
                                  TMP[:, :], scratch[:, :], stats, 0,
                                  scdep=scdep)
        war["FF"] = t_x2
        ring[r] = t_x2
        out_toks.append(b.dma("sy", x2_r[:, t, :], x2a, deps=[t_x2]))
        t_xn5, scdep = emit_ln_tile(b, x2a, [t_x2], XN5[:, t * D:(t + 1) * D],
                                    scratch[:, :], stats, 8, scdep=scdep)
        xn5_toks.append(t_xn5)
        x2ring[r] = t_xn5
        ring[r] = t_xn5

    WA = YB
    t_wa = b.dma("gp", WA[:, 0:8 * 3 * D].rearrange("p (c d) -> p c d", c=8),
                 w_attn.rearrange("(c p) d -> p c d", p=128),
                 deps=b.barrier_tokens())
    xt_tok = {}
    outs = {0: o_q, 1: o_k, 2: o_v}
    qs_dep = [None]
    for t in range(NT):
        for k in range(8):
            pt = pTs[k % 2]
            ttok = b.op("te", lambda e, t=t, k=k, pt=pt: e.transpose(
                pt[:, :], XN5[:, t * D + k * 128: t * D + (k + 1) * 128],
                ident[:, :]),
                deps=[xn5_toks[t], tid, t_wa] + pt.write_deps())
            pt.note_write(ttok)
            xt_tok[(t, k)] = pread(b, "ve", pt,
                lambda e, t=t, k=k, pt=pt: e.tensor_copy(
                    GEN8[:, k * TLOC + t * 128: k * TLOC + (t + 1) * 128],
                    pt[:, :]))
        ev_toks = []
        for cs in range(6):
            ph = pQ if cs % 2 == 0 else pO
            for k in range(8):
                mm(b, ph, ph[:, 0:512],
                   GEN8[:, k * TLOC + t * 128: k * TLOC + (t + 1) * 128],
                   WA[:, k * 3 * D + cs * 512: k * 3 * D + (cs + 1) * 512],
                   deps=[xt_tok[(t, k)], t_wa], start=(k == 0), stop=(k == 7),
                   first=(k == 0))
            ev_toks.append(pread(b, "ve", ph,
                lambda e, cs=cs, ph=ph: e.scalar_tensor_tensor(
                    QS[:, cs * 512:(cs + 1) * 512], ph[:, 0:512],
                    0.125 if cs < 2 else 1.0, BA[:, cs * 512:(cs + 1) * 512],
                    op0=ALU.mult, op1=ALU.add),
                deps=[t_baq, qs_dep[0]]))
        for pi in range(3):
            tok = b.dma("sy", outs[pi].rearrange("(t p) d -> p t d", p=128)[:, t, :],
                        QS[:, pi * D:(pi + 1) * D],
                        deps=[ev_toks[2 * pi], ev_toks[2 * pi + 1]])
            out_toks.append(tok)
        qs_dep[0] = out_toks[-1]


    nc = b.finalize(out_toks)
    return nc



def build_C2(nu=4, nqg=16):
    b = Builder("C2")
    d_qf = b.din("d_qf", [nu, 64, SEQ], BF16)
    d_kf = b.din("d_kf", [nu, 64, SEQ], BF16)
    d_v = b.din("d_v", [nu, SEQ, 64], BF16)
    d_o = b.dout("d_o", [nu, nqg, 65, 512], F32)

    QF = b.sbuf("QF", [64, 4 * SEQ], BF16)
    KF = b.sbuf("KF", [64, 4 * SEQ], BF16)
    V65 = b.sbuf("V65", [128, 4 * 64 * 65], BF16)
    PB = b.sbuf("PB", [128, 2 * 1024], BF16)
    PMD = b.sbuf("PMD", [128, 2 * 512], BF16)
    DM = b.sbuf("DM", [128, 4 * 512], BF16)
    OS = b.sbuf("OS", [65, 2 * 512], F32)
    B15 = b.sbuf("B15", [128, 1], F32)

    p_scs = [b.psum("p_sc0", [128, 1024]), b.psum("p_sc1", [128, 1024])]
    pOs = [b.psum("pO0", [65, 512]), b.psum("pO1", [65, 512])]

    ld_toks = []
    for u in range(nu):
        ld_toks.append(b.dma("sy", QF[:, u * SEQ:(u + 1) * SEQ], d_qf[u]))
        ld_toks.append(b.dma("sy", KF[:, u * SEQ:(u + 1) * SEQ], d_kf[u]))
    t_b15 = b.op("gp", lambda e: e.memset(B15[:, :], -15.0))
    t_vm = b.op("gp", lambda e: e.memset(V65[:, :], 1.0))
    v_toks = {}
    for u in range(nu):
        v_toks[u] = b.dma("gp",
                          V65[:, u * 4160:(u + 1) * 4160].rearrange(
                              "p (ch w) -> p ch w", w=65)[:, :, 0:64],
                          d_v[u].rearrange("(ch p) v -> p ch v", p=128),
                          deps=[t_vm])
    dm_toks = []
    for dd in range(4):
        t0 = b.op("gp", lambda e, dd=dd: e.memset(DM[:, dd * 512:(dd + 1) * 512], 1.0))
        dm_toks.append(b.op("gp", lambda e, dd=dd: e.affine_select(
            out=DM[:, dd * 512:(dd + 1) * 512],
            in_=DM[:, dd * 512:(dd + 1) * 512],
            pattern=[[1, 512]], compare_op=ALU.is_ge, fill=0.0,
            base=-128 * dd, channel_multiplier=-1), deps=[t0]))

    out_toks = []
    war_pb = [None, None]
    war_pmd = [None, None]
    os_dep = [None, None]
    steps = []
    for u in range(nu):
        for qg in range(nqg):
            for kc2 in range((4 * qg + 4) // 2):
                steps.append((u, qg, kc2))

    exp_toks = [None] * len(steps)

    def emit_scores(i):
        u, qg, kc2 = steps[i]
        pp = p_scs[i % 2]
        qsl = QF[0:64, u * SEQ + qg * 512: u * SEQ + (qg + 1) * 512]
        for h2 in range(2):
            kc = 2 * kc2 + h2
            mm(b, pp, pp[:, h2 * 512:(h2 + 1) * 512],
               KF[0:64, u * SEQ + kc * 128: u * SEQ + (kc + 1) * 128],
               qsl, deps=[ld_toks[2 * u], ld_toks[2 * u + 1]],
               start=True, stop=True, first=(h2 == 0))
        pba = PB[:, (i % 2) * 1024:(i % 2) * 1024 + 1024]
        exp_toks[i] = pread(b, "se", pp, lambda e, pba=pba: e.activation(
            pba, pp[:, :], AF.Exp, bias=B15[:, :]),
            deps=[war_pb[i % 2], t_b15])

    def emit_av(i):
        u, qg, kc2 = steps[i]
        nk = 4 * qg + 4
        pO = pOs[qg % 2]
        pba = PB[:, (i % 2) * 1024:(i % 2) * 1024 + 1024]
        t_av = None
        for h2 in range(2):
            kc = 2 * kc2 + h2
            dd = kc - (nk - 4)
            psrc = pba[:, h2 * 512:(h2 + 1) * 512]
            extra = []
            if dd >= 0:
                pm = PMD[:, (dd % 2) * 512:(dd % 2) * 512 + 512]
                t_msk = b.op("ve", lambda e, psrc=psrc, dd=dd, pm=pm: e.tensor_tensor(
                    pm, psrc, DM[:, dd * 512:(dd + 1) * 512], op=ALU.mult),
                    deps=[exp_toks[i], dm_toks[dd], war_pmd[dd % 2]])
                psrc = pm
                extra = [t_msk]
            t_av = mm(b, pO, pO[:, :],
                      V65[:, u * 4160 + kc * 65: u * 4160 + (kc + 1) * 65],
                      psrc, deps=[exp_toks[i], v_toks[u]] + extra,
                      start=(kc == 0), stop=(kc == nk - 1),
                      first=(kc == 0))
            if dd >= 0:
                war_pmd[dd % 2] = t_av
        war_pb[i % 2] = t_av
        if kc2 == nk // 2 - 1:
            osl = OS[:, (qg % 2) * 512:(qg % 2) * 512 + 512]
            t_oe = pread(b, "ve", pO, lambda e, osl=osl, pO=pO: e.tensor_copy(
                osl, pO[:, :]), deps=[os_dep[qg % 2]])
            t_od = b.dma("sy", d_o[u, qg], osl, deps=[t_oe])
            os_dep[qg % 2] = t_od
            out_toks.append(t_od)

    for i in range(len(steps) + 1):
        if i < len(steps):
            emit_scores(i)
        if i > 0:
            emit_av(i - 1)

    nc = b.finalize(out_toks)
    return nc



def build_CD():
    b = Builder("CD")
    d_yin = b.din("d_yin", [TLOC, D], F32)
    d_sums = b.din("d_sums", [TLOC, 16], F32)
    d_x2 = b.din("d_x2", [TLOC, D], F32)
    w_proj = b.din("w_proj", [D, D], F32)
    b_proj = b.din("b_proj", [1, D], F32)
    rw_T = b.din("rw_T", [D, 16], F32)
    cbias = b.din("cbias", [1, 16], F32)
    d_gw = b.din("d_gw", [12, D, 512], F32)
    d_uw = b.din("d_uw", [12, D, 512], F32)
    d_dw = b.din("d_dw", [24, 256, D], F32)
    o_x4 = b.dout("o_x4", [TLOC, D], F32)
    d_x3 = b.dtmp("d_x3", [TLOC, D], F32)

    ident, tid = emit_const_identity(b)
    XT4 = b.sbuf("XT4", [128, 8 * TLOC], BF16)
    HF = b.sbuf("HF", [128, 24 * 1024], BF16)
    WP = HF
    RWT = b.sbuf("RWT", [128, 8 * 16], BF16)
    GW = b.sbuf("GW", [128, 3 * 4096], BF16)
    UW = b.sbuf("UW", [128, 3 * 4096], BF16)
    DWB = b.sbuf("DWB", [128, 3 * 1024], BF16)
    BP = b.sbuf("BP", [128, D], F32)
    CBC = b.sbuf("CBC", [128, 16], F32)
    YR = b.sbuf("YR", [128, 2 * D], F32)
    SR = b.sbuf("SR", [128, 2 * 16], F32)
    X2R = b.sbuf("X2R", [128, 2 * D], F32)
    X3R = b.sbuf("X3R", [128, 2 * D], F32)
    YN = None
    YT = None
    AT = b.sbuf("AT", [128, D], F32)
    TMP = b.sbuf("TMP", [128, D], F32)
    XN4 = None
    GS = b.sbuf("GS_", [128, 2 * 512], BF16)
    HT = b.sbuf("HT", [128, 2 * 512], BF16)
    FF4 = b.sbuf("FF4", [128, 4 * D], F32)
    CB = b.sbuf("CB", [128, NT * 24], F32)
    SCB = b.sbuf("SCB", [128, 16], F32)
    SB2 = b.sbuf("SB2", [128, 16], F32)
    SB2o = b.sbuf("SB2o", [128, 16], F32)
    GRP = b.sbuf("GRP", [128, 4 * 8], F32)
    EM = b.sbuf("EM", [128, 16], F32)
    X4R = b.sbuf("X4R", [128, 2 * D], F32)
    scratch = b.sbuf("scratch", [128, D], F32)
    stats = b.sbuf("stats", [128, 48], F32)

    T0 = b.psum("T0", [128, 1024])
    T1 = b.psum("T1", [128, 1024])
    pTs = [b.psum("pT0", [128, 128], BF16), b.psum("pT1", [128, 128], BF16)]
    pR = b.psum("pR", [128, 512])

    YN = HF[:, 8 * D:9 * D]
    YT = HF[:, 9 * D:10 * D]
    XN4 = HF[:, 10 * D:11 * D]
    t_wp = b.dma("gp", WP[:, 0:8 * D].rearrange("p (c d) -> p c d", c=8),
                 w_proj.rearrange("(c p) d -> p c d", p=128))
    t_bp = b.dma("gp", BP[:, :], b_proj[0:1, :].partition_broadcast(128))
    t_rw = b.dma("gp", RWT[:, :].rearrange("p (c r) -> p c r", c=8),
                 rw_T.rearrange("(c p) r -> p c r", p=128))
    t_cbc = b.dma("gp", CBC[:, :], cbias[0:1, :].partition_broadcast(128))
    t_cb1 = b.op("gp", lambda e: e.memset(CB[:, :], 1.0))

    y_r = d_yin.rearrange("(t p) d -> p t d", p=128)
    s_r = d_sums.rearrange("(t p) r -> p t r", p=128)
    x2_r = d_x2.rearrange("(t p) d -> p t d", p=128)
    x3_r = d_x3.rearrange("(t p) d -> p t d", p=128)
    x4_r = o_x4.rearrange("(t p) d -> p t d", p=128)

    ring = {k: [None, None] for k in ("YR", "SR", "X2R", "X3R", "X4R")}
    war = {k: None for k in ("YN", "YT", "AT", "TMP", "XN4", "SCB", "SB2",
                             "SB2o", "GRP", "EM", "rs")}
    scdep = None
    out_toks = []
    x3_toks = []
    cb_toks = {}
    BIG = 16.0

    for t in range(NT):
        r = t % 2
        ya = YR[:, r * D:(r + 1) * D]
        sa = SR[:, r * 16:(r + 1) * 16]
        x2a = X2R[:, r * D:(r + 1) * D]
        t_y = b.dma("sy", ya, y_r[:, t, :], deps=[ring["YR"][r]])
        t_s = b.dma("sy", sa, s_r[:, t, :], deps=[ring["SR"][r]])
        t_x2 = b.dma("sy", x2a, x2_r[:, t, :], deps=[ring["X2R"][r]])
        rs = stats[:, 32:48]
        t_rs = b.op("ve", lambda e, sa=sa, rs=rs: e.reciprocal(rs, sa),
                    deps=[t_s, war["rs"]])
        t_yn = b.op("ve", lambda e, ya=ya, rs=rs: e.tensor_tensor(
            YN.rearrange("p (h w) -> p h w", h=16),
            ya.rearrange("p (h w) -> p h w", h=16),
            rs.unsqueeze(2).broadcast_to([128, 16, 64]), op=ALU.mult),
            deps=[t_rs, t_y, war["YN"]])
        ring["YR"][r] = t_yn
        war["rs"] = t_yn
        yt_toks = []
        for k in range(8):
            pt = pTs[k % 2]
            ttok = b.op("te", lambda e, k=k, pt=pt: e.transpose(
                pt[:, :], YN[:, k * 128:(k + 1) * 128], ident[:, :]),
                deps=[t_yn, tid] + pt.write_deps())
            pt.note_write(ttok)
            yt_toks.append(pread(b, "ve", pt, lambda e, k=k, pt=pt: e.tensor_copy(
                YT[:, k * 128:(k + 1) * 128], pt[:, :]), deps=[war["YT"]]))
        war["YN"] = yt_toks[-1]
        for h in range(2):
            for k in range(8):
                mm(b, T0, T0[:, h * 512:(h + 1) * 512],
                   YT[:, k * 128:(k + 1) * 128],
                   WP[:, k * D + h * 512: k * D + (h + 1) * 512],
                   deps=[yt_toks[k], t_wp], start=(k == 0), stop=(k == 7),
                   first=(k == 0))
        t_at = pread(b, "ve", T0, lambda e: e.tensor_tensor(
            AT[:, :], T0[:, :], BP[:, :], op=ALU.add), deps=[t_bp, war["AT"]])
        war["YT"] = t_at
        x3a = X3R[:, r * D:(r + 1) * D]
        t_x3, scdep = emit_ln_res(b, AT[:, :], [t_at], x2a, [t_x2], x3a,
                                  TMP[:, :], scratch[:, :], stats, 0,
                                  scdep=scdep)
        war["AT"] = t_x3
        ring["X2R"][r] = t_x3
        t_x3d = b.dma("sy", x3_r[:, t, :], x3a, deps=[t_x3])
        x3_toks.append(t_x3d)
        t_xn4, scdep = emit_ln_tile(b, x3a, [t_x3, war["XN4"]], XN4[:, :],
                                    scratch[:, :], stats, 8, scdep=scdep)
        ring["X3R"][r] = t_xn4
        xt_toks = []
        for k in range(8):
            pt = pTs[k % 2]
            ttok = b.op("te", lambda e, t=t, k=k, pt=pt: e.transpose(
                pt[:, :], XN4[:, k * 128:(k + 1) * 128], ident[:, :]),
                deps=[t_xn4, tid] + pt.write_deps())
            pt.note_write(ttok)
            xt_toks.append(pread(b, "ve", pt,
                lambda e, t=t, k=k, pt=pt: e.tensor_copy(
                    XT4[:, k * TLOC + t * 128: k * TLOC + (t + 1) * 128],
                    pt[:, :])))
        war["XN4"] = xt_toks[-1]
        for k in range(8):
            mm(b, pR, pR[:, 0:16],
               XT4[:, k * TLOC + t * 128: k * TLOC + (t + 1) * 128],
               RWT[:, k * 16:(k + 1) * 16],
               deps=[xt_toks[k], t_rw], start=(k == 0), stop=(k == 7),
               first=(k == 0))
        t_sc = pread(b, "se", pR, lambda e: e.activation(
            SCB[:, :], pR[:, 0:16], AF.Sigmoid), deps=[war["SCB"]])
        t_sb = b.op("ve", lambda e: e.tensor_tensor(
            SB2[:, :], SCB[:, :], CBC[:, :], op=ALU.add),
            deps=[t_sc, t_cbc, war["SB2"]])
        s3 = SB2[:, :].rearrange("p (g m) -> p g m", g=4)
        pair_toks = []
        pairs = [(0, 1), (0, 2), (0, 3), (1, 2), (1, 3), (2, 3)]
        for pi2, (i1, i2) in enumerate(pairs):
            pair_toks.append(b.op("ve", lambda e, s3=s3, pi2=pi2, i1=i1, i2=i2:
                e.tensor_tensor(GRP[:, pi2 * 4:(pi2 + 1) * 4],
                                s3[:, :, i1], s3[:, :, i2], op=ALU.add),
                deps=[t_sb, war["GRP"]]))
        t_m = b.op("ve", lambda e: e.tensor_tensor(
            GRP[:, 24:28], GRP[:, 0:4], GRP[:, 4:8], op=ALU.max),
            deps=pair_toks)
        for pi2 in range(2, 6):
            t_m = b.op("ve", lambda e, pi2=pi2: e.tensor_tensor(
                GRP[:, 24:28], GRP[:, 24:28], GRP[:, pi2 * 4:(pi2 + 1) * 4],
                op=ALU.max), deps=[t_m])
        t_m1 = b.op("ve", lambda e: e.reduce_max(
            stats[:, 16:17], GRP[:, 24:28], axis=AX.X), deps=[t_m])
        t_im = b.op("ve", lambda e: e.tensor_scalar(
            GRP[:, 28:32], GRP[:, 24:28], stats[:, 16:17], None,
            op0=ALU.is_ge, op1=ALU.bypass), deps=[t_m1])
        t_gm = b.op("ve", lambda e: e.scalar_tensor_tensor(
            GRP[:, 0:4], GRP[:, 28:32], -BIG, GRP[:, 24:28],
            op0=ALU.mult, op1=ALU.add), deps=[t_im])
        t_m2 = b.op("ve", lambda e: e.reduce_max(
            stats[:, 17:18], GRP[:, 0:4], axis=AX.X), deps=[t_gm])
        t_gmask = b.op("ve", lambda e: e.tensor_scalar(
            GRP[:, 28:32], GRP[:, 24:28], stats[:, 17:18], None,
            op0=ALU.is_ge, op1=ALU.bypass), deps=[t_m2, t_im])
        t_em = b.op("ve", lambda e: e.tensor_copy(
            EM[:, :].rearrange("p (g m) -> p g m", g=4),
            GRP[:, 28:32].unsqueeze(2).broadcast_to([128, 4, 4])),
            deps=[t_gmask, war["EM"]])
        t_mk = b.op("ve", lambda e: e.scalar_tensor_tensor(
            SB2o[:, :], EM[:, :], BIG, SB2[:, :], op0=ALU.mult, op1=ALU.add),
            deps=[t_em, war["SB2o"]])
        t_mk2 = b.op("ve", lambda e: e.tensor_scalar(
            SB2[:, :], SB2o[:, :], BIG, None, op0=ALU.subtract, op1=ALU.bypass),
            deps=[t_mk])
        t_cp = b.op("ve", lambda e: e.tensor_copy(SB2o[:, :], SB2[:, :]),
                    deps=[t_mk2])
        war["SB2"] = t_cp
        t_prev = t_cp
        for it in range(3):
            t_mx = b.op("ve", lambda e: e.reduce_max(
                stats[:, 18:19], SB2[:, :], axis=AX.X), deps=[t_prev])
            t_is = b.op("ve", lambda e: e.tensor_scalar(
                EM[:, :], SB2[:, :], stats[:, 18:19], None,
                op0=ALU.is_ge, op1=ALU.bypass), deps=[t_mx])
            t_prev = b.op("ve", lambda e: e.scalar_tensor_tensor(
                SB2[:, :], EM[:, :], -BIG, SB2[:, :], op0=ALU.mult, op1=ALU.add),
                deps=[t_is])
        t_tau = b.op("ve", lambda e: e.reduce_max(
            stats[:, 18:19], SB2[:, :], axis=AX.X), deps=[t_prev])
        war["EM"] = t_tau
        t_sel = b.op("ve", lambda e: e.tensor_scalar(
            EM[:, :], SB2o[:, :], stats[:, 18:19], None,
            op0=ALU.is_ge, op1=ALU.bypass), deps=[t_tau, t_cp])
        war["SB2o"] = t_sel
        t_w = b.op("ve", lambda e: e.tensor_tensor(
            SB2[:, :], SCB[:, :], EM[:, :], op=ALU.mult), deps=[t_sel, t_sc])
        war["SCB"] = t_w
        t_ws = b.op("ve", lambda e: e.reduce_sum(
            stats[:, 19:20], SB2[:, :], axis=AX.X), deps=[t_w])
        t_wse = b.op("ve", lambda e: e.tensor_scalar(
            stats[:, 19:20], stats[:, 19:20], 1e-20, None,
            op0=ALU.add, op1=ALU.bypass), deps=[t_ws])
        t_rc = b.op("ve", lambda e: e.reciprocal(
            stats[:, 20:21], stats[:, 19:20]), deps=[t_wse])
        t_cb = b.op("ve", lambda e, t=t: e.tensor_scalar(
            CB[:, t * 24:t * 24 + 16], SB2[:, :], stats[:, 20:21], None,
            op0=ALU.mult, op1=ALU.bypass), deps=[t_rc, t_w, t_cb1])
        cb_toks[t] = t_cb
        war["GRP"] = t_gmask

    gw_ring = [None, None, None]
    uw_ring = [None, None, None]
    dw_ring = [None, None, None]
    hf_guard = None
    gs_war = [None, None]
    ht_war = [None, None]
    ff_war = {}
    for tg in range(4):
        hf_toks = []
        for pe in range(12):
            rr = pe % 3
            gwa = GW[:, rr * 4096:(rr + 1) * 4096]
            uwa = UW[:, rr * 4096:(rr + 1) * 4096]
            t_gw = b.dma("gp", gwa.rearrange("p (c h) -> p c h", c=8),
                         d_gw[pe].rearrange("(c p) h -> p c h", p=128),
                         deps=[gw_ring[rr]])
            t_uw = b.dma("gp", uwa.rearrange("p (c h) -> p c h", c=8),
                         d_uw[pe].rearrange("(c p) h -> p c h", p=128),
                         deps=[uw_ring[rr]])
            last_g = last_u = None
            for ti in range(4):
                t_abs = tg * 4 + ti
                T = T0 if ti % 2 == 0 else T1
                for k in range(8):
                    last_g = mm(b, T, T[:, 0:512],
                                XT4[:, k * TLOC + t_abs * 128:
                                    k * TLOC + (t_abs + 1) * 128],
                                gwa[:, k * 512:(k + 1) * 512],
                                deps=[t_gw], start=(k == 0), stop=(k == 7),
                                first=(k == 0))
                for k in range(8):
                    last_u = mm(b, T, T[:, 512:1024],
                                XT4[:, k * TLOC + t_abs * 128:
                                    k * TLOC + (t_abs + 1) * 128],
                                uwa[:, k * 512:(k + 1) * 512],
                                deps=[t_uw], start=(k == 0), stop=(k == 7),
                                first=(k == 0))
                tb = ti % 2
                gsa = GS[:, tb * 512:(tb + 1) * 512]
                hta = HT[:, tb * 512:(tb + 1) * 512]
                t_si = pread(b, "se", T, lambda e, T=T, gsa=gsa: e.activation(
                    gsa, T[:, 0:512], AF.Silu), deps=[gs_war[tb]])
                t_hs = []
                for eip in range(2):
                    e2 = 2 * pe + eip
                    t_hs.append(pread(b, "ve", T,
                        lambda e, T=T, t_abs=t_abs, e2=e2, eip=eip, hta=hta, gsa=gsa:
                        e.scalar_tensor_tensor(
                            hta[:, eip * 256:(eip + 1) * 256],
                            T[:, 512 + eip * 256: 512 + (eip + 1) * 256],
                            CB[:, t_abs * 24 + e2: t_abs * 24 + e2 + 1],
                            gsa[:, eip * 256:(eip + 1) * 256],
                            op0=ALU.mult, op1=ALU.mult),
                        deps=[t_si, cb_toks[t_abs], ht_war[tb]]))
                gs_war[tb] = t_hs[-1]
                tr_toks = []
                for q4 in range(4):
                    eip, hc = q4 // 2, q4 % 2
                    e2 = 2 * pe + eip
                    pt = pTs[q4 % 2]
                    ttok = b.op("te", lambda e, q4=q4, pt=pt, hta=hta: e.transpose(
                        pt[:, :], hta[:, q4 * 128:(q4 + 1) * 128], ident[:, :]),
                        deps=[t_hs[eip], tid] + pt.write_deps())
                    pt.note_write(ttok)
                    tr_toks.append(pread(b, "ve", pt,
                        lambda e, e2=e2, hc=hc, ti=ti, pt=pt: e.tensor_copy(
                            HF[:, e2 * 1024 + hc * 512 + ti * 128:
                               e2 * 1024 + hc * 512 + (ti + 1) * 128],
                            pt[:, :]), deps=[hf_guard]))
                ht_war[tb] = tr_toks[-1]
                hf_toks += tr_toks
            gw_ring[rr] = last_g
            uw_ring[rr] = last_u
        dw_last = None
        for dsl in range(2):
            ev_toks = []
            for e2 in range(24):
                rr = e2 % 3
                dwa = DWB[:, rr * 1024:(rr + 1) * 1024]
                t_dw = b.dma("gp", dwa.rearrange("p (hc d) -> p hc d", hc=2),
                             d_dw[e2].rearrange("(hc p) d -> p hc d", p=128)
                             [:, :, dsl * 512:(dsl + 1) * 512],
                             deps=[dw_ring[rr]])
                last_dw = None
                for ti in range(4):
                    T = T0 if ti < 2 else T1
                    col = (ti % 2) * 512
                    for hc in range(2):
                        last_dw = mm(
                            b, T, T[:, col:col + 512],
                            HF[:, e2 * 1024 + hc * 512 + ti * 128:
                               e2 * 1024 + hc * 512 + (ti + 1) * 128],
                            dwa[:, hc * 512:(hc + 1) * 512],
                            deps=[t_dw] + hf_toks,
                            start=(e2 == 0 and hc == 0),
                            stop=(e2 == 23 and hc == 1),
                            first=(e2 == 0 and hc == 0))
                dw_ring[rr] = last_dw
            for ti in range(4):
                T = T0 if ti < 2 else T1
                col = (ti % 2) * 512
                ev_toks.append(pread(b, "ve", T,
                    lambda e, ti=ti, T=T, col=col, tg=tg, dsl=dsl: e.tensor_copy(
                        FF4[:, ti * D + dsl * 512: ti * D + (dsl + 1) * 512],
                        T[:, col:col + 512]),
                    deps=[ff_war.get((tg - 1, ti))]))
                ff_war[(tg, ti, dsl)] = ev_toks[-1]
        for ti in range(4):
            t_abs = tg * 4 + ti
            r = t_abs % 2
            x3a = X3R[:, r * D:(r + 1) * D]
            t_x3l = b.dma("sy", x3a, x3_r[:, t_abs, :],
                          deps=[ring["X3R"][r], x3_toks[t_abs]])
            x4a = X4R[:, r * D:(r + 1) * D]
            t_x4, scdep = emit_ln_res(
                b, FF4[:, ti * D:(ti + 1) * D],
                [ff_war[(tg, ti, 0)], ff_war[(tg, ti, 1)]],
                x3a, [t_x3l], x4a, TMP[:, :], scratch[:, :], stats, 24,
                scdep=scdep)
            war["TMP"] = t_x4
            ring["X3R"][r] = t_x4
            t_od = b.dma("sy", x4_r[:, t_abs, :], x4a, deps=[t_x4, ring["X4R"][r]])
            ring["X4R"][r] = t_od
            out_toks.append(t_od)
            ff_war[(tg, ti)] = t_x4
        hf_guard = max(dw_ring[0], dw_ring[1], key=lambda tk: tk[1])

    nc = b.finalize(out_toks)
    return nc



def _shard_rows(a):
    out = []
    for c in range(NC):
        sl = a[:, c * MLOC:(c + 1) * MLOC]
        out.append(np.ascontiguousarray(sl.reshape((TLOC,) + a.shape[2:])))
    return out


def kernel(L0_attn_x, L0_attn_Wq, L0_attn_Wk, L0_attn_Wv, L0_attn_Wg,
           L0_attn_Wgk1, L0_attn_Wgk2, L0_attn_bgk2, L0_attn_gamma,
           L0_attn_beta, L0_attn_Wout, L0_ffn_phi, L0_ffn_W, L0_ffn_b,
           L1_attn_w_attn, L1_attn_b_attn, L1_attn_w_proj, L1_attn_b_proj,
           L1_ffn_router_w, L1_ffn_correction_bias, L1_ffn_gate_w,
           L1_ffn_up_w, L1_ffn_down_w, L1_ffn_shared_gate_w,
           L1_ffn_shared_up_w, L1_ffn_shared_down_w, _upto=None, _dbg=None):
    x = np.asarray(L0_attn_x, dtype=np.float32)
    xs = _shard_rows(x)

    nc = build_A1()
    in_maps = []
    for c in range(NC):
        in_maps.append({
            "x": xs[c],
            "wq": np.asarray(L0_attn_Wq, np.float32),
            "wk": np.asarray(L0_attn_Wk, np.float32),
            "wv": np.asarray(L0_attn_Wv, np.float32),
            "wg": np.asarray(L0_attn_Wg, np.float32),
            "wgk1": np.asarray(L0_attn_Wgk1, np.float32),
            "wgk2": np.asarray(L0_attn_Wgk2, np.float32),
            "bgk2": np.asarray(L0_attn_bgk2, np.float32).reshape(1, D),
        })
    res = run_bass_kernel_spmd(nc, in_maps, core_ids=list(range(NC)))
    if _upto == "A1":
        return [r for r in res.results]
    rA1 = res.results

    def asm(key, dt):
        full = np.empty((B_, SEQ, D), dt)
        for c in range(NC):
            full[:, c * MLOC:(c + 1) * MLOC] = rA1[c][key].reshape(B_, MLOC, D)
        return full
    qf = asm("o_q", BF)
    kf = asm("o_k", BF)
    vf = asm("o_v", BF)
    gkf = asm("o_gk", np.float32)

    nc2 = build_A2()
    in_maps2 = []
    for c in range(NC):
        bsel = c // 4
        heads = [4 * (c % 4) + u for u in range(4)]
        qr = qf[bsel].reshape(NCH, 128, 16, 64)[:, :, heads]
        kr = kf[bsel].reshape(NCH, 128, 16, 64)[:, :, heads]
        vr = vf[bsel].reshape(NCH, 128, 16, 64)[:, :, heads]
        gr = gkf[bsel].reshape(NCH, 128, 16, 64)[:, :, heads]
        d_fm = np.zeros((NCH, 64, 1024), BF)
        for u in range(4):
            d_fm[:, :, u * 128:(u + 1) * 128] = qr[:, :, u].transpose(0, 2, 1)
            d_fm[:, :, 512 + u * 128:512 + (u + 1) * 128] = kr[:, :, u].transpose(0, 2, 1)
        d_tv = np.zeros((NCH, 128, 512), BF)
        d_tv[:, :, 0:256] = kr.reshape(NCH, 128, 256)
        d_tv[:, :, 256:512] = vr.reshape(NCH, 128, 256)
        d_gk = np.ascontiguousarray(gr.reshape(NCH, 128, 256))
        in_maps2.append({"d_fm": d_fm, "d_tv": d_tv, "d_gk": d_gk})
    res2 = run_bass_kernel_spmd(nc2, in_maps2, core_ids=list(range(NC)))
    o_gla = np.empty((B_, SEQ, D), np.float32)
    for c in range(NC):
        bsel = c // 4
        heads = [4 * (c % 4) + u for u in range(4)]
        do = res2.results[c]["d_o"].reshape(NCH, 128, 4, 64)
        for u in range(4):
            o_gla[bsel, :, heads[u] * 64:(heads[u] + 1) * 64] = \
                do[:, :, u].reshape(SEQ, 64)
    if _upto == "A2":
        return o_gla

    nc3 = build_AB()
    ogs = _shard_rows(o_gla)
    in_maps3 = []
    for c in range(NC):
        in_maps3.append({
            "x": xs[c],
            "og": ogs[c],
            "gsil": np.ascontiguousarray(rA1[c]["o_g"]),
            "gamma": np.asarray(L0_attn_gamma, np.float32).reshape(1, D),
            "beta": np.asarray(L0_attn_beta, np.float32).reshape(1, D),
            "wout": np.asarray(L0_attn_Wout, np.float32),
            "phi": np.ascontiguousarray(np.asarray(L0_ffn_phi, np.float32).reshape(D, 2048)),
        })
    res3 = run_bass_kernel_spmd(nc3, in_maps3, core_ids=list(range(NC)))
    rAB = res3.results
    if _upto == "AB":
        return rAB

    nc4 = build_B2()
    W4 = np.asarray(L0_ffn_W, np.float32)
    bn4 = np.asarray(L0_ffn_b, np.float32)
    in_maps4 = []
    for c in range(NC):
        d_ps = np.stack([
            np.ascontiguousarray(rAB[s]["o_ps"][:, c * 256:(c + 1) * 256, :]
                                 .transpose(0, 2, 1))
            for s in range(NC)])
        in_maps4.append({
            "d_ps": d_ps,
            "d_wn": np.ascontiguousarray(W4[8 * c:8 * c + 8]),
            "d_bn": np.ascontiguousarray(bn4[8 * c:8 * c + 8]),
        })
    res4 = run_bass_kernel_spmd(nc4, in_maps4, core_ids=list(range(NC)))
    rB2 = res4.results
    if _upto == "B2":
        return rB2

    nc5 = build_BC()
    Yfull = np.concatenate([rB2[c]["o_y"].reshape(B_, 256, D) for c in range(NC)],
                           axis=1)
    in_maps5 = []
    for c in range(NC):
        in_maps5.append({
            "d_y": Yfull,
            "d_c": rAB[c]["o_C"],
            "d_x1": rAB[c]["o_x1"],
            "w_attn": np.asarray(L1_attn_w_attn, np.float32),
            "b_attn": np.asarray(L1_attn_b_attn, np.float32).reshape(1, 3 * D),
        })
    res5 = run_bass_kernel_spmd(nc5, in_maps5, core_ids=list(range(NC)))
    rBC = res5.results
    if _upto == "BC":
        return rBC

    qf2 = asm2 = None
    def asmBC(key):
        full = np.empty((B_, SEQ, D), BF)
        for c in range(NC):
            full[:, c * MLOC:(c + 1) * MLOC] = rBC[c][key].reshape(B_, MLOC, D)
        return full
    qf2 = asmBC("o_q")
    kf2 = asmBC("o_k")
    vf2 = asmBC("o_v")
    nc6 = build_C2()
    in_maps6 = []
    for c in range(NC):
        bsel = c // 4
        heads = [4 * (c % 4) + u for u in range(4)]
        d_qf = np.stack([np.ascontiguousarray(qf2[bsel, :, h * 64:(h + 1) * 64].T)
                         for h in heads])
        d_kf = np.stack([np.ascontiguousarray(kf2[bsel, :, h * 64:(h + 1) * 64].T)
                         for h in heads])
        d_v = np.stack([np.ascontiguousarray(vf2[bsel, :, h * 64:(h + 1) * 64])
                        for h in heads])
        in_maps6.append({"d_qf": d_qf, "d_kf": d_kf, "d_v": d_v})
    res6 = run_bass_kernel_spmd(nc6, in_maps6, core_ids=list(range(NC)))
    rC2 = res6.results
    if _upto == "C2":
        return rC2

    Yall = np.empty((B_, SEQ, D), np.float32)
    Sall = np.empty((B_, SEQ, 16), np.float32)
    for bb in range(B_):
        for h in range(16):
            qc = bb * 4 + h // 4
            u = h % 4
            ot = rC2[qc]["d_o"][u].transpose(1, 0, 2).reshape(65, SEQ)
            Yall[bb, :, h * 64:(h + 1) * 64] = ot[0:64].T
            Sall[bb, :, h] = ot[64]
    gw24 = np.concatenate([
        np.asarray(L1_ffn_gate_w, np.float32),
        np.ascontiguousarray(np.asarray(L1_ffn_shared_gate_w, np.float32).T
                             .reshape(D, 8, 256).transpose(1, 0, 2))], axis=0)
    uw24 = np.concatenate([
        np.asarray(L1_ffn_up_w, np.float32),
        np.ascontiguousarray(np.asarray(L1_ffn_shared_up_w, np.float32).T
                             .reshape(D, 8, 256).transpose(1, 0, 2))], axis=0)
    dw24 = np.concatenate([
        np.asarray(L1_ffn_down_w, np.float32),
        np.ascontiguousarray(np.asarray(L1_ffn_shared_down_w, np.float32).T
                             .reshape(8, 256, D))], axis=0)
    gwp = np.ascontiguousarray(gw24.reshape(12, 2, D, 256).transpose(0, 2, 1, 3)
                               .reshape(12, D, 512))
    uwp = np.ascontiguousarray(uw24.reshape(12, 2, D, 256).transpose(0, 2, 1, 3)
                               .reshape(12, D, 512))
    nc7 = build_CD()
    in_maps7 = []
    for c in range(NC):
        in_maps7.append({
            "d_yin": np.ascontiguousarray(
                Yall[:, c * MLOC:(c + 1) * MLOC].reshape(TLOC, D)),
            "d_sums": np.ascontiguousarray(
                Sall[:, c * MLOC:(c + 1) * MLOC].reshape(TLOC, 16)),
            "d_x2": rBC[c]["o_x2"],
            "w_proj": np.asarray(L1_attn_w_proj, np.float32),
            "b_proj": np.asarray(L1_attn_b_proj, np.float32).reshape(1, D),
            "rw_T": np.ascontiguousarray(np.asarray(L1_ffn_router_w, np.float32).T),
            "cbias": np.asarray(L1_ffn_correction_bias, np.float32).reshape(1, 16),
            "d_gw": gwp, "d_uw": uwp, "d_dw": dw24,
        })
    res7 = run_bass_kernel_spmd(nc7, in_maps7, core_ids=list(range(NC)))
    if _dbg == "CD":
        return res7.results
    out = np.empty((B_, SEQ, D), np.float32)
    for c in range(NC):
        out[:, c * MLOC:(c + 1) * MLOC] = res7.results[c]["o_x4"].reshape(B_, MLOC, D)
    return out
